# version 11
# baseline (speedup 1.0000x reference)
import numpy as np
import ml_dtypes
from contextlib import ExitStack

import concourse.bass as bass
import concourse.mybir as mybir
from concourse.alu_op_type import AluOpType as OP
from concourse.bass_utils import run_bass_kernel_spmd

F32 = mybir.dt.float32
BF16 = mybir.dt.bfloat16
AF = mybir.ActivationFunctionType
AX = mybir.AxisListType
BF = ml_dtypes.bfloat16

N_CORES = 8
M = 1024
NH = 8
DH = 128
RH = 16
RHS = 64
E = 8
FFN = 4096
P_LOC = 512
LN_EPS = 1e-5
GN_EPS = 64e-5
RCHUNK = 128

ENGINES = ("sync", "tensor", "scalar", "vector", "gpsimd")


class _Call:
    __slots__ = ("meth", "args", "kwargs", "chain", "marks")

    def __init__(self, meth, args, kwargs):
        self.meth = meth
        self.args = args
        self.kwargs = kwargs
        self.chain = []
        self.marks = []

    def then_inc(self, sem, n):
        self.chain.append(("then_inc", (sem, n)))
        return self


class _EngRec:
    def __init__(self, rec, name):
        object.__setattr__(self, "_rec", rec)
        object.__setattr__(self, "_name", name)

    def __getattr__(self, meth):
        rec, name = self._rec, self._name

        def call(*a, **k):
            node = _Call(meth, a, k)
            rec.ops[name].append(node)
            return node
        return call


class Rec:

    def __init__(self, nc, es):
        self.nc = nc
        self.es = es
        self.ops = {e: [] for e in ENGINES}
        self.sems = {}
        self.sync = _EngRec(self, "sync")
        self.tensor = _EngRec(self, "tensor")
        self.scalar = _EngRec(self, "scalar")
        self.vector = _EngRec(self, "vector")
        self.gpsimd = _EngRec(self, "gpsimd")

    def sem(self, name):
        if name not in self.sems:
            self.sems[name] = [self.es.enter_context(self.nc.semaphore(name)), 0]
        return self.sems[name][0]

    def bump(self, name, k=1):
        self.sem(name)
        self.sems[name][1] += k
        return self.sems[name][1]

    def val(self, name):
        self.sem(name)
        return self.sems[name][1]

    def dma(self, eng, sem_name, out, in_):
        getattr(self, eng).dma_start(out=out, in_=in_).then_inc(self.sem(sem_name), 16)
        return 16 * self.bump(sem_name)

    def mark(self, eng, sem_name):
        ops = self.ops[eng]
        assert ops and ops[-1].meth not in ("wait_ge", "sem_inc"), \
            f"mark({eng}): no completion-capable instruction to attach to"
        node = ops[-1]
        for nm, cnt in node.marks:
            if nm == sem_name:
                return cnt
        node.then_inc(self.sem(sem_name), 1)
        cnt = self.bump(sem_name)
        node.marks.append((sem_name, cnt))
        return cnt

    def fence(self, eng, sem_name=None):
        node = self.ops[eng][-1]
        if node.marks:
            nm, cnt = node.marks[0]
        else:
            nm = sem_name or (eng + "_f")
            cnt = self.mark(eng, nm)
        getattr(self, eng).wait_ge(self.sem(nm), cnt)
        return cnt

    def sbuf(self, name, shape, dtype):
        return self.es.enter_context(self.nc.sbuf_tensor(name, shape, dtype))

    def psum(self, name, shape, dtype=F32):
        return self.es.enter_context(self.nc.psum_tensor(name, shape, dtype))

    def flush(self):
        with self.nc.allow_low_precision("bf16 psum transpose staging by design"), \
                self.nc.Block() as block:
            for e in ENGINES:
                ops = self.ops[e]
                if not ops:
                    continue

                def emit(eng, ops=ops):
                    for node in ops:
                        r = getattr(eng, node.meth)(*node.args, **node.kwargs)
                        for cname, cargs in node.chain:
                            r = getattr(r, cname)(*cargs)
                getattr(block, e)(emit)



def host_consts(T):
    c = {}
    pos = np.arange(T, dtype=np.float32)
    half = DH // 2
    freqs = np.power(10000.0, -np.arange(half, dtype=np.float32) / half)
    rad = pos[None, :] * freqs[:, None]
    cos = np.cos(rad).astype(np.float32)
    sin = np.sin(rad).astype(np.float32)
    c["ropeC"] = np.ascontiguousarray(np.concatenate([cos, cos], 0)).astype(BF)
    c["ropeS"] = np.ascontiguousarray(np.concatenate([-sin, sin], 0)).astype(BF)
    perm = np.zeros((DH, DH), np.float32)
    for j in range(DH):
        perm[(j + half) % DH, j] = 1.0
    c["permT"] = perm.astype(BF)
    for j in range(4):
        rr = np.arange(128)[:, None] + 128 * j
        cq = np.arange(512)[None, :]
        c[f"cmask{j}"] = np.ascontiguousarray((rr <= cq).astype(np.float32)).astype(BF)
    c["ones_col"] = np.ones((128, 1), np.float32).astype(BF)
    c["ones_row_f32"] = np.ones((1, 128), np.float32)
    c["ident"] = np.eye(128, dtype=np.float32).astype(BF)
    c["epsln"] = np.full((128, 1), LN_EPS, np.float32)
    c["epsgn"] = np.full((128, 1), GN_EPS, np.float32)
    c["triu_f32"] = np.ascontiguousarray(np.triu(np.ones((128, 128), np.float32)))
    s127 = np.zeros((128, 1), np.float32); s127[127, 0] = 1.0
    c["sel127"] = s127
    c["iota512"] = np.ascontiguousarray(
        np.broadcast_to(np.arange(512, dtype=np.float32), (128, 512)))
    c["iota8"] = np.ascontiguousarray(
        np.broadcast_to(np.arange(8, dtype=np.float32), (128, 8)))
    c["ident_f32"] = np.eye(128, dtype=np.float32)
    c["slotid"] = np.arange(128, dtype=np.float32).reshape(128, 1)
    c["triu16_f32"] = np.ascontiguousarray(
        np.triu(np.ones((16, 16), np.float32), 1))
    c["ones16_f32"] = np.ones((16, 1), np.float32)
    return c



def build_l1(T_loc):
    nc = bass.Bass("TRN2", target_bir_lowering=False)
    x_in = nc.dram_tensor("x", [T_loc, M], F32, kind="ExternalInput").ap()
    eps_d = nc.dram_tensor("epsln", [128, 1], F32, kind="ExternalInput").ap()
    ln_out = nc.dram_tensor("ln1", [T_loc, M], BF16, kind="ExternalOutput").ap()
    nt = T_loc // 128
    with ExitStack() as es:
        r = Rec(nc, es)
        xs = r.sbuf("xs", [128, nt * M], F32)
        lo = r.sbuf("lo", [128, nt * M], BF16)
        eps_s = r.sbuf("eps_s", [128, 1], F32)
        sc = ln_scratch(r, nt)
        r.dma("sync", "ld", eps_s[:], eps_d[:])
        for i in range(nt):
            tgt = r.dma("sync", "ld", xs[:, i * M:(i + 1) * M],
                        x_in[i * 128:(i + 1) * 128, :])
        r.scalar.wait_ge(r.sem("ld"), tgt)
        r.vector.wait_ge(r.sem("ld"), tgt)
        dsem, done = emit_ln(r, lambda i: xs[:, i * M:(i + 1) * M],
                             lambda i: lo[:, i * M:(i + 1) * M], nt, sc, eps_s[:])
        r.gpsimd.wait_ge(r.sem(dsem), done)
        for i in range(nt):
            tgt = r.dma("gpsimd", "st", ln_out[i * 128:(i + 1) * 128, :],
                        lo[:, i * M:(i + 1) * M])
        r.gpsimd.wait_ge(r.sem("st"), tgt)
        r.flush()
    return nc


def ln_scratch(r, nt, tag=""):
    sc = {k: r.sbuf(f"lnsc_{k}{tag}", [128, nt], F32)
          for k in ("sum", "sumsq", "mean", "rstd")}
    sc["junk"] = r.sbuf(f"lnsc_junk{tag}", [128, M], F32)
    sc["sq"] = r.sbuf(f"lnsc_sq{tag}", [128, 2 * M], F32)
    return sc


def emit_ln(r, xt, ot, nt, sc, eps_ap, tag="", out_also=None):
    s, v = r.scalar, r.vector
    sS, sV = "lnS" + tag, "lnV" + tag
    for i in range(nt):
        if i >= 2:
            s.wait_ge(r.sem(sV), i - 1)
        s.activation(sc["sq"][:, (i % 2) * M:(i % 2 + 1) * M], xt(i), AF.Square)
        ns = r.mark("scalar", sS)
        v.wait_ge(r.sem(sS), i + 1)
        v.reduce_sum(sc["sum"][:, i:i + 1], xt(i), axis=AX.X)
        v.reduce_sum(sc["sumsq"][:, i:i + 1], sc["sq"][:, (i % 2) * M:(i % 2 + 1) * M],
                     axis=AX.X)
        r.mark("vector", sV)
    r.fence("vector")
    v.tensor_scalar(sc["mean"][:, 0:nt], sc["sum"][:, 0:nt], 1.0 / M, None, OP.mult)
    v.tensor_scalar(sc["sumsq"][:, 0:nt], sc["sumsq"][:, 0:nt], 1.0 / M, None, OP.mult)
    r.fence("vector")
    v.tensor_tensor(sc["rstd"][:, 0:nt], sc["mean"][:, 0:nt], sc["mean"][:, 0:nt],
                    OP.mult)
    r.fence("vector")
    v.tensor_tensor(sc["sumsq"][:, 0:nt], sc["sumsq"][:, 0:nt], sc["rstd"][:, 0:nt],
                    OP.subtract)
    n = r.mark("vector", sV + "c")
    s.wait_ge(r.sem(sV + "c"), n)
    s.activation(sc["sumsq"][:, 0:nt], sc["sumsq"][:, 0:nt], AF.Sqrt, bias=eps_ap)
    n = r.mark("scalar", sS + "c")
    v.wait_ge(r.sem(sS + "c"), n)
    v.reciprocal(sc["rstd"][:, 0:nt], sc["sumsq"][:, 0:nt])
    r.fence("vector")
    v.tensor_tensor(sc["mean"][:, 0:nt], sc["mean"][:, 0:nt], sc["rstd"][:, 0:nt],
                    OP.mult)
    r.fence("vector")
    v.tensor_scalar(sc["mean"][:, 0:nt], sc["mean"][:, 0:nt], -1.0, None, OP.mult)
    n = r.mark("vector", sV + "c")
    s.wait_ge(r.sem(sV + "c"), n)
    for i in range(nt):
        s.activation(ot(i), xt(i), AF.Identity,
                     bias=sc["mean"][:, i:i + 1], scale=sc["rstd"][:, i:i + 1])
        if out_also is not None:
            s.activation(out_also(i), xt(i), AF.Identity,
                         bias=sc["mean"][:, i:i + 1], scale=sc["rstd"][:, i:i + 1])
    nf = r.mark("scalar", sS + "d")
    return sS + "d", nf



def build_l2(T, debug=False):
    assert T % 512 == 0
    NS = T // 512
    nc = bass.Bass("TRN2", target_bir_lowering=False)
    ln1 = nc.dram_tensor("ln1fm", [M, T], BF16, kind="ExternalInput").ap()
    w_d = {k: nc.dram_tensor(k, [M, DH], F32, kind="ExternalInput").ap()
           for k in ("wq", "wk", "wv")}
    ropeC_d = nc.dram_tensor("ropeC", [DH, T], BF16, kind="ExternalInput").ap()
    ropeS_d = nc.dram_tensor("ropeS", [DH, T], BF16, kind="ExternalInput").ap()
    perm_d = nc.dram_tensor("permT", [DH, DH], BF16, kind="ExternalInput").ap()
    cmask_d = [nc.dram_tensor(f"cmask{j}", [128, 512], BF16, kind="ExternalInput").ap()
               for j in range(4)]
    onescol_d = nc.dram_tensor("ones_col", [128, 1], BF16, kind="ExternalInput").ap()
    onesrow_d = nc.dram_tensor("ones_row_f32", [1, 128], F32, kind="ExternalInput").ap()
    ident_d = nc.dram_tensor("ident", [128, 128], BF16, kind="ExternalInput").ap()
    o_out = nc.dram_tensor("o", [DH, T], BF16, kind="ExternalOutput").ap()
    if debug:
        dbgq = nc.dram_tensor("dbgq", [DH, T], BF16, kind="ExternalOutput").ap()
        dbgk = nc.dram_tensor("dbgk", [DH, T], BF16, kind="ExternalOutput").ap()
        dbgv = nc.dram_tensor("dbgv", [128, T], BF16, kind="ExternalOutput").ap()
        dbgc = nc.dram_tensor("dbgc", [128, 4], F32, kind="ExternalOutput").ap()

    with ExitStack() as es:
        r = Rec(nc, es)
        Q = r.sbuf("Q", [128, T], BF16)
        K = r.sbuf("K", [128, T], BF16)
        Vt = r.sbuf("Vt", [128, T], BF16)
        w_s = r.sbuf("w_s", [128, 3 * M], F32)
        w_bf = r.sbuf("w_bf", [128, 3 * M], BF16)
        perm_s = r.sbuf("perm_s", [DH, DH], BF16)
        cmask_s = r.sbuf("cmask_s", [128, 4 * 512], BF16)
        onescol = r.sbuf("onescol", [128, 1], BF16)
        onesrow = r.sbuf("onesrow", [1, 128], F32)
        ident_s = r.sbuf("ident_s", [128, 128], BF16)
        ln_buf = r.sbuf("ln_buf", [128, 2 * 4096], BF16)
        ropeC_s = r.sbuf("ropeC_s", [128, 2 * 512], BF16)
        ropeS_s = r.sbuf("ropeS_s", [128, 2 * 512], BF16)
        qraw = r.sbuf("qraw", [128, 512], BF16)
        kraw = r.sbuf("kraw", [128, 512], BF16)
        vraw = r.sbuf("vraw", [128, 512], BF16)
        tmp1 = r.sbuf("tmp1", [128, 512], BF16)
        tmp2 = r.sbuf("tmp2", [128, 512], BF16)
        nsq = r.sbuf("nsq", [128, 512], BF16)
        nrow = r.sbuf("nrow", [1, 2 * NS], F32)
        sc1 = r.sbuf("sc1", [1, 4], F32)
        cb = r.sbuf("cb", [128, 1], F32)
        P_T = r.sbuf("P_T", [128, 2 * 1024], BF16)
        lrec = r.sbuf("lrec", [128, 512], F32)
        o_sb = r.sbuf("o_sb", [128, 2 * 512], BF16)

        ps_S = r.psum("ps_S", [128, 2 * 1024])
        ps_qkv = ps_S[:, 0:512]
        ps_sw = ps_S[:, 512:1024]
        ps_tr = r.psum("ps_tr", [128, 128], BF16)
        ps_O = r.psum("ps_O", [128, 512])
        ps_l = r.psum("ps_l", [1, 512])
        ps_n = r.psum("ps_n", [128, 512])

        t, s, v, g, sy = r.tensor, r.scalar, r.vector, r.gpsimd, r.sync

        for i, k in enumerate(("wq", "wk", "wv")):
            r.dma("sync", "ld",
                  w_s[:, i * M:(i + 1) * M].rearrange("p (c d) -> p c d", c=8),
                  w_d[k].rearrange("(c p) d -> p c d", p=128))
        r.dma("sync", "ld", perm_s[:], perm_d[:])
        for j in range(4):
            r.dma("sync", "ld", cmask_s[:, j * 512:(j + 1) * 512], cmask_d[j][:])
        r.dma("sync", "ld", onescol[:], onescol_d[:])
        r.dma("sync", "ld", onesrow[:], onesrow_d[:])
        tgt = r.dma("sync", "ld", ident_s[:], ident_d[:])
        r.vector.wait_ge(r.sem("ld"), tgt)
        v.tensor_copy(w_bf[:], w_s[:])
        n_wc = r.mark("vector", "wc")

        for i in range(NS):
            if i >= 2:
                r.sync.wait_ge(r.sem("qk"), 3 * (i - 1))
                r.sync.wait_ge(r.sem("qv"), 3 * (i - 1))
            r.dma("sync", "ld2",
                  ln_buf[:, (i % 2) * 4096:(i % 2 + 1) * 4096].rearrange(
                      "p (c t) -> p c t", c=8),
                  ln1[:, i * 512:(i + 1) * 512].rearrange("(c p) t -> p c t", p=128))
            r.dma("sync", "rld", ropeC_s[:, (i % 2) * 512:(i % 2 + 1) * 512],
                  ropeC_d[:, i * 512:(i + 1) * 512])
            r.dma("sync", "rld", ropeS_s[:, (i % 2) * 512:(i % 2 + 1) * 512],
                  ropeS_d[:, i * 512:(i + 1) * 512])

        t.wait_ge(r.sem("wc"), n_wc)
        nv = 0
        for i in range(NS):
            lnb = ln_buf[:, (i % 2) * 4096:(i % 2 + 1) * 4096]
            rC = ropeC_s[:, (i % 2) * 512:(i % 2 + 1) * 512]
            rS = ropeS_s[:, (i % 2) * 512:(i % 2 + 1) * 512]
            qsl = Q[:, i * 512:(i + 1) * 512]
            ksl = K[:, i * 512:(i + 1) * 512]
            t.wait_ge(r.sem("ld2"), 16 * (i + 1))
            if i > 0:
                t.wait_ge(r.sem("qv"), 3 * i)
            for c in range(8):
                t.matmul(ps_qkv, w_bf[:, 0 * M + c * DH:0 * M + (c + 1) * DH],
                         lnb[:, c * 512:(c + 1) * 512], start=(c == 0), stop=(c == 7))
            nq = r.mark("tensor", "qk")
            v.wait_ge(r.sem("qk"), nq)
            v.tensor_copy(qraw[:], ps_qkv)
            nv = r.mark("vector", "qv")
            t.wait_ge(r.sem("qv"), nv)
            for c in range(8):
                t.matmul(ps_qkv, w_bf[:, 1 * M + c * DH:1 * M + (c + 1) * DH],
                         lnb[:, c * 512:(c + 1) * 512], start=(c == 0), stop=(c == 7))
            t.matmul(ps_sw, perm_s[:], qraw[:], start=True, stop=True)
            nq = r.mark("tensor", "qk")
            v.wait_ge(r.sem("qk"), nq)
            v.tensor_copy(kraw[:], ps_qkv)
            v.tensor_tensor(tmp1[:], qraw[:], rC, OP.mult)
            v.tensor_tensor(tmp2[:], ps_sw, rS, OP.mult)
            v.tensor_tensor(tmp1[:], tmp1[:], tmp2[:], OP.add)
            v.tensor_scalar(qsl, tmp1[:], 1.0 / DH, None, OP.mult)
            nv = r.mark("vector", "qv")
            t.wait_ge(r.sem("qv"), nv)
            for c in range(8):
                t.matmul(ps_qkv, w_bf[:, 2 * M + c * DH:2 * M + (c + 1) * DH],
                         lnb[:, c * 512:(c + 1) * 512], start=(c == 0), stop=(c == 7))
            t.matmul(ps_sw, perm_s[:], kraw[:], start=True, stop=True)
            nq = r.mark("tensor", "qk")
            v.wait_ge(r.sem("qk"), nq)
            v.tensor_copy(vraw[:], ps_qkv)
            v.tensor_tensor(tmp1[:], kraw[:], rC, OP.mult)
            v.tensor_tensor(tmp2[:], ps_sw, rS, OP.mult)
            v.tensor_tensor(ksl, tmp1[:], tmp2[:], OP.add)
            nv = r.mark("vector", "qv")
            t.wait_ge(r.sem("qv"), nv)
            for j in range(4):
                t.transpose(ps_tr[:], vraw[:, j * 128:(j + 1) * 128], ident_s[:])
                njt = r.mark("tensor", "vtr")
                v.wait_ge(r.sem("vtr"), njt)
                v.tensor_copy(Vt[:, (i * 4 + j) * 128:(i * 4 + j + 1) * 128], ps_tr[:])
                r.mark("vector", "vtrv")
                if j < 3:
                    t.wait_ge(r.sem("vtrv"), r.val("vtrv"))
            s.wait_ge(r.sem("qv"), nv)
            s.wait_ge(r.sem("nr"), 2 * i)
            s.activation(nsq[:], qsl, AF.Square)
            ns_ = r.mark("scalar", "nsq")
            t.wait_ge(r.sem("nsq"), ns_)
            t.wait_ge(r.sem("nrv"), r.val("nrv"))
            t.matmul(ps_l[:], onescol[:], nsq[:], start=True, stop=True)
            nt_ = r.mark("tensor", "nr")
            s.wait_ge(r.sem("nr"), nt_)
            s.activation(nsq[:], ksl, AF.Square)
            ns_ = r.mark("scalar", "nsq")
            v.wait_ge(r.sem("nr"), nt_)
            v.reduce_max(nrow[0:1, i:i + 1], ps_l[0:1, :], axis=AX.X)
            r.mark("vector", "nrv")
            t.wait_ge(r.sem("nsq"), ns_)
            t.wait_ge(r.sem("nrv"), r.val("nrv"))
            t.matmul(ps_l[:], onescol[:], nsq[:], start=True, stop=True)
            nt_ = r.mark("tensor", "nr")
            v.wait_ge(r.sem("nr"), nt_)
            v.reduce_max(nrow[0:1, NS + i:NS + i + 1], ps_l[0:1, :], axis=AX.X)
            r.mark("vector", "nrv")

        r.fence("vector")
        v.reduce_max(sc1[0:1, 0:1], nrow[0:1, 0:NS], axis=AX.X)
        v.reduce_max(sc1[0:1, 1:2], nrow[0:1, NS:2 * NS], axis=AX.X)
        r.fence("vector")
        v.tensor_tensor(sc1[0:1, 2:3], sc1[0:1, 0:1], sc1[0:1, 1:2], OP.mult)
        n_ = r.mark("vector", "cb1")
        s.wait_ge(r.sem("cb1"), n_)
        s.activation(sc1[0:1, 3:4], sc1[0:1, 2:3], AF.Sqrt)
        n_ = r.mark("scalar", "cb2")
        t.wait_ge(r.sem("cb2"), n_)
        t.matmul(ps_O[:, 0:1], onesrow[:], sc1[0:1, 3:4], start=True, stop=True)
        n_ = r.mark("tensor", "cb3")
        v.wait_ge(r.sem("cb3"), n_)
        v.tensor_scalar(cb[:], ps_O[:, 0:1], -1.0, None, OP.mult)
        n_cb = r.mark("vector", "cb4")

        if debug:
            g.wait_ge(r.sem("cb4"), n_cb)
            r.dma("gpsimd", "st", dbgq[:], Q[:])
            r.dma("gpsimd", "st", dbgk[:], K[:])
            r.dma("gpsimd", "st", dbgv[:], Vt[:])
            r.dma("gpsimd", "st", dbgc[0:1, 0:4], sc1[0:1, 0:4])
            r.dma("gpsimd", "st", dbgc[1:2, 0:1], cb[0:1, 0:1])

        s.wait_ge(r.sem("cb4"), n_cb)
        batches = []
        for qs in range(NS):
            nkt = 4 * (qs + 1)
            nb = (nkt + 1) // 2
            for b in range(nb):
                kts = [kt for kt in (2 * b, 2 * b + 1) if kt < nkt]
                batches.append((qs, b, nb, kts))
        NB = len(batches)
        fm_cnt = {}
        oev_cnt = {}

        def emit_mm1(bi):
            qs, b, nb, kts = batches[bi]
            half = bi % 2
            qsl_ = Q[:, qs * 512:(qs + 1) * 512]
            if bi >= 2:
                t.wait_ge(r.sem("fx"), bi - 1)
            for ji, kt in enumerate(kts):
                t.matmul(ps_S[:, half * 1024 + ji * 512:half * 1024 + (ji + 1) * 512],
                         K[:, kt * 128:(kt + 1) * 128], qsl_,
                         start=True, stop=True)
            n1 = r.mark("tensor", "f1")
            s.wait_ge(r.sem("f1"), n1)
            if bi >= 2:
                s.wait_ge(r.sem("f2"), bi - 1)
            s.activation(P_T[:, half * 1024:half * 1024 + len(kts) * 512],
                         ps_S[:, half * 1024:half * 1024 + len(kts) * 512],
                         AF.Exp, bias=cb[:])
            nfx = r.mark("scalar", "fx")
            need_mask = [ji for ji, kt in enumerate(kts) if kt >= 4 * qs]
            if need_mask:
                v.wait_ge(r.sem("fx"), nfx)
                for ji in need_mask:
                    j = kts[ji] - 4 * qs
                    v.tensor_tensor(
                        P_T[:, half * 1024 + ji * 512:half * 1024 + (ji + 1) * 512],
                        P_T[:, half * 1024 + ji * 512:half * 1024 + (ji + 1) * 512],
                        cmask_s[:, j * 512:(j + 1) * 512], OP.mult)
                fm_cnt[bi] = r.mark("vector", "fm")

        def emit_mm2(bi):
            qs, b, nb, kts = batches[bi]
            half = bi % 2
            if bi in fm_cnt:
                t.wait_ge(r.sem("fm"), fm_cnt[bi])
            else:
                t.wait_ge(r.sem("fx"), bi + 1)
            if b == 0 and qs >= 1:
                t.wait_ge(r.sem("oev"), oev_cnt[qs - 1])
            for ji, kt in enumerate(kts):
                first = (b == 0 and ji == 0)
                last = (b == nb - 1 and ji == len(kts) - 1)
                t.matmul(ps_O[:], Vt[:, kt * 128:(kt + 1) * 128],
                         P_T[:, half * 1024 + ji * 512:half * 1024 + (ji + 1) * 512],
                         start=first, stop=last)
                t.matmul(ps_l[:], onescol[:],
                         P_T[:, half * 1024 + ji * 512:half * 1024 + (ji + 1) * 512],
                         start=first, stop=last)
            r.mark("tensor", "f2")
            if b == nb - 1:
                emit_norm(qs)

        def emit_norm(qs):
            v.wait_ge(r.sem("f2"), r.val("f2"))
            if qs >= 2:
                v.wait_ge(r.sem("st"), 16 * (qs - 1))
            v.reciprocal(lrec[0:1, :], ps_l[0:1, :])
            nrc = r.mark("vector", "orc")
            t.wait_ge(r.sem("orc"), nrc)
            t.matmul(ps_n[:], onesrow[:], lrec[0:1, :], start=True, stop=True)
            nbc = r.mark("tensor", "obc")
            v.wait_ge(r.sem("obc"), nbc)
            v.tensor_copy(lrec[:], ps_n[:])
            v.tensor_tensor(o_sb[:, (qs % 2) * 512:(qs % 2 + 1) * 512], ps_O[:],
                            lrec[:], OP.mult)
            oev_cnt[qs] = r.mark("vector", "oev")
            g.wait_ge(r.sem("oev"), oev_cnt[qs])
            r.dma("gpsimd", "st", o_out[:, qs * 512:(qs + 1) * 512],
                  o_sb[:, (qs % 2) * 512:(qs % 2 + 1) * 512])

        for bi in range(NB):
            emit_mm1(bi)
            if bi >= 1:
                emit_mm2(bi - 1)
        emit_mm2(NB - 1)

        g.wait_ge(r.sem("st"), 16 * r.val("st"))
        r.flush()
    return nc



def _np_ln(x, eps=LN_EPS):
    m = x.mean(-1, keepdims=True)
    v = x.var(-1, keepdims=True)
    return (x - m) / np.sqrt(v + eps)



def build_l3(T_loc):
    nt = T_loc // 128
    nc = bass.Bass("TRN2", target_bir_lowering=False)
    o_d = nc.dram_tensor("ofm", [M, T_loc], BF16, kind="ExternalInput").ap()
    wao_d = nc.dram_tensor("wao", [M, M], F32, kind="ExternalInput").ap()
    x_d = nc.dram_tensor("x", [T_loc, M], F32, kind="ExternalInput").ap()
    wg_d = nc.dram_tensor("wg", [M, E], F32, kind="ExternalInput").ap()
    eps_d = nc.dram_tensor("epsln", [128, 1], F32, kind="ExternalInput").ap()
    lt_d = nc.dram_tensor("triu_f32", [128, 128], F32, kind="ExternalInput").ap()
    onesrow_d = nc.dram_tensor("ones_row_f32", [1, 128], F32, kind="ExternalInput").ap()
    sel127_d = nc.dram_tensor("sel127", [128, 1], F32, kind="ExternalInput").ap()
    iota512_d = nc.dram_tensor("iota512", [128, 512], F32, kind="ExternalInput").ap()
    iota8_d = nc.dram_tensor("iota8", [128, 8], F32, kind="ExternalInput").ap()
    triu16_d = nc.dram_tensor("triu16_f32", [16, 16], F32, kind="ExternalInput").ap()
    ones16_d = nc.dram_tensor("ones16_f32", [16, 1], F32, kind="ExternalInput").ap()

    x1_o = nc.dram_tensor("x1", [T_loc, M], F32, kind="ExternalOutput").ap()
    tot_scr = nc.dram_tensor("tot_scr", [16, 8], F32, kind="Internal").ap()
    pref_scr = nc.dram_tensor("pref_scr", [16, 8], F32, kind="Internal").ap()
    ln2_o = nc.dram_tensor("ln2", [T_loc, M], BF16, kind="ExternalOutput").ap()
    xsel_o = nc.dram_tensor("xsel", [E * M, P_LOC], BF16, kind="ExternalOutput").ap()
    meta_o = nc.dram_tensor("meta", [T_loc, 4], F32, kind="ExternalOutput").ap()
    cnt_o = nc.dram_tensor("cnt", [1, E], F32, kind="ExternalOutput").ap()

    with ExitStack() as es:
        r = Rec(nc, es)
        t, s, v, g = r.tensor, r.scalar, r.vector, r.gpsimd
        o_fm = r.sbuf("o_fm", [128, 8 * T_loc], BF16)
        wstg = r.sbuf("wstg", [128, M], F32)
        wao_bf = r.sbuf("wao_bf", [128, 8 * M], BF16)
        wg_s = r.sbuf("wg_s", [128, 8 * E], F32)
        wg_bf = r.sbuf("wg_bf", [128, 8 * E], BF16)
        x1 = r.sbuf("x1_s", [128, nt * M], F32)
        ln2 = r.sbuf("ln2_s", [128, nt * M], BF16)
        ln2f = r.sbuf("ln2f", [128, M], BF16)
        xt_s = r.sbuf("xt_s", [128, 2 * M], F32)
        eps_s = r.sbuf("eps_s", [128, 1], F32)
        lt_s = r.sbuf("lt_s", [128, 128], F32)
        onesrow = r.sbuf("onesrow", [1, 128], F32)
        sel127 = r.sbuf("sel127_s", [128, 1], F32)
        iota512 = r.sbuf("iota512_s", [128, 512], F32)
        iota8 = r.sbuf("iota8_s", [128, 8], F32)
        lg = r.sbuf("lg", [128, 8], F32)
        mx = r.sbuf("mx", [128, 4], F32)
        ex8 = r.sbuf("ex8", [128, 8], F32)
        oh = r.sbuf("oh", [128, nt * 8], F32)
        cum = r.sbuf("cum", [128, 8], F32)
        carry = r.sbuf("carry", [1, 8], F32)
        locx = r.sbuf("locx", [128, nt], F32)
        eid = r.sbuf("eid", [128, nt], F32)
        gate = r.sbuf("gate", [128, nt], F32)
        locsel = r.sbuf("locsel", [128, 1], F32)
        dtile = r.sbuf("dtile", [128, nt * 512], BF16)
        lgall = r.sbuf("lgall", [128, nt * 8], F32)
        exall = r.sbuf("exall", [128, nt * 8], F32)
        cumall = r.sbuf("cumall", [128, nt * 8], F32)
        mxa = r.sbuf("mxa", [128, nt], F32)
        nega = r.sbuf("nega", [128, nt], F32)
        suma = r.sbuf("suma", [128, nt], F32)
        ehall = r.sbuf("ehall", [128, nt], F32)
        totrow = r.sbuf("totrow", [1, 128], F32)
        tot16 = r.sbuf("tot16", [16, 8], F32)
        pref16 = r.sbuf("pref16", [16, 8], F32)
        prefrow = r.sbuf("prefrow", [1, 128], F32)
        triu16 = r.sbuf("triu16_s", [16, 16], F32)
        ones16 = r.sbuf("ones16_s", [16, 1], F32)
        ln2f2 = r.sbuf("ln2f2", [128, M], BF16)
        xsel_s = r.sbuf("xsel_sb", [128, 8 * 512], BF16)
        sc = ln_scratch(r, nt, tag="3")
        ps8 = r.psum("ps8", [128, 4096])

        r.dma("sync", "ld", eps_s[:], eps_d[:])
        r.dma("sync", "ld", lt_s[:], lt_d[:])
        r.dma("sync", "ld", onesrow[:], onesrow_d[:])
        r.dma("sync", "ld", sel127[:], sel127_d[:])
        r.dma("sync", "ld", iota512[:], iota512_d[:])
        r.dma("sync", "ld", iota8[:], iota8_d[:])
        r.dma("sync", "ld", triu16[:], triu16_d[:])
        r.dma("sync", "ld", ones16[:], ones16_d[:])
        r.dma("sync", "ld", wg_s[:].rearrange("p (c e) -> p c e", c=8),
              wg_d.rearrange("(c p) e -> p c e", p=128))
        for c in range(8):
            r.dma("sync", "ld", o_fm[:, c * T_loc:(c + 1) * T_loc],
                  o_d[c * 128:(c + 1) * 128, :])
        n_ld = 16 * r.val("ld")
        for c in range(8):
            tgt_w = r.dma("sync", "ldw", wstg[:], wao_d[c * 128:(c + 1) * 128, :])
            v.wait_ge(r.sem("ldw"), tgt_w)
            if c > 0:
                pass
            v.tensor_copy(wao_bf[:, c * M:(c + 1) * M], wstg[:])
            nw = r.mark("vector", "wc")
            if c < 7:
                r.sync.wait_ge(r.sem("wc"), nw)
        v.wait_ge(r.sem("ld"), n_ld)
        v.tensor_copy(wg_bf[:], wg_s[:])
        r.mark("vector", "wc")

        for i in range(nt):
            if i >= 2:
                g.wait_ge(r.sem("xev"), i - 1)
            r.dma("gpsimd", "ldx", xt_s[:, (i % 2) * M:(i % 2 + 1) * M],
                  x_d[i * 128:(i + 1) * 128, :])
        t.wait_ge(r.sem("wc"), 9)
        t.wait_ge(r.sem("ld"), n_ld)
        for i in range(nt):
            if i >= 2:
                t.wait_ge(r.sem("xev"), i - 1)
            half = (i % 2) * 1024
            for c in range(8):
                lhs = o_fm[:, c * T_loc + i * 128:c * T_loc + (i + 1) * 128]
                t.matmul(ps8[:, half:half + 512], lhs,
                         wao_bf[:, c * M:c * M + 512], start=(c == 0), stop=False)
                t.matmul(ps8[:, half + 512:half + 1024], lhs,
                         wao_bf[:, c * M + 512:(c + 1) * M],
                         start=(c == 0), stop=(c == 7))
            nmm = r.mark("tensor", "xmm")
            v.wait_ge(r.sem("xmm"), nmm)
            v.wait_ge(r.sem("ldx"), 16 * (i + 1))
            v.tensor_tensor(x1[:, i * M:(i + 1) * M], ps8[:, half:half + 1024],
                            xt_s[:, (i % 2) * M:(i % 2 + 1) * M], OP.add)
            nev = r.mark("vector", "xev")
            g.wait_ge(r.sem("xev"), nev)
            r.dma("gpsimd", "st", x1_o[i * 128:(i + 1) * 128, :],
                  x1[:, i * M:(i + 1) * M])

        s.wait_ge(r.sem("xev"), nt)
        v.wait_ge(r.sem("xev"), nt)
        dsem, done = emit_ln(r, lambda i: x1[:, i * M:(i + 1) * M],
                             lambda i: ln2[:, i * M:(i + 1) * M], nt, sc, eps_s[:],
                             tag="2")
        g.wait_ge(r.sem(dsem), done)
        for i in range(nt):
            r.dma("gpsimd", "st", ln2_o[i * 128:(i + 1) * 128, :],
                  ln2[:, i * M:(i + 1) * M])

        v.wait_ge(r.sem(dsem), done)
        t.wait_ge(r.sem(dsem), done)
        r.scalar.wait_ge(r.sem(dsem), done)
        for i in range(nt):
            lf = (ln2f if i % 2 == 0 else ln2f2)
            if i > 1:
                r.scalar.wait_ge(r.sem("lgt"), i - 1)
            for c in range(8):
                r.scalar.dma_start_transpose(
                    lf[:, c * 128:(c + 1) * 128],
                    ln2[:, i * M + c * 128:i * M + (c + 1) * 128]
                ).then_inc(r.sem("trp"), 16)
                r.bump("trp")
            t.wait_ge(r.sem("trp"), 16 * 8 * (i + 1))
            if i > 1:
                t.wait_ge(r.sem("lgv"), i - 1)
            for c in range(8):
                t.matmul(ps8[:, 2048 + (i % 2) * 8:2056 + (i % 2) * 8],
                         lf[:, c * 128:(c + 1) * 128],
                         wg_bf[:, c * E:(c + 1) * E], start=(c == 0), stop=(c == 7))
            nlg = r.mark("tensor", "lgt")
            v.wait_ge(r.sem("lgt"), nlg)
            v.tensor_copy(lgall[:, i * 8:(i + 1) * 8],
                          ps8[:, 2048 + (i % 2) * 8:2056 + (i % 2) * 8])
            r.mark("vector", "lgv")
        for i in range(nt):
            v.reduce_max(mxa[:, i:i + 1], lgall[:, i * 8:(i + 1) * 8], axis=AX.X)
        r.fence("vector")
        v.tensor_scalar(nega[:], mxa[:], -1.0, None, OP.mult)
        r.fence("vector")
        for i in range(nt):
            v.tensor_scalar(oh[:, i * 8:(i + 1) * 8], lgall[:, i * 8:(i + 1) * 8],
                            mxa[:, i:i + 1], None, OP.is_equal)
        nv1 = r.mark("vector", "lgv")
        s.wait_ge(r.sem("lgv"), nv1)
        for i in range(nt):
            s.activation(exall[:, i * 8:(i + 1) * 8], lgall[:, i * 8:(i + 1) * 8],
                         AF.Exp, bias=nega[:, i:i + 1])
        nse = r.mark("scalar", "lgs")
        v.wait_ge(r.sem("lgs"), nse)
        for i in range(nt):
            v.reduce_sum(suma[:, i:i + 1], exall[:, i * 8:(i + 1) * 8], axis=AX.X)
        r.fence("vector")
        v.reciprocal(gate[:, 0:nt], suma[:, 0:nt])
        for i in range(nt):
            v.tensor_tensor(exall[:, i * 8:(i + 1) * 8], oh[:, i * 8:(i + 1) * 8],
                            iota8[:], OP.mult)
        for i in range(nt):
            v.reduce_sum(eid[:, i:i + 1], exall[:, i * 8:(i + 1) * 8], axis=AX.X)
        nv2 = r.mark("vector", "lgv")
        t.wait_ge(r.sem("lgv"), nv2)
        for i in range(nt):
            t.matmul(ps8[:, 2176 + i * 8:2184 + i * 8], lt_s[:],
                     oh[:, i * 8:(i + 1) * 8], start=True, stop=True)
        ncm = r.mark("tensor", "cmt")
        v.wait_ge(r.sem("cmt"), ncm)
        v.tensor_copy(cumall[:], ps8[:, 2176:2176 + nt * 8])
        ncv = r.mark("vector", "cmv")
        t.wait_ge(r.sem("cmv"), ncv)
        for i in range(nt):
            t.matmul(ps8[0:1, 2304 + i * 8:2312 + i * 8], sel127[:],
                     cumall[:, i * 8:(i + 1) * 8], start=True, stop=True)
        ntt = r.mark("tensor", "cmt")
        v.wait_ge(r.sem("cmt"), ntt)
        v.tensor_copy(totrow[0:1, 0:nt * 8], ps8[0:1, 2304:2304 + nt * 8])
        ntv = r.mark("vector", "cmv")
        g.wait_ge(r.sem("cmv"), ntv)
        r.dma("gpsimd", "t16",
              tot_scr.rearrange("(o i) e -> o (i e)", o=1)[:, 0:nt * 8],
              totrow[0:1, 0:nt * 8])
        g.wait_ge(r.sem("t16"), 16)
        r.dma("gpsimd", "t16", tot16[0:nt, 0:8], tot_scr[0:nt, :])
        t.wait_ge(r.sem("t16"), 32)
        t.matmul(ps8[0:nt, 2432:2440], triu16[0:nt, 0:nt], tot16[0:nt, :],
                 start=True, stop=True)
        t.matmul(ps8[0:1, 2448:2456], ones16[0:nt, :], tot16[0:nt, :],
                 start=True, stop=True)
        npf = r.mark("tensor", "cmt")
        v.wait_ge(r.sem("cmt"), npf)
        v.tensor_copy(pref16[0:nt, :], ps8[0:nt, 2432:2440])
        v.tensor_copy(carry[:], ps8[0:1, 2448:2456])
        npv = r.mark("vector", "crv")
        g.wait_ge(r.sem("crv"), npv)
        r.dma("gpsimd", "t16", pref_scr[0:nt, :], pref16[0:nt, :])
        g.wait_ge(r.sem("t16"), 48)
        r.dma("gpsimd", "t16", prefrow[0:1, 0:nt * 8],
              pref_scr.rearrange("(o i) e -> o (i e)", o=1)[:, 0:nt * 8])
        t.wait_ge(r.sem("t16"), 64)
        for i in range(nt):
            t.matmul(ps8[:, 1536 + i * 8:1544 + i * 8], onesrow[:],
                     prefrow[0:1, i * 8:(i + 1) * 8], start=True, stop=True)
        nbc = r.mark("tensor", "cmt")
        v.wait_ge(r.sem("cmt"), nbc)
        for i in range(nt):
            v.tensor_tensor(cumall[:, i * 8:(i + 1) * 8],
                            cumall[:, i * 8:(i + 1) * 8],
                            ps8[:, 1536 + i * 8:1544 + i * 8], OP.add)
        for i in range(nt):
            v.tensor_tensor(cumall[:, i * 8:(i + 1) * 8],
                            cumall[:, i * 8:(i + 1) * 8],
                            oh[:, i * 8:(i + 1) * 8], OP.subtract)
        for i in range(nt):
            v.tensor_tensor(cumall[:, i * 8:(i + 1) * 8],
                            cumall[:, i * 8:(i + 1) * 8],
                            oh[:, i * 8:(i + 1) * 8], OP.mult)
        for i in range(nt):
            v.reduce_sum(locx[:, i:i + 1], cumall[:, i * 8:(i + 1) * 8],
                         axis=AX.X)
        r.mark("vector", "locv")
        g.wait_ge(r.sem("crv"), npv)
        r.dma("gpsimd", "st", cnt_o[:], carry[:])

        mz = r.sbuf("mz", [128, nt * 4], F32)
        v.memset(mz[:], 0.0)
        r.fence("vector")
        for i in range(nt):
            v.tensor_copy(mz[:, i * 4:i * 4 + 1], eid[:, i:i + 1])
            v.tensor_copy(mz[:, i * 4 + 1:i * 4 + 2], locx[:, i:i + 1])
            v.tensor_copy(mz[:, i * 4 + 2:i * 4 + 3], gate[:, i:i + 1])
        nmz = r.mark("vector", "mzv")
        g.wait_ge(r.sem("mzv"), nmz)
        for i in range(nt):
            r.dma("gpsimd", "st", meta_o[i * 128:(i + 1) * 128, :],
                  mz[:, i * 4:(i + 1) * 4])

        v.wait_ge(r.sem("locv"), r.val("locv"))
        for e in range(E):
            if e > 0:
                v.wait_ge(r.sem("dmm"), e)
            for i in range(nt):
                v.tensor_scalar(ehall[:, i:i + 1], eid[:, i:i + 1], float(e),
                                None, OP.is_equal)
            for i in range(nt):
                v.tensor_scalar(dtile[:, i * 512:(i + 1) * 512], iota512[:],
                                locx[:, i:i + 1], ehall[:, i:i + 1],
                                OP.is_equal, OP.mult)
            ndt = r.mark("vector", "dbt")
            t.wait_ge(r.sem("dbt"), ndt)
            if e > 0:
                t.wait_ge(r.sem("dse"), e)
            for i in range(nt):
                for mc in range(8):
                    t.matmul(ps8[:, mc * 512:(mc + 1) * 512],
                             ln2[:, i * M + mc * 128:i * M + (mc + 1) * 128],
                             dtile[:, i * 512:(i + 1) * 512],
                             start=(i == 0), stop=(i == nt - 1))
            ndm = r.mark("tensor", "dmm")
            v.wait_ge(r.sem("dmm"), ndm)
            for mc in range(8):
                v.tensor_copy(xsel_s[:, mc * 512:(mc + 1) * 512],
                              ps8[:, mc * 512:(mc + 1) * 512])
            nds = r.mark("vector", "dse")
            g.wait_ge(r.sem("dse"), nds)
            for mc in range(8):
                r.dma("gpsimd", "st", xsel_o[e * M + mc * 128:e * M + (mc + 1) * 128, :],
                      xsel_s[:, mc * 512:(mc + 1) * 512])

        g.wait_ge(r.sem("st"), 16 * r.val("st"))
        r.flush()
    return nc



def build_l3b():
    NSLOT = N_CORES * P_LOC
    nseg = NSLOT // 512
    nc = bass.Bass("TRN2", target_bir_lowering=False)
    xsel_d = nc.dram_tensor("xsel", [M, NSLOT], BF16, kind="ExternalInput").ap()
    w1_d = nc.dram_tensor("w1", [M, FFN], F32, kind="ExternalInput").ap()
    w2_d = nc.dram_tensor("w2", [FFN, M], F32, kind="ExternalInput").ap()
    eo_o = nc.dram_tensor("eo", [NSLOT, M], BF16, kind="ExternalOutput").ap()

    with ExitStack() as es:
        r = Rec(nc, es)
        t, s, v, g = r.tensor, r.scalar, r.vector, r.gpsimd
        w1_bf = r.sbuf("w1_bf", [128, 8 * FFN], BF16)
        w2_bf = r.sbuf("w2_bf", [128, 32 * M], BF16)
        xs_s = r.sbuf("xs_s", [128, 2 * 8 * 512], BF16)
        h_s = r.sbuf("h_s", [128, 32 * 512], BF16)
        eo_s = r.sbuf("eo_s", [128, 2 * 4 * M], BF16)
        ps8 = r.psum("ps8", [128, 4096])

        for j in range(8):
            r.dma("gpsimd", "ldw1",
                  w1_bf[:].rearrange("p (c n) -> p c n", c=8)
                  [:, :, j * 512:(j + 1) * 512],
                  w1_d[:, j * 512:(j + 1) * 512].rearrange(
                      "(c p) n -> p c n", p=128))
            r.dma("gpsimd", "ldw2",
                  w2_bf[:].rearrange("p (c n) -> p c n", c=32)
                  [:, j * 4:(j + 1) * 4, :],
                  w2_d[j * 4 * 128:(j + 1) * 4 * 128, :].rearrange(
                      "(c p) n -> p c n", p=128))
        for sg in range(nseg):
            if sg >= 2:
                r.sync.wait_ge(r.sem("h1t"), 32 * (sg - 1))
            r.dma("sync", "ldx",
                  xs_s[:, (sg % 2) * 4096:(sg % 2 + 1) * 4096].rearrange(
                      "p (c t) -> p c t", c=8),
                  xsel_d[:, sg * 512:(sg + 1) * 512].rearrange(
                      "(c p) t -> p c t", p=128))
        for sg in range(nseg):
            xb = xs_s[:, (sg % 2) * 4096:(sg % 2 + 1) * 4096]
            eb = eo_s[:, (sg % 2) * 4096:(sg % 2 + 1) * 4096]
            t.wait_ge(r.sem("ldx"), 16 * (sg + 1))
            for fg in range(4):
                if sg == 0:
                    t.wait_ge(r.sem("ldw1"), 16 * (2 * fg + 2))
                if sg > 0 or fg > 0:
                    t.wait_ge(r.sem("hev"), 8 * (4 * sg + fg))
                if fg == 0 and sg > 0:
                    t.wait_ge(r.sem("eev"), 4 * sg)
                for fc8 in range(8):
                    f = fg * 8 + fc8
                    for mc in range(8):
                        t.matmul(ps8[:, fc8 * 512:(fc8 + 1) * 512],
                                 w1_bf[:, mc * FFN + f * 128:mc * FFN + (f + 1) * 128],
                                 xb[:, mc * 512:(mc + 1) * 512],
                                 start=(mc == 0), stop=(mc == 7))
                    r.mark("tensor", "h1t")
                nh = r.val("h1t")
                s.wait_ge(r.sem("h1t"), nh)
                for fc8 in range(8):
                    f = fg * 8 + fc8
                    s.activation(h_s[:, f * 512:(f + 1) * 512],
                                 ps8[:, fc8 * 512:(fc8 + 1) * 512], AF.Relu)
                    r.mark("scalar", "hev")
            nhe = r.val("hev")
            t.wait_ge(r.sem("hev"), nhe)
            if sg >= 2:
                t.wait_ge(r.sem("eev"), 4 * (sg - 1))
            for st in range(4):
                for fc in range(32):
                    if sg == 0 and st == 0 and fc % 4 == 0:
                        t.wait_ge(r.sem("ldw2"), 16 * (fc // 4 + 1))
                    t.matmul(ps8[:, st * 1024:st * 1024 + 512],
                             h_s[:, fc * 512 + st * 128:fc * 512 + (st + 1) * 128],
                             w2_bf[:, fc * M:fc * M + 512],
                             start=(fc == 0), stop=False)
                    t.matmul(ps8[:, st * 1024 + 512:(st + 1) * 1024],
                             h_s[:, fc * 512 + st * 128:fc * 512 + (st + 1) * 128],
                             w2_bf[:, fc * M + 512:(fc + 1) * M],
                             start=(fc == 0), stop=(fc == 31))
            ne2 = r.mark("tensor", "e2t")
            v.wait_ge(r.sem("e2t"), ne2)
            for st in range(4):
                v.tensor_copy(eb[:, st * M:(st + 1) * M],
                              ps8[:, st * 1024:(st + 1) * 1024])
                r.mark("vector", "eev")
            nev = r.val("eev")
            g.wait_ge(r.sem("eev"), nev)
            for st in range(4):
                r.dma("gpsimd", "st",
                      eo_o[sg * 512 + st * 128:sg * 512 + (st + 1) * 128, :],
                      eb[:, st * M:(st + 1) * M])
        g.wait_ge(r.sem("st"), 16 * r.val("st"))
        r.flush()
    return nc



def build_l4(T_loc, T_total):
    nt = T_loc // 128
    CAP = T_total // E
    nc = bass.Bass("TRN2", target_bir_lowering=False)
    x1_d = nc.dram_tensor("x1", [T_loc, M], F32, kind="ExternalInput").ap()
    eo_d = nc.dram_tensor("eo", [E * P_LOC, M], BF16, kind="ExternalInput").ap()
    meta_d = nc.dram_tensor("meta", [T_loc, 4], F32, kind="ExternalInput").ap()
    cnts_d = nc.dram_tensor("cnts", [N_CORES, E], F32, kind="ExternalInput").ap()
    pmask_d = nc.dram_tensor("prevmask", [N_CORES, 1], F32, kind="ExternalInput").ap()
    ident_d = nc.dram_tensor("ident_f32", [128, 128], F32, kind="ExternalInput").ap()
    onesrow_d = nc.dram_tensor("ones_row_f32", [1, 128], F32, kind="ExternalInput").ap()
    iota8_d = nc.dram_tensor("iota8", [128, 8], F32, kind="ExternalInput").ap()
    slotid_d = nc.dram_tensor("slotid", [128, 1], F32, kind="ExternalInput").ap()
    x2_o = nc.dram_tensor("x2", [T_loc, M], F32, kind="ExternalOutput").ap()

    with ExitStack() as es:
        r = Rec(nc, es)
        t, s, v, g = r.tensor, r.scalar, r.vector, r.gpsimd
        x1 = r.sbuf("x1_s", [128, nt * M], F32)
        eo_s = r.sbuf("eo_sb", [128, 32 * M], BF16)
        meta_s = r.sbuf("meta_sb", [128, nt * 4], F32)
        cnts_s = r.sbuf("cnts_s", [N_CORES, E], F32)
        pmask_s = r.sbuf("pmask_s", [N_CORES, 1], F32)
        ident_s = r.sbuf("ident_fs", [128, 128], F32)
        onesrow = r.sbuf("onesrow_s", [1, 128], F32)
        iota8 = r.sbuf("iota8_sb", [128, 8], F32)
        slotid = r.sbuf("slotid_s", [128, 1], F32)
        off_bc = r.sbuf("off_bc", [128, 8], F32)
        ohx = r.sbuf("ohx", [128, 8], F32)
        colv = r.sbuf("colv", [128, 4], F32)
        rows = r.sbuf("rows", [1, 3 * T_loc], F32)
        bce = r.sbuf("bce", [128, T_loc], F32)
        bcl = r.sbuf("bcl", [128, T_loc], F32)
        bcc = r.sbuf("bcc", [128, T_loc], BF16)
        lsel = r.sbuf("lsel", [128, T_loc], F32)
        dte = r.sbuf("dte", [128, 4 * T_loc], BF16)
        ps8 = r.psum("ps8", [128, 4096])

        for dst, srcd in ((cnts_s, cnts_d), (pmask_s, pmask_d), (ident_s, ident_d),
                          (onesrow, onesrow_d), (iota8, iota8_d), (slotid, slotid_d)):
            r.dma("sync", "ld", dst[:], srcd[:])
        for i in range(nt):
            r.dma("sync", "ld", x1[:, i * M:(i + 1) * M], x1_d[i * 128:(i + 1) * 128, :])
            r.dma("sync", "ld", meta_s[:, i * 4:(i + 1) * 4],
                  meta_d[i * 128:(i + 1) * 128, :])
        n_ld = 16 * r.val("ld")
        r.dma("gpsimd", "ldeo", eo_s[:].rearrange("p (c n) -> p c n", c=32),
              eo_d.rearrange("(c p) n -> p c n", p=128))
        n_eo = 16 * r.val("ldeo")

        t.wait_ge(r.sem("ld"), n_ld)
        t.matmul(ps8[0:1, 0:8], pmask_s[:], cnts_s[:], start=True, stop=True)
        nof = r.mark("tensor", "oft")
        v.wait_ge(r.sem("oft"), nof)
        v.tensor_copy(rows[0:1, 0:8], ps8[0:1, 0:8])
        nov = r.mark("vector", "ofv")
        t.wait_ge(r.sem("ofv"), nov)
        t.matmul(ps8[:, 16:24], onesrow[:], rows[0:1, 0:8], start=True, stop=True)
        nof = r.mark("tensor", "oft")
        v.wait_ge(r.sem("oft"), nof)
        v.tensor_copy(off_bc[:], ps8[:, 16:24])
        r.fence("vector")

        v.wait_ge(r.sem("ld"), n_ld)
        for i in range(nt):
            eidc = meta_s[:, i * 4:i * 4 + 1]
            locc = meta_s[:, i * 4 + 1:i * 4 + 2]
            gatec = meta_s[:, i * 4 + 2:i * 4 + 3]
            v.tensor_scalar(ohx[:], iota8[:], eidc, None, OP.is_equal)
            r.fence("vector")
            v.tensor_tensor(ohx[:], ohx[:], off_bc[:], OP.mult)
            r.fence("vector")
            v.reduce_sum(colv[:, 0:1], ohx[:], axis=AX.X)
            r.fence("vector")
            v.tensor_tensor(colv[:, 1:2], colv[:, 0:1], locc, OP.add)
            r.fence("vector")
            v.tensor_scalar(colv[:, 2:3], colv[:, 1:2], float(CAP), None, OP.is_lt)
            v.tensor_scalar(colv[:, 3:4], locc, float(P_LOC), None, OP.is_lt)
            r.fence("vector")
            v.tensor_tensor(colv[:, 2:3], colv[:, 2:3], colv[:, 3:4], OP.mult)
            r.fence("vector")
            v.tensor_tensor(colv[:, 3:4], colv[:, 2:3], gatec, OP.mult)
            nvc = r.mark("vector", "cfv")
            t.wait_ge(r.sem("cfv"), nvc)
            t.matmul(ps8[0:1, 1024:1152], eidc, ident_s[:], start=True, stop=True)
            t.matmul(ps8[0:1, 1152:1280], locc, ident_s[:], start=True, stop=True)
            t.matmul(ps8[0:1, 1280:1408], colv[:, 3:4], ident_s[:], start=True,
                     stop=True)
            nrt = r.mark("tensor", "rwt")
            v.wait_ge(r.sem("rwt"), nrt)
            v.tensor_copy(rows[0:1, i * 128:(i + 1) * 128], ps8[0:1, 1024:1152])
            v.tensor_copy(rows[0:1, T_loc + i * 128:T_loc + (i + 1) * 128],
                          ps8[0:1, 1152:1280])
            v.tensor_copy(rows[0:1, 2 * T_loc + i * 128:2 * T_loc + (i + 1) * 128],
                          ps8[0:1, 1280:1408])
            r.mark("vector", "rwv")
        SEG = 512 if T_loc % 512 == 0 else T_loc
        nseg3 = T_loc // SEG
        for whichsb, ofs in ((bce, 0), (bcl, T_loc), (bcc, 2 * T_loc)):
            for sgi in range(nseg3):
                t.wait_ge(r.sem("rwv"), nt)
                t.matmul(ps8[:, 2048:2048 + SEG], onesrow[:],
                         rows[0:1, ofs + sgi * SEG:ofs + (sgi + 1) * SEG],
                         start=True, stop=True)
                nbc = r.mark("tensor", "bct")
                v.wait_ge(r.sem("bct"), nbc)
                v.tensor_copy(whichsb[:, sgi * SEG:(sgi + 1) * SEG],
                              ps8[:, 2048:2048 + SEG])
                nbv = r.mark("vector", "bcv")
                t.wait_ge(r.sem("bcv"), nbv)

        g.wait_ge(r.sem("ldeo"), n_eo)
        v.wait_ge(r.sem("ld"), n_ld)
        for e in range(E):
            v.wait_ge(r.sem("bcv"), 3 * nseg3)
            if e > 0:
                v.wait_ge(r.sem("cbt"), e * nt)
            v.tensor_scalar(lsel[:], bce[:], float(e), None, OP.is_equal)
            v.tensor_scalar(dte[:, 0:T_loc], bce[:], float(e), None, OP.is_equal)
            r.fence("vector")
            v.tensor_tensor(lsel[:], lsel[:], bcl[:], OP.mult)
            r.fence("vector")
            v.tensor_tensor(lsel[:], lsel[:], dte[:, 0:T_loc], OP.add)
            r.fence("vector")
            v.tensor_scalar(lsel[:], lsel[:], -1.0, None, OP.add)
            r.fence("vector")
            for st in range(4):
                v.tensor_scalar(dte[:, st * T_loc:(st + 1) * T_loc], lsel[:],
                                slotid[:], None, OP.is_equal)
                r.fence("vector")
                v.tensor_tensor(dte[:, st * T_loc:(st + 1) * T_loc],
                                dte[:, st * T_loc:(st + 1) * T_loc], bcc[:], OP.mult)
                if st < 3:
                    v.tensor_scalar(lsel[:], lsel[:], -128.0, None, OP.add)
                    r.fence("vector")
            ndb = r.mark("vector", "dbv")
            t.wait_ge(r.sem("dbv"), ndb)
            for i in range(nt):
                k = e * nt + i
                half = (i % 2) * 2048
                if k >= 2:
                    t.wait_ge(r.sem("cbv"), k - 1)
                for st in range(4):
                    lhs = dte[:, st * T_loc + i * 128:st * T_loc + (i + 1) * 128]
                    rhsbase = (e * 4 + st) * M
                    t.matmul(ps8[:, half:half + 512], lhs,
                             eo_s[:, rhsbase:rhsbase + 512],
                             start=(st == 0), stop=False)
                    t.matmul(ps8[:, half + 512:half + 1024], lhs,
                             eo_s[:, rhsbase + 512:rhsbase + M],
                             start=(st == 0), stop=(st == 3))
                ncb = r.mark("tensor", "cbt")
                v.wait_ge(r.sem("cbt"), ncb)
                v.tensor_tensor(x1[:, i * M:(i + 1) * M], ps8[:, half:half + 1024],
                                x1[:, i * M:(i + 1) * M], OP.add)
                r.mark("vector", "cbv")
        nxv = r.val("cbv")
        g.wait_ge(r.sem("cbv"), nxv)
        for i in range(nt):
            r.dma("gpsimd", "st", x2_o[i * 128:(i + 1) * 128, :],
                  x1[:, i * M:(i + 1) * M])
        g.wait_ge(r.sem("st"), 16 * r.val("st"))
        r.flush()
    return nc



def build_l5(T_loc, debug=False):
    NCH = T_loc // RCHUNK
    HALF = min(T_loc, 1024)
    NHALF = T_loc // HALF
    CH_H = HALF // RCHUNK
    SUP = max(1, HALF // 512)
    SW = HALF // SUP
    nc = bass.Bass("TRN2", target_bir_lowering=False)
    x2_d = nc.dram_tensor("x2", [T_loc, M], F32, kind="ExternalInput").ap()
    prev_d = nc.dram_tensor("prevrow", [1, M], F32, kind="ExternalInput").ap()
    wbig_d = {k: nc.dram_tensor(k, [M, M], F32, kind="ExternalInput").ap()
              for k in ("Wr", "Wk", "Wv")}
    w1_d = nc.dram_tensor("w1", [M, 64], F32, kind="ExternalInput").ap()
    w2_d = nc.dram_tensor("w2", [64, M], F32, kind="ExternalInput").ap()
    a1_d = nc.dram_tensor("a1", [M, 64], F32, kind="ExternalInput").ap()
    a2_d = nc.dram_tensor("a2", [64, M], F32, kind="ExternalInput").ap()
    g1_d = nc.dram_tensor("g1", [M, 128], F32, kind="ExternalInput").ap()
    g2_d = nc.dram_tensor("g2", [128, M], F32, kind="ExternalInput").ap()
    cv_d = {nm: nc.dram_tensor("cv_" + nm, [128, 8], F32, kind="ExternalInput").ap()
            for nm in ("w0", "a0", "k_k", "k_a", "rk",
                       "mr", "mw", "mk", "mv", "ma", "mg")}
    eps_d = nc.dram_tensor("epsln", [128, 1], F32, kind="ExternalInput").ap()
    triu1_d = nc.dram_tensor("triu1_bf", [128, 128], BF16, kind="ExternalInput").ap()
    tril1_d = nc.dram_tensor("tril1_bf", [128, 128], BF16, kind="ExternalInput").ap()
    triu0_d = nc.dram_tensor("triu0_bf", [128, 128], BF16, kind="ExternalInput").ap()
    ident_d = nc.dram_tensor("ident", [128, 128], BF16, kind="ExternalInput").ap()
    bones_d = nc.dram_tensor("blockones64", [128, 2], BF16, kind="ExternalInput").ap()
    sel2_d = nc.dram_tensor("sel2", [2, 128], BF16, kind="ExternalInput").ap()
    eye64_d = nc.dram_tensor("eye64_f32", [64, 64], F32, kind="ExternalInput").ap()
    bmask_d = nc.dram_tensor("bmask_bf", [128, 640], BF16, kind="ExternalInput").ap()

    gfm_o = nc.dram_tensor("gfm", [M, T_loc], BF16, kind="ExternalOutput").ap()
    rtfm_o = nc.dram_tensor("rtfm", [M, T_loc], BF16, kind="ExternalOutput").ap()
    rhat_o = nc.dram_tensor("rhblob", [NCH * RH * 64, 128], BF16,
                            kind="ExternalOutput").ap()
    yloc_o = nc.dram_tensor("ylblob", [NCH * RH * 64, 128], BF16,
                            kind="ExternalOutput").ap()
    phi_o = nc.dram_tensor("phiblob", [NCH * RH * 64, 64], F32,
                           kind="ExternalOutput").ap()
    zp_o = nc.dram_tensor("zpblob", [NCH * RH * 64, 64], F32,
                          kind="ExternalOutput").ap()
    wl_o = nc.dram_tensor("wlblob", [NHALF * 128, 8 * CH_H], F32,
                          kind="ExternalOutput").ap()
    spp_o = nc.dram_tensor("span_P", [RH * 64, 64], F32, kind="ExternalOutput").ap()
    spq_o = nc.dram_tensor("span_Q", [RH * 64, 64], F32, kind="ExternalOutput").ap()
    if debug:
        dbg_o = {nm: nc.dram_tensor("dbg_" + nm, [128, 8 * min(T_loc, 1024)], BF16,
                                    kind="ExternalOutput").ap()
                 for nm in ("lnf", "rf", "kf", "vf", "agf", "lwf", "kkn",
                            "dRf", "dKf", "dBf", "dAf")}
        dbg2_o = nc.dram_tensor("dbg_b1", [128, 1024], BF16,
                                kind="ExternalOutput").ap()
        dbg3_o = nc.dram_tensor("dbg_phi", [64, 128], F32,
                                kind="ExternalOutput").ap()

    with ExitStack() as es:
        r = Rec(nc, es)
        t, s, v, g = r.tensor, r.scalar, r.vector, r.gpsimd
        eps_s = r.sbuf("eps_sb", [128, 1], F32)
        triu1 = r.sbuf("triu1", [128, 128], BF16)
        tril1 = r.sbuf("tril1", [128, 128], BF16)
        triu0 = r.sbuf("triu0", [128, 128], BF16)
        ident_s = r.sbuf("ident_sb", [128, 128], BF16)
        bones = r.sbuf("bones", [128, 2], BF16)
        sel2 = r.sbuf("sel2_s", [2, 128], BF16)
        eye64 = r.sbuf("eye64_s", [64, 64], F32)
        prow = r.sbuf("prow", [1, M], F32)
        pstat = r.sbuf("pstat", [1, 8], F32)
        cvec = {nm: r.sbuf("cvs_" + nm, [128, 8], F32) for nm in cv_d}
        cneg = {nm: r.sbuf("cvn_" + nm, [128, 8], F32)
                for nm in ("mr", "mw", "mk", "mv", "ma", "mg")}
        w1s = r.sbuf("w1s", [128, 8 * 64], BF16)
        w2s = r.sbuf("w2s", [64, M], BF16)
        a1s = r.sbuf("a1s", [128, 8 * 64], BF16)
        a2s = r.sbuf("a2s", [64, M], BF16)
        g1s = r.sbuf("g1s", [128, 8 * 128], BF16)
        g2s = r.sbuf("g2s", [128, M], BF16)
        wbig = r.sbuf("wbig", [128, 8 * 512], BF16)
        x2t = r.sbuf("x2t", [128, 2 * M], F32)
        ln3 = r.sbuf("ln3_sb", [128, 2 * M], BF16)
        lnf = r.sbuf("lnf", [128, 8 * (HALF + 16)], BF16)
        stash8 = r.sbuf("stash8", [128, 8], BF16)
        mixb = r.sbuf("mixb", [128, 8 * HALF], BF16)
        rf = r.sbuf("rf", [128, 8 * HALF], BF16)
        kf = r.sbuf("kf", [128, 8 * HALF], BF16)
        vf = r.sbuf("vf", [128, 8 * HALF], BF16)
        kkn = r.sbuf("kkn", [128, 8 * HALF], BF16)
        agf = r.sbuf("agf", [128, 8 * HALF], BF16)
        lwf = r.sbuf("lwf", [128, 8 * HALF], BF16)
        Af, Bf, Kf, Rf = kkn, agf, kf, rf
        wlh = r.sbuf("wlh", [128, 8 * CH_H], F32)
        lnstat = r.sbuf("lnstat", [128, 32], F32)
        nrmsc = r.sbuf("nrmsc", [2, 512], BF16)
        SBb0 = r.sbuf("SBb0", [128, 768], BF16)
        SBb1 = r.sbuf("SBb1", [128, 768], BF16)
        Tpb0 = r.sbuf("Tpb0", [128, 256], BF16)
        Tpb1 = r.sbuf("Tpb1", [128, 256], BF16)
        bmask_s = r.sbuf("bmask_s", [128, 640], BF16)
        tscr = r.sbuf("tscr", [128, 512], BF16)
        zer128 = r.sbuf("zer128", [128, 128], BF16)
        tscr2 = r.sbuf("tscr2", [128, 512], BF16)
        tf32 = r.sbuf("tf32", [128, 128], F32)
        outY = r.sbuf("outY", [64, RH * 128], BF16)
        outR = r.sbuf("outR", [64, RH * 128], BF16)
        outP = r.sbuf("outP", [64, RH * 64], F32)
        outZ = r.sbuf("outZ", [64, RH * 64], F32)
        spanP = r.sbuf("spanP", [64, RH * 64], F32)
        spanQ = r.sbuf("spanQ", [64, RH * 64], F32)
        ps8 = r.psum("ps8", [128, 3584])
        psT = r.psum("psT", [128, 512], BF16)

        for dst, srcd in ((eps_s, eps_d), (triu1, triu1_d), (tril1, tril1_d),
                          (triu0, triu0_d), (ident_s, ident_d), (bones, bones_d),
                          (sel2, sel2_d), (eye64, eye64_d), (prow, prev_d),
                          (bmask_s, bmask_d)):
            r.dma("sync", "ld", dst[:], srcd[:])
        for nm in cv_d:
            r.dma("sync", "ld", cvec[nm][:], cv_d[nm][:])
        r.dma("gpsimd", "ldw", w1s[:].rearrange("p (c n) -> p c n", c=8),
              w1_d.rearrange("(c p) n -> p c n", p=128))
        r.dma("gpsimd", "ldw", w2s[:], w2_d[:])
        r.dma("gpsimd", "ldw", a1s[:].rearrange("p (c n) -> p c n", c=8),
              a1_d.rearrange("(c p) n -> p c n", p=128))
        r.dma("gpsimd", "ldw", a2s[:], a2_d[:])
        r.dma("gpsimd", "ldw", g1s[:].rearrange("p (c n) -> p c n", c=8),
              g1_d.rearrange("(c p) n -> p c n", p=128))
        n_ldw = r.dma("gpsimd", "ldw", g2s[:], g2_d[:])
        n_ld = 16 * r.val("ld")

        v.wait_ge(r.sem("ld"), n_ld)
        for nm in cneg:
            v.tensor_scalar(cneg[nm][:], cvec[nm][:], -1.0, None, OP.mult)
        r.fence("vector")
        for nm in cneg:
            v.tensor_scalar(cneg[nm][:], cneg[nm][:], 1.0, None, OP.add)
        r.fence("vector")
        v.memset(spanQ[:], 0.0)
        v.memset(spanP[:], 0.0)
        r.fence("vector")
        for h in range(RH):
            v.tensor_tensor(spanP[:, h * 64:(h + 1) * 64],
                            spanP[:, h * 64:(h + 1) * 64], eye64[:], OP.add)
        _spinit = r.mark("vector", "b1v")

        s.wait_ge(r.sem("ld"), n_ld)
        s.activation(x2t[0:1, 0:M], prow[:], AF.Square)
        nps = r.mark("scalar", "pls")
        v.wait_ge(r.sem("pls"), nps)
        v.reduce_sum(pstat[0:1, 0:1], prow[:], axis=AX.X)
        v.reduce_sum(pstat[0:1, 1:2], x2t[0:1, 0:M], axis=AX.X)
        r.fence("vector")
        v.tensor_scalar(pstat[0:1, 2:3], pstat[0:1, 0:1], 1.0 / M, None, OP.mult)
        v.tensor_scalar(pstat[0:1, 3:4], pstat[0:1, 1:2], 1.0 / M, None, OP.mult)
        r.fence("vector")
        v.tensor_tensor(pstat[0:1, 4:5], pstat[0:1, 2:3], pstat[0:1, 2:3], OP.mult)
        r.fence("vector")
        v.tensor_tensor(pstat[0:1, 3:4], pstat[0:1, 3:4], pstat[0:1, 4:5], OP.subtract)
        npv = r.mark("vector", "plv")
        s.wait_ge(r.sem("plv"), npv)
        s.activation(pstat[0:1, 3:4], pstat[0:1, 3:4], AF.Sqrt, bias=eps_s[0:1, :])
        nps = r.mark("scalar", "pls")
        v.wait_ge(r.sem("pls"), nps)
        v.reciprocal(pstat[0:1, 5:6], pstat[0:1, 3:4])
        r.fence("vector")
        v.tensor_tensor(pstat[0:1, 6:7], pstat[0:1, 2:3], pstat[0:1, 5:6], OP.mult)
        r.fence("vector")
        v.tensor_scalar(pstat[0:1, 6:7], pstat[0:1, 6:7], -1.0, None, OP.mult)
        npv = r.mark("vector", "plv")
        s.wait_ge(r.sem("plv"), npv)
        s.activation(prow[:], prow[:], AF.Identity, bias=pstat[0:1, 6:7],
                     scale=pstat[0:1, 5:6])
        n_prow = r.mark("scalar", "pls")
        prow_dram = nc.dram_tensor("prow_scr", [1, M], F32, kind="Internal").ap()
        g.wait_ge(r.sem("pls"), n_prow)
        r.dma("gpsimd", "prd", prow_dram[:], prow[:])

        gc = 0
        b1_pi = 0
        b1_state = {}
        b1_state_s = {}
        span_cnt = {h: ("b1v", _spinit) for h in range(RH)}
        last_vsp = [("b1v", _spinit)]
        blb_gate = 0
        for hf in range(NHALF):
            if hf > 0:
                t.wait_ge(r.sem("b1v"), r.val("b1v"))
                t.wait_ge(r.sem("b1s"), r.val("b1s"))
                s.wait_ge(r.sem("b1v"), r.val("b1v"))
            tb = hf * HALF
            ntile = HALF // 128
            a0_marks = []
            if hf > 0:
                g.wait_ge(r.sem("lap"), r.val("lap"))
            for i in range(ntile):
                if i >= 2:
                    g.wait_ge(r.sem("lnv"), a0_marks[i - 2])
                r.dma("gpsimd", "lxd", x2t[:, (i % 2) * M:(i % 2 + 1) * M],
                      x2_d[tb + i * 128:tb + (i + 1) * 128, :])
                s.wait_ge(r.sem("lxd"), 16 * r.val("lxd"))
                s.activation(ps8[:, (i % 2) * 1024:(i % 2 + 1) * 1024],
                             x2t[:, (i % 2) * M:(i % 2 + 1) * M], AF.Square)
                nss = r.mark("scalar", "lns")
                v.wait_ge(r.sem("lns"), nss)
                v.reduce_sum(lnstat[:, i:i + 1], x2t[:, (i % 2) * M:(i % 2 + 1) * M],
                             axis=AX.X)
                v.reduce_sum(lnstat[:, 8 + i:9 + i],
                             ps8[:, (i % 2) * 1024:(i % 2 + 1) * 1024],
                             axis=AX.X)
                a0_marks.append(r.mark("vector", "lnv"))
            r.fence("vector")
            v.tensor_scalar(lnstat[:, 16:16 + ntile], lnstat[:, 0:ntile], 1.0 / M,
                            None, OP.mult)
            v.tensor_scalar(lnstat[:, 8:8 + ntile], lnstat[:, 8:8 + ntile], 1.0 / M,
                            None, OP.mult)
            r.fence("vector")
            v.tensor_tensor(lnstat[:, 24:24 + ntile], lnstat[:, 16:16 + ntile],
                            lnstat[:, 16:16 + ntile], OP.mult)
            r.fence("vector")
            v.tensor_tensor(lnstat[:, 8:8 + ntile], lnstat[:, 8:8 + ntile],
                            lnstat[:, 24:24 + ntile], OP.subtract)
            nlv = r.mark("vector", "lnv2")
            s.wait_ge(r.sem("lnv2"), nlv)
            s.activation(lnstat[:, 8:8 + ntile], lnstat[:, 8:8 + ntile], AF.Sqrt,
                         bias=eps_s[:])
            nls = r.mark("scalar", "lns2")
            v.wait_ge(r.sem("lns2"), nls)
            v.reciprocal(lnstat[:, 24:24 + ntile], lnstat[:, 8:8 + ntile])
            r.fence("vector")
            v.tensor_tensor(lnstat[:, 16:16 + ntile], lnstat[:, 16:16 + ntile],
                            lnstat[:, 24:24 + ntile], OP.mult)
            r.fence("vector")
            v.tensor_scalar(lnstat[:, 16:16 + ntile], lnstat[:, 16:16 + ntile],
                            -1.0, None, OP.mult)
            nlv = r.mark("vector", "lnv2")

            if hf == 0:
                g.wait_ge(r.sem("prd"), 16)
                for c in range(8):
                    r.dma("gpsimd", "pcl",
                          lnf[:, c * (HALF + 16) + 15:c * (HALF + 16) + 16],
                          prow_dram[0:1, c * 128:(c + 1) * 128].rearrange(
                              "x p -> p x"))
                n_pcl = 16 * r.val("pcl")
            else:
                v.tensor_copy(lnf[:, 0:8].rearrange("p (c x) -> p c x", c=8)
                              if False else lnf[:, 0:1], stash8[:, 0:1])
                for c in range(8):
                    v.tensor_copy(lnf[:, c * (HALF + 16) + 15:c * (HALF + 16) + 16],
                                  stash8[:, c:c + 1])
                r.mark("vector", "pcv")
            s.wait_ge(r.sem("lnv2"), nlv)
            g.wait_ge(r.sem("lnv"), a0_marks[-1])
            a1_marks = []
            for i in range(ntile):
                if i >= 2:
                    g.wait_ge(r.sem("lap"), a1_marks[i - 2])
                r.dma("gpsimd", "lxd", x2t[:, (i % 2) * M:(i % 2 + 1) * M],
                      x2_d[tb + i * 128:tb + (i + 1) * 128, :])
                s.wait_ge(r.sem("lxd"), 16 * r.val("lxd"))
                if i >= 2:
                    s.wait_ge(r.sem("trp"), 16 * (r.val("trp") - 8))
                s.activation(ln3[:, (i % 2) * M:(i % 2 + 1) * M],
                             x2t[:, (i % 2) * M:(i % 2 + 1) * M], AF.Identity,
                             bias=lnstat[:, 16 + i:17 + i],
                             scale=lnstat[:, 24 + i:25 + i])
                a1_marks.append(r.mark("scalar", "lap"))
                for c in range(8):
                    r.scalar.dma_start_transpose(
                        lnf[:, c * (HALF + 16) + 16 + i * 128:
                            c * (HALF + 16) + 16 + (i + 1) * 128],
                        ln3[:, (i % 2) * M + c * 128:(i % 2) * M + (c + 1) * 128]
                    ).then_inc(r.sem("trp"), 16)
                    r.bump("trp")
            n_trp = 16 * r.val("trp")
            if hf + 1 < NHALF:
                v.wait_ge(r.sem("trp"), n_trp)
                for c in range(8):
                    v.tensor_copy(stash8[:, c:c + 1],
                                  lnf[:, c * (HALF + 16) + 15 + HALF:
                                      c * (HALF + 16) + 16 + HALF])
                r.mark("vector", "stv")

            def emit_mix(mu):
                v.wait_ge(r.sem("pjt"), r.val("pjt"))
                v.wait_ge(r.sem("z1t"), r.val("z1t"))
                v.wait_ge(r.sem("g1t"), r.val("g1t"))
                v.wait_ge(r.sem("trp"), n_trp)
                if hf == 0:
                    v.wait_ge(r.sem("pcl"), n_pcl)
                for c in range(8):
                    cb = c * (HALF + 16) + 16
                    for sg_ in range(SUP):
                        dsl = mixb[:, c * HALF + sg_ * SW:c * HALF + (sg_ + 1) * SW]
                        v.tensor_scalar(tscr[:, 0:SW],
                                        lnf[:, cb - 1 + sg_ * SW:
                                            cb - 1 + (sg_ + 1) * SW],
                                        cvec[mu][:, c:c + 1], None, OP.mult)
                        v.tensor_scalar(dsl,
                                        lnf[:, cb + sg_ * SW:cb + (sg_ + 1) * SW],
                                        cneg[mu][:, c:c + 1], None, OP.mult)
                        r.fence("vector")
                        v.tensor_tensor(dsl, dsl, tscr[:, 0:SW], OP.add)
                return r.mark("vector", "mxv")

            def emit_bigproj(wd, dst):
                nmx = r.val("mxv")
                for ochalf in range(2):
                    g.wait_ge(r.sem("pjt"), r.val("pjt"))
                    tgw = r.dma("gpsimd", "ldw2",
                                wbig[:].rearrange("p (c n) -> p c n", c=8),
                                wd[:, ochalf * 512:(ochalf + 1) * 512].rearrange(
                                    "(c p) n -> p c n", p=128))
                    t.wait_ge(r.sem("ldw2"), tgw)
                    t.wait_ge(r.sem("mxv"), nmx)
                    for oc4 in range(4):
                        oc = ochalf * 4 + oc4
                        for sg_ in range(SUP):
                            k = r.val("pjt")
                            if k >= 2:
                                t.wait_ge(r.sem("pjv"), k - 1)
                            for mc in range(8):
                                t.matmul(ps8[:, (k % 2) * 512:(k % 2) * 512 + SW],
                                         wbig[:, mc * 512 + oc4 * 128:
                                              mc * 512 + (oc4 + 1) * 128],
                                         mixb[:, mc * HALF + sg_ * SW:
                                              mc * HALF + (sg_ + 1) * SW],
                                         start=(mc == 0), stop=(mc == 7))
                            npj = r.mark("tensor", "pjt")
                            v.wait_ge(r.sem("pjt"), npj)
                            v.tensor_copy(dst[:, oc * HALF + sg_ * SW:
                                              oc * HALF + (sg_ + 1) * SW],
                                          ps8[:, (k % 2) * 512:(k % 2) * 512 + SW])
                            r.mark("vector", "pjv")

            def emit_smallpath(m1s, m2s, biasname, dst, kdim):
                nmx = r.val("mxv")
                t.wait_ge(r.sem("mxv"), nmx)
                for sg_ in range(SUP):
                    t.wait_ge(r.sem("z2s"), r.val("z2s"))
                    for mc in range(8):
                        t.matmul(ps8[0:kdim, 3072:3072 + SW],
                                 m1s[:, mc * kdim:(mc + 1) * kdim],
                                 mixb[:, mc * HALF + sg_ * SW:
                                      mc * HALF + (sg_ + 1) * SW],
                                 start=(mc == 0), stop=(mc == 7))
                    nz1 = r.mark("tensor", "z1t")
                    s.wait_ge(r.sem("z1t"), nz1)
                    s.activation(tscr2[0:kdim, 0:SW], ps8[0:kdim, 3072:3072 + SW],
                                 AF.Tanh if biasname == "w0" else AF.Identity)
                    nz2 = r.mark("scalar", "z2s")
                    t.wait_ge(r.sem("z2s"), nz2)
                    for oc in range(8):
                        if oc > 0 or sg_ > 0:
                            t.wait_ge(r.sem("z4s"), r.val("z4s"))
                        t.matmul(ps8[:, 1024:1024 + SW],
                                 m2s[0:kdim, oc * 128:(oc + 1) * 128],
                                 tscr2[0:kdim, 0:SW], start=True, stop=True)
                        nz3 = r.mark("tensor", "z3t")
                        s.wait_ge(r.sem("z3t"), nz3)
                        s.activation(dst[:, oc * HALF + sg_ * SW:
                                         oc * HALF + (sg_ + 1) * SW],
                                     ps8[:, 1024:1024 + SW], AF.Sigmoid,
                                     bias=cvec[biasname][:, oc:oc + 1])
                        r.mark("scalar", "z4s")

            emit_mix("mv"); emit_bigproj(wbig_d["Wv"], vf)
            emit_mix("mk"); emit_bigproj(wbig_d["Wk"], kf)
            emit_mix("mr"); emit_bigproj(wbig_d["Wr"], rf)
            emit_mix("mw")
            emit_smallpath(w1s, w2s, "w0", lwf, 64)
            nz = r.val("z4s")
            v.wait_ge(r.sem("z4s"), nz)
            for c in range(8):
                v.tensor_scalar(lwf[:, c * HALF:(c + 1) * HALF],
                                lwf[:, c * HALF:(c + 1) * HALF],
                                -0.6065306597126334, None, OP.mult)
            r.mark("vector", "lwv")
            emit_mix("ma")
            emit_smallpath(a1s, a2s, "a0", agf, 64)
            emit_mix("mg")
            nmx = r.val("mxv")
            t.wait_ge(r.sem("mxv"), nmx)
            for sg_ in range(SUP):
                t.wait_ge(r.sem("g2s2"), r.val("g2s2"))
                for mc in range(8):
                    t.matmul(ps8[:, 3072:3072 + SW], g1s[:, mc * 128:(mc + 1) * 128],
                             mixb[:, mc * HALF + sg_ * SW:mc * HALF + (sg_ + 1) * SW],
                             start=(mc == 0), stop=(mc == 7))
                ng1 = r.mark("tensor", "g1t")
                s.wait_ge(r.sem("g1t"), ng1)
                s.activation(tscr2[:, 0:SW], ps8[:, 3072:3072 + SW], AF.Sigmoid)
                ng2 = r.mark("scalar", "g2s2")
                t.wait_ge(r.sem("g2s2"), ng2)
                for oc in range(8):
                    if oc > 0 or sg_ > 0:
                        t.wait_ge(r.sem("g4v"), r.val("g4v"))
                    t.matmul(ps8[:, 1024:1024 + SW], g2s[:, oc * 128:(oc + 1) * 128],
                             tscr2[:, 0:SW], start=True, stop=True)
                    ng3 = r.mark("tensor", "g3t")
                    v.wait_ge(r.sem("g3t"), ng3)
                    v.wait_ge(r.sem("gst"), 16 * r.val("gst"))
                    v.tensor_copy(tscr[:, 0:SW], ps8[:, 1024:1024 + SW])
                    ng4 = r.mark("vector", "g4v")
                    g.wait_ge(r.sem("g4v"), ng4)
                    r.dma("gpsimd", "gst",
                          gfm_o[oc * 128:(oc + 1) * 128,
                                tb + sg_ * SW:tb + (sg_ + 1) * SW], tscr[:, 0:SW])
            npj_all = r.val("pjv")
            v.wait_ge(r.sem("pjv"), npj_all)
            s.wait_ge(r.sem("gst"), 16 * r.val("gst"))
            for oc in range(8):
                for sg_ in range(SUP):
                    sl = slice(oc * HALF + sg_ * SW, oc * HALF + (sg_ + 1) * SW)
                    v.wait_ge(r.sem("kkt"), r.val("kkt"))
                    v.tensor_scalar(kkn[:, sl], kf[:, sl],
                                    cvec["k_k"][:, oc:oc + 1], None, OP.mult)
                    nkk = r.mark("vector", "kkv")
                    s.wait_ge(r.sem("kkv"), nkk)
                    s.activation(tscr[:, 0:SW], kkn[:, sl], AF.Square)
                    nsq = r.mark("scalar", "kks")
                    t.wait_ge(r.sem("kks"), nsq)
                    t.matmul(ps8[0:2, 1024:1024 + SW], bones[:], tscr[:, 0:SW],
                             start=True, stop=True)
                    nbm = r.mark("tensor", "kkt")
                    s.wait_ge(r.sem("kkt"), nbm)
                    s.activation(ps8[0:2, 2048:2048 + SW], ps8[0:2, 1024:1024 + SW],
                                 AF.Sqrt)
                    nsr = r.mark("scalar", "kks")
                    v.wait_ge(r.sem("kks"), nsr)
                    v.tensor_scalar(ps8[0:2, 2048:2048 + SW],
                                    ps8[0:2, 2048:2048 + SW], 1e-12, None, OP.max)
                    r.fence("vector")
                    v.reciprocal(nrmsc[0:2, 0:SW], ps8[0:2, 2048:2048 + SW])
                    nrc = r.mark("vector", "kkv")
                    t.wait_ge(r.sem("kkv"), nrc)
                    t.matmul(ps8[:, 1024:1024 + SW], sel2[:], nrmsc[0:2, 0:SW],
                             start=True, stop=True)
                    nbc = r.mark("tensor", "kkt")
                    v.wait_ge(r.sem("kkt"), nbc)
                    v.tensor_tensor(kkn[:, sl], kkn[:, sl], ps8[:, 1024:1024 + SW],
                                    OP.mult)
                    r.mark("vector", "kkv")
            r.fence("vector")
            for oc in range(8):
                for sg_ in range(SUP):
                    sl = slice(oc * HALF + sg_ * SW, oc * HALF + (sg_ + 1) * SW)
                    v.tensor_scalar(tscr[:, 0:SW], agf[:, sl], -1.0,
                                    cvec["k_a"][:, oc:oc + 1], OP.add, OP.mult)
                    r.fence("vector")
                    v.tensor_scalar(tscr[:, 0:SW], tscr[:, 0:SW], 1.0, None, OP.add)
                    r.fence("vector")
                    v.tensor_tensor(kf[:, sl], kf[:, sl], tscr[:, 0:SW], OP.mult)
            r.mark("vector", "kev")
            r.fence("vector")
            v.wait_ge(r.sem("kkt"), r.val("kkt"))
            for oc in range(8):
                for sg_ in range(SUP):
                    sl = slice(oc * HALF + sg_ * SW, oc * HALF + (sg_ + 1) * SW)
                    v.wait_ge(r.sem("rtt"), r.val("rtt"))
                    v.wait_ge(r.sem("rst"), 16 * r.val("rst"))
                    v.tensor_tensor(tscr[:, 0:SW], rf[:, sl], kf[:, sl], OP.mult)
                    r.fence("vector")
                    v.tensor_scalar(tscr[:, 0:SW], tscr[:, 0:SW],
                                    cvec["rk"][:, oc:oc + 1], None, OP.mult)
                    nrt = r.mark("vector", "rtv")
                    t.wait_ge(r.sem("rtv"), nrt)
                    t.matmul(ps8[0:2, 1024:1024 + SW], bones[:], tscr[:, 0:SW],
                             start=True, stop=True)
                    nrm_ = r.mark("tensor", "rtt")
                    v.wait_ge(r.sem("rtt"), nrm_)
                    v.tensor_copy(nrmsc[0:2, 0:SW], ps8[0:2, 1024:1024 + SW])
                    nrv = r.mark("vector", "rtv")
                    t.wait_ge(r.sem("rtv"), nrv)
                    t.matmul(ps8[:, 1024:1024 + SW], sel2[:], nrmsc[0:2, 0:SW],
                             start=True, stop=True)
                    nbc = r.mark("tensor", "rtt")
                    v.wait_ge(r.sem("rtt"), nbc)
                    v.wait_ge(r.sem("rst"), 16 * r.val("rst"))
                    v.tensor_tensor(tscr[:, 0:SW], vf[:, sl], ps8[:, 1024:1024 + SW],
                                    OP.mult)
                    nrv = r.mark("vector", "rtv")
                    g.wait_ge(r.sem("rtv"), nrv)
                    r.dma("gpsimd", "rst",
                          rtfm_o[oc * 128:(oc + 1) * 128,
                                 tb + sg_ * SW:tb + (sg_ + 1) * SW], tscr[:, 0:SW])

            if debug and hf == 0:
                v.wait_ge(r.sem("rtv"), r.val("rtv"))
                g.wait_ge(r.sem("rtv"), r.val("rtv"))
                g.wait_ge(r.sem("lwv"), r.val("lwv"))
                for nm, tens in (("rf", rf), ("kf", kf), ("vf", vf), ("agf", agf),
                                 ("lwf", lwf), ("kkn", kkn)):
                    r.dma("gpsimd", "dbg", dbg_o[nm][:], tens[:])
                r.dma("gpsimd", "dbg", dbg_o["lnf"][:, 0:8 * HALF],
                      lnf[:].rearrange("p (c t) -> p c t", c=8)[:, :, 16:HALF + 16])
                g.wait_ge(r.sem("dbg"), 16 * r.val("dbg"))
                v.wait_ge(r.sem("dbg"), 16 * r.val("dbg"))
                s.wait_ge(r.sem("dbg"), 16 * r.val("dbg"))

            v.wait_ge(r.sem("rtv"), r.val("rtv"))
            v.wait_ge(r.sem("lwv"), r.val("lwv"))
            v.wait_ge(r.sem("rst"), 16 * r.val("rst"))
            v.memset(zer128[:], 0.0)
            r.fence("vector")
            for oc in range(8):
                for c in range(CH_H):
                    v.tensor_tensor_scan(
                        ps8[:, c * 128:(c + 1) * 128],
                        lwf[:, oc * HALF + c * 128:oc * HALF + (c + 1) * 128],
                        zer128[:], 0.0, OP.add, OP.add)
                nsc = r.mark("vector", "dsv")
                for c in range(CH_H):
                    v.tensor_copy(wlh[:, oc * CH_H + c:oc * CH_H + c + 1],
                                  ps8[:, c * 128 + 127:c * 128 + 128])
                nwl = r.mark("vector", "dsv")
                s.wait_ge(r.sem("dsv"), nwl)
                s.activation(wlh[:, oc * CH_H:(oc + 1) * CH_H],
                             wlh[:, oc * CH_H:(oc + 1) * CH_H], AF.Exp)
                s.activation(ps8[:, 1024:1024 + HALF], ps8[:, 0:HALF], AF.Exp)
                s.activation(ps8[:, 2048:2048 + HALF], ps8[:, 0:HALF], AF.Exp,
                             scale=-1.0)
                nex = r.mark("scalar", "dss")
                for sg_ in range(SUP):
                    s.activation((tscr2 if sg_ == 0 else tscr)[:, 0:SW],
                                 lwf[:, oc * HALF + sg_ * SW:
                                     oc * HALF + (sg_ + 1) * SW],
                                 AF.Exp, scale=-1.0)
                nem = r.mark("scalar", "dss")
                v.wait_ge(r.sem("dss"), nem)
                sl = slice(oc * HALF, (oc + 1) * HALF)
                v.tensor_tensor(tf32[:, 0:128] if False else mixb[:, 0:HALF],
                                kkn[:, sl], agf[:, sl], OP.mult)
                r.fence("vector")
                v.tensor_tensor(kkn[:, sl], kkn[:, sl], ps8[:, 1024:1024 + HALF],
                                OP.mult)
                r.fence("vector")
                for sg_ in range(SUP):
                    v.tensor_tensor(kkn[:, oc * HALF + sg_ * SW:
                                        oc * HALF + (sg_ + 1) * SW],
                                    kkn[:, oc * HALF + sg_ * SW:
                                        oc * HALF + (sg_ + 1) * SW],
                                    (tscr2 if sg_ == 0 else tscr)[:, 0:SW], OP.mult)
                r.fence("vector")
                v.tensor_scalar(kkn[:, sl], kkn[:, sl], -1.0, None, OP.mult)
                v.tensor_tensor(agf[:, sl], mixb[:, 0:HALF],
                                ps8[:, 2048:2048 + HALF], OP.mult)
                v.tensor_tensor(rf[:, sl], rf[:, sl], ps8[:, 1024:1024 + HALF],
                                OP.mult)
                v.tensor_tensor(kf[:, sl], kf[:, sl], ps8[:, 2048:2048 + HALF],
                                OP.mult)
                r.mark("vector", "dsv")
            ndv = r.val("dsv")

            if debug and hf == 0:
                v.wait_ge(r.sem("dsv"), ndv)
                g.wait_ge(r.sem("dsv"), ndv)
                for nm, tens in (("dRf", Rf), ("dKf", Kf), ("dBf", Bf), ("dAf", Af)):
                    r.dma("gpsimd", "dbg", dbg_o[nm][:], tens[:])
                g.wait_ge(r.sem("dbg"), 16 * r.val("dbg"))
                t.wait_ge(r.sem("dbg"), 16 * r.val("dbg"))

            for c in range(CH_H):
                for gi in range(RH // 2):
                    infos = []
                    for h in (2 * gi, 2 * gi + 1):
                        pi = b1_pi
                        b1_pi += 1
                        p = pi % 2
                        oc = h // 2
                        b0 = (h % 2) * 64
                        csl = slice(oc * HALF + c * 128, oc * HALF + (c + 1) * 128)
                        infos.append({
                            "pi": pi, "p": p, "h": h, "oc": oc, "b0": b0,
                            "SB": (SBb0 if p == 0 else SBb1),
                            "Tp": (Tpb0 if p == 0 else Tpb1),
                            "b": p * 1536,
                            "Asl": Af[b0:b0 + 64, csl], "Bsl": Bf[b0:b0 + 64, csl],
                            "Ksl": Kf[b0:b0 + 64, csl], "Rsl": Rf[b0:b0 + 64, csl],
                            "Vsl": vf[b0:b0 + 64, csl], "tp": (b0, 0),
                            "t_s": {}, "v_sx": {}, "v_mc": {}, "s_mc": {}})
                    for nf in infos:
                        b = nf["b"]
                        t.wait_ge(r.sem("dsv"), ndv)
                        if nf["pi"] >= 2:
                            t.wait_ge(r.sem("b1v"), b1_state[nf["pi"] - 2])
                            t.wait_ge(r.sem("b1s"), b1_state_s[nf["pi"] - 2])
                        tp = nf["tp"]
                        t.matmul(ps8[:, b:b + 128], nf["Asl"], nf["Bsl"],
                                 start=True, stop=True, tile_position=tp)
                        t.matmul(ps8[:, b + 128:b + 256], nf["Bsl"], nf["Asl"],
                                 start=True, stop=True, tile_position=tp)
                        t.matmul(ps8[:, b + 256:b + 384], nf["Ksl"], nf["Asl"],
                                 start=True, stop=True, tile_position=tp)
                        t.matmul(ps8[:, b + 384:b + 512], nf["Bsl"], nf["Rsl"],
                                 start=True, stop=True, tile_position=tp)
                        t.matmul(ps8[:, b + 512:b + 640], nf["Ksl"], nf["Rsl"],
                                 start=True, stop=True, tile_position=tp)
                        nf["t_build"] = r.mark("tensor", "b1t")
                    for nf in infos:
                        b, SB = nf["b"], nf["SB"]
                        v.wait_ge(r.sem("b1t"), nf["t_build"])
                        for q in range(5):
                            dsts = (0, 128, 256, 512, 640)[q]
                            srcs = (256, 384, 512, 0, 128)[q]
                            msk = (0, 128, 256, 384, 512)[q]
                            v.tensor_tensor(SB[:, dsts:dsts + 128],
                                            ps8[:, b + srcs:b + srcs + 128],
                                            bmask_s[:, msk:msk + 128], OP.mult)
                        nf["v_evac"] = r.mark("vector", "b1v")
                    for nf in infos:
                        idb = ident_s[nf["b0"]:nf["b0"] + 64, nf["b0"]:nf["b0"] + 64]
                        t.wait_ge(r.sem("b1v"), nf["v_evac"])
                        pT = psT[:, nf["p"] * 256:(nf["p"] + 1) * 256]
                        t.transpose(pT[:, 0:64], nf["Asl"], idb)
                        t.transpose(pT[:, 64:128], nf["Vsl"], idb)
                        t.transpose(pT[:, 128:192], nf["Bsl"], idb)
                        t.transpose(pT[:, 192:256], nf["Ksl"], idb)
                        nf["t_tr"] = r.mark("tensor", "b1t")
                    for nf in infos:
                        Tp = nf["Tp"]
                        v.wait_ge(r.sem("b1t"), nf["t_tr"])
                        v.tensor_copy(Tp[:], psT[:, nf["p"] * 256:(nf["p"] + 1) * 256])
                        nf["v_tp"] = r.mark("vector", "b1v")
                    for nf in infos:
                        b, SB, Tp = nf["b"], nf["SB"], nf["Tp"]
                        t.wait_ge(r.sem("b1v"), nf["v_tp"])
                        t.matmul(ps8[:, b + 640:b + 704], SB[:, 0:128],
                                 Tp[:, 64:128], start=True, stop=True)
                        nf["t_nv"] = r.mark("tensor", "b1t")
                    for nf in infos:
                        b, SB, Tp = nf["b"], nf["SB"], nf["Tp"]
                        s.wait_ge(r.sem("b1t"), nf["t_nv"])
                        s.activation(SB[:, 384:448], Tp[:, 0:64], AF.Identity)
                        s.activation(SB[:, 448:512], ps8[:, b + 640:b + 704],
                                     AF.Identity)
                        nf["s_x"] = r.mark("scalar", "b1s")
                    for k in range(7):
                        for nf in infos:
                            b, SB = nf["b"], nf["SB"]
                            if k == 0:
                                t.wait_ge(r.sem("b1s"), nf["s_x"])
                            else:
                                t.wait_ge(r.sem("b1v"), nf["v_mc"][k - 1])
                                t.wait_ge(r.sem("b1s"), nf["s_mc"][k - 1])
                            w = 256 if k < 6 else 128
                            t.matmul(ps8[:, b + 512:b + 512 + w],
                                     SB[:, 640:768], SB[:, 384:384 + w],
                                     start=True, stop=True)
                            if k < 6:
                                t.matmul(ps8[:, b + 768:b + 896],
                                         SB[:, 512:640], SB[:, 640:768],
                                         start=True, stop=True)
                            nf["t_s"][k] = r.mark("tensor", "b1t")
                        for nf in infos:
                            b, SB = nf["b"], nf["SB"]
                            v.wait_ge(r.sem("b1t"), nf["t_s"][k])
                            v.tensor_tensor(SB[:, 384:512], SB[:, 384:512],
                                            ps8[:, b + 512:b + 640], OP.add)
                            nf["v_mc"][k] = r.mark("vector", "b1v")
                            if k < 6:
                                s.wait_ge(r.sem("b1v"), nf["v_mc"][k])
                                s.activation(SB[:, 512:640],
                                             ps8[:, b + 640:b + 768],
                                             AF.Identity)
                                s.activation(SB[:, 640:768],
                                             ps8[:, b + 768:b + 896],
                                             AF.Identity)
                                nf["s_mc"][k] = r.mark("scalar", "b1s")
                    for nf in infos:
                        b, SB, Tp, tp = nf["b"], nf["SB"], nf["Tp"], nf["tp"]
                        t.wait_ge(r.sem("b1v"), nf["v_mc"][6])
                        if nf["pi"] >= 2:
                            t.wait_ge(r.sem("b1s"), b1_state_s[nf["pi"] - 2])
                        t.matmul(ps8[0:64, b + 896:b + 1024], SB[:, 448:512],
                                 SB[:, 128:256], start=True, stop=False)
                        t.matmul(ps8[0:64, b + 896:b + 1024], Tp[:, 64:128],
                                 SB[:, 256:384], start=False, stop=True)
                        t.matmul(ps8[0:64, b + 1024:b + 1152], SB[:, 384:448],
                                 SB[:, 128:256], start=True, stop=False)
                        t.matmul(ps8[0:64, b + 1024:b + 1152],
                                 ident_s[nf["b0"]:nf["b0"] + 64,
                                         nf["b0"]:nf["b0"] + 64], nf["Rsl"],
                                 start=False, stop=True, tile_position=tp)
                        t.matmul(ps8[0:64, b + 1152:b + 1216], SB[:, 384:448],
                                 Tp[:, 128:192], start=True, stop=False)
                        idb_ = ident_s[nf["b0"]:nf["b0"] + 64,
                                       nf["b0"]:nf["b0"] + 64]
                        t.matmul(ps8[0:64, b + 1152:b + 1216], idb_, idb_,
                                 start=False, stop=True)
                        t.matmul(ps8[0:64, b + 1216:b + 1280], Tp[:, 128:192],
                                 SB[:, 448:512], start=True, stop=False)
                        t.matmul(ps8[0:64, b + 1216:b + 1280], Tp[:, 192:256],
                                 Tp[:, 64:128], start=False, stop=True)
                        nf["t_out"] = r.mark("tensor", "b1t")
                    for nf in infos:
                        b, h = nf["b"], nf["h"]
                        s.wait_ge(r.sem("b1t"), nf["t_out"])
                        s.wait_ge(r.sem("blb"), blb_gate)
                        s.activation(outP[:, h * 64:(h + 1) * 64],
                                     ps8[0:64, b + 1152:b + 1216], AF.Identity)
                        s.activation(outZ[:, h * 64:(h + 1) * 64],
                                     ps8[0:64, b + 1216:b + 1280], AF.Identity)
                        s.activation(outY[:, h * 128:(h + 1) * 128],
                                     ps8[0:64, b + 896:b + 1024], AF.Identity)
                        s.activation(outR[:, h * 128:(h + 1) * 128],
                                     ps8[0:64, b + 1024:b + 1152], AF.Identity)
                        nf["s_op"] = r.mark("scalar", "b1s")
                    for nf in infos:
                        b, h, p = nf["b"], nf["h"], nf["p"]
                        oc, b0 = nf["oc"], nf["b0"]
                        t.wait_ge(r.sem("b1s"), nf["s_op"])
                        t.wait_ge(r.sem(last_vsp[0][0]), last_vsp[0][1])
                        if h in span_cnt:
                            t.wait_ge(r.sem(span_cnt[h][0]), span_cnt[h][1])
                        t.matmul(ps8[0:64, 1280:1344],
                                 outP[:, h * 64:(h + 1) * 64],
                                 spanP[:, h * 64:(h + 1) * 64],
                                 start=True, stop=True)
                        t.matmul(ps8[0:64, 1344:1408],
                                 outP[:, h * 64:(h + 1) * 64],
                                 spanQ[:, h * 64:(h + 1) * 64],
                                 start=True, stop=False)
                        t.matmul(ps8[0:64, 1344:1408], eye64[:],
                                 outZ[:, h * 64:(h + 1) * 64],
                                 start=False, stop=True)
                        nf["t_sp"] = r.mark("tensor", "b1t")
                        wlcol = wlh[b0:b0 + 64, oc * CH_H + c:oc * CH_H + c + 1]
                        s.wait_ge(r.sem("b1t"), nf["t_sp"])
                        s.activation(spanP[:, h * 64:(h + 1) * 64],
                                     ps8[0:64, 1280:1344], AF.Identity,
                                     scale=wlcol)
                        s.activation(spanQ[:, h * 64:(h + 1) * 64],
                                     ps8[0:64, 1344:1408], AF.Identity,
                                     scale=wlcol)
                        nf["s_sp"] = r.mark("scalar", "b1s")
                        span_cnt[h] = ("b1s", nf["s_sp"])
                        last_vsp[0] = ("b1s", nf["s_sp"])
                        b1_state[nf["pi"]] = nf["v_mc"][6]
                        b1_state_s[nf["pi"]] = nf["s_sp"]
                    if debug and hf == 0 and c == 0 and gi == 0:
                        g.wait_ge(r.sem("b1v"), r.val("b1v"))
                        r.dma("gpsimd", "dbg", dbg2_o[:, 0:256], SBb0[:, 512:768])
                        r.dma("gpsimd", "dbg", dbg2_o[:, 256:384], SBb0[:, 0:128])
                        r.dma("gpsimd", "dbg", dbg2_o[:, 384:640], SBb0[:, 128:384])
                        r.dma("gpsimd", "dbg", dbg2_o[:, 640:896], Tpb0[:])
                        r.dma("gpsimd", "dbg", dbg2_o[:, 896:1024], SBb0[:, 384:512])
                        r.dma("gpsimd", "dbg", dbg3_o[:, 0:64], outP[:, 0:64])
                        g.wait_ge(r.sem("dbg"), 16 * r.val("dbg"))
                        t.wait_ge(r.sem("dbg"), 16 * r.val("dbg"))
                        v.wait_ge(r.sem("dbg"), 16 * r.val("dbg"))
                        s.wait_ge(r.sem("dbg"), 16 * r.val("dbg"))
                nbv = r.val("b1v")
                g.wait_ge(r.sem("b1v"), nbv)
                g.wait_ge(r.sem("b1s"), r.val("b1s"))
                r.dma("gpsimd", "blb",
                      rhat_o[gc * RH * 64:(gc + 1) * RH * 64, :].rearrange(
                          "(h p) t -> p h t", p=64),
                      outR[:].rearrange("p (h t) -> p h t", h=RH))
                r.dma("gpsimd", "blb",
                      yloc_o[gc * RH * 64:(gc + 1) * RH * 64, :].rearrange(
                          "(h p) t -> p h t", p=64),
                      outY[:].rearrange("p (h t) -> p h t", h=RH))
                r.dma("gpsimd", "blb",
                      phi_o[gc * RH * 64:(gc + 1) * RH * 64, :].rearrange(
                          "(h p) t -> p h t", p=64),
                      outP[:].rearrange("p (h t) -> p h t", h=RH))
                r.dma("gpsimd", "blb",
                      zp_o[gc * RH * 64:(gc + 1) * RH * 64, :].rearrange(
                          "(h p) t -> p h t", p=64),
                      outZ[:].rearrange("p (h t) -> p h t", h=RH))
                blb_gate = 16 * r.val("blb")
                gc += 1
            g.wait_ge(r.sem("dsv"), ndv)
            r.dma("gpsimd", "blb", wl_o[hf * 128:(hf + 1) * 128, :], wlh[:])

        g.wait_ge(r.sem("b1v"), r.val("b1v"))
        g.wait_ge(r.sem("b1s"), r.val("b1s"))
        r.dma("gpsimd", "blb", spp_o[:].rearrange("(h p) n -> p h n", p=64),
              spanP[:].rearrange("p (h n) -> p h n", h=RH))
        r.dma("gpsimd", "blb", spq_o[:].rearrange("(h p) n -> p h n", p=64),
              spanQ[:].rearrange("p (h n) -> p h n", h=RH))
        g.wait_ge(r.sem("blb"), 16 * r.val("blb"))
        g.wait_ge(r.sem("gst"), 16 * r.val("gst"))
        g.wait_ge(r.sem("rst"), 16 * r.val("rst"))
        r.flush()
    return nc




def build_l6(T_loc, debug=False):
    NCH = T_loc // RCHUNK
    HALF = min(T_loc, 1024)
    NHALF = T_loc // HALF
    CH_H = HALF // RCHUNK
    TS = min(512, T_loc)
    NT = T_loc // TS
    nc = bass.Bass("TRN2", target_bir_lowering=False)
    yl_d = nc.dram_tensor("ylblob", [NCH * RH * 64, 128], BF16,
                          kind="ExternalInput").ap()
    rh_d = nc.dram_tensor("rhblob", [NCH * RH * 64, 128], BF16,
                          kind="ExternalInput").ap()
    phi_d = nc.dram_tensor("phiblob", [NCH * RH * 64, 64], F32,
                           kind="ExternalInput").ap()
    zp_d = nc.dram_tensor("zpblob", [NCH * RH * 64, 64], F32,
                          kind="ExternalInput").ap()
    wl_d = nc.dram_tensor("wlblob", [NHALF * 128, 8 * CH_H], F32,
                          kind="ExternalInput").ap()
    gfm_d = nc.dram_tensor("gfm", [M, T_loc], BF16, kind="ExternalInput").ap()
    rtfm_d = nc.dram_tensor("rtfm", [M, T_loc], BF16, kind="ExternalInput").ap()
    sppt_d = nc.dram_tensor("spPT", [7 * RH * 64, 64], F32,
                            kind="ExternalInput").ap()
    spq_d = nc.dram_tensor("spQ", [7 * RH * 64, 64], F32,
                           kind="ExternalInput").ap()
    lnw_d = nc.dram_tensor("lnw_fm", [128, 8], F32, kind="ExternalInput").ap()
    lnb_d = nc.dram_tensor("lnb_fm", [128, 8], F32, kind="ExternalInput").ap()
    eps_d = nc.dram_tensor("epsgn", [128, 1], F32, kind="ExternalInput").ap()
    bones_d = nc.dram_tensor("blockones64", [128, 2], BF16,
                             kind="ExternalInput").ap()
    sel2_d = nc.dram_tensor("sel2", [2, 128], BF16, kind="ExternalInput").ap()
    yout_d = nc.dram_tensor("yout", [M, T_loc], BF16, kind="ExternalOutput").ap()
    if debug:
        dys_o = nc.dram_tensor("dbg_ysb", [128, 8 * T_loc], BF16,
                               kind="ExternalOutput").ap()
        ds0_o = nc.dram_tensor("dbg_s0", [64, RH * 64], F32,
                               kind="ExternalOutput").ap()
        ds1_o = nc.dram_tensor("dbg_s1", [64, RH * 64], F32,
                               kind="ExternalOutput").ap()
        dgn_o = nc.dram_tensor("dbg_gn", [2, 4 * 512], F32,
                               kind="ExternalOutput").ap()
        dtg_o = nc.dram_tensor("dbg_tg", [128, 2 * 512], F32,
                               kind="ExternalOutput").ap()

    with ExitStack() as es:
        r = Rec(nc, es)
        t, s, v, g = r.tensor, r.scalar, r.vector, r.gpsimd
        yl_s = r.sbuf("yl_s", [64, 2 * RH * 128], BF16)
        rh_s = r.sbuf("rh_s", [64, 2 * RH * 128], BF16)
        phi_s = r.sbuf("phi_s", [64, 2 * RH * 64], F32)
        zp_s = r.sbuf("zp_s", [64, 2 * RH * 64], F32)
        wl_s = r.sbuf("wl_s", [128, NHALF * 8 * CH_H], F32)
        sppt_s = r.sbuf("sppt_s", [64, 7 * RH * 64], F32)
        spq_s = r.sbuf("spq_s", [64, 7 * RH * 64], F32)
        Ssb = r.sbuf("Ssb", [64, RH * 64], F32)
        Sbf = r.sbuf("Sbf", [64, RH * 64], BF16)
        ysb = r.sbuf("ysb", [128, 8 * T_loc], BF16)
        lnw_s = r.sbuf("lnw_s", [128, 8], F32)
        lnb_s = r.sbuf("lnb_s", [128, 8], F32)
        eps_s = r.sbuf("eps_s6", [128, 1], F32)
        bones = r.sbuf("bones6", [128, 2], BF16)
        sel2s = r.sbuf("sel2_6", [2, 128], BF16)
        sqs = r.sbuf("sqs", [128, TS], BF16)
        gn1 = r.sbuf("gn1", [2, TS], F32)
        gn2 = r.sbuf("gn2", [2, TS], F32)
        gn3 = r.sbuf("gn3", [2, TS], F32)
        nrm = r.sbuf("nrm6", [2, 2 * TS], BF16)
        tgn = r.sbuf("tgn", [128, TS], F32)
        tg2 = r.sbuf("tg2", [128, TS], F32)
        yo = r.sbuf("yo", [128, TS], BF16)
        gt = r.sbuf("gt", [128, 2 * TS], BF16)
        rtt = r.sbuf("rtt", [128, 2 * TS], BF16)
        psL = r.psum("psL", [128, 512])
        psG = r.psum("psG", [128, 2048])

        for dst, srcd in ((wl_s, None), ):
            pass
        r.dma("sync", "ld6", wl_s[:].rearrange("p (f c) -> p f c", f=NHALF),
              wl_d.rearrange("(f p) c -> p f c", p=128))
        r.dma("sync", "ld6", sppt_s[:].rearrange("p (q n) -> p q n", q=7 * RH),
              sppt_d.rearrange("(q p) n -> p q n", p=64))
        r.dma("sync", "ld6", spq_s[:].rearrange("p (q n) -> p q n", q=7 * RH),
              spq_d.rearrange("(q p) n -> p q n", p=64))
        for dst, srcd in ((lnw_s, lnw_d), (lnb_s, lnb_d), (eps_s, eps_d),
                          (bones, bones_d), (sel2s, sel2_d)):
            r.dma("sync", "ld6", dst[:], srcd[:])
        nld = 16 * r.val("ld6")

        def load_chunk(c):
            b = c % 2
            r.dma("gpsimd", "blL",
                  yl_s[:, b * RH * 128:(b + 1) * RH * 128].rearrange(
                      "p (h t) -> p h t", h=RH),
                  yl_d[c * RH * 64:(c + 1) * RH * 64, :].rearrange(
                      "(h p) t -> p h t", p=64))
            r.dma("gpsimd", "blL",
                  rh_s[:, b * RH * 128:(b + 1) * RH * 128].rearrange(
                      "p (h t) -> p h t", h=RH),
                  rh_d[c * RH * 64:(c + 1) * RH * 64, :].rearrange(
                      "(h p) t -> p h t", p=64))
            r.dma("gpsimd", "blL",
                  phi_s[:, b * RH * 64:(b + 1) * RH * 64].rearrange(
                      "p (h n) -> p h n", h=RH),
                  phi_d[c * RH * 64:(c + 1) * RH * 64, :].rearrange(
                      "(h p) n -> p h n", p=64))
            r.dma("gpsimd", "blL",
                  zp_s[:, b * RH * 64:(b + 1) * RH * 64].rearrange(
                      "p (h n) -> p h n", h=RH),
                  zp_d[c * RH * 64:(c + 1) * RH * 64, :].rearrange(
                      "(h p) n -> p h n", p=64))

        load_chunk(0)
        if NCH > 1:
            load_chunk(1)

        v.wait_ge(r.sem("ld6"), nld)
        v.memset(Ssb[:], 0.0)
        r.mark("vector", "yv")
        for j in range(7):
            for h in range(RH):
                q = j * RH + h
                reg = (q % 2) * 256 + 128
                t.wait_ge(r.sem("ld6"), nld)
                t.wait_ge(r.sem("yv"), r.val("yv"))
                t.matmul(psL[0:64, reg:reg + 64], sppt_s[:, q * 64:(q + 1) * 64],
                         Ssb[:, h * 64:(h + 1) * 64], start=True, stop=True)
                nmm = r.mark("tensor", "yt")
                v.wait_ge(r.sem("yt"), nmm)
                v.tensor_tensor(Ssb[:, h * 64:(h + 1) * 64],
                                psL[0:64, reg:reg + 64],
                                spq_s[:, q * 64:(q + 1) * 64], OP.add)
                r.mark("vector", "yv")

        if debug:
            g.wait_ge(r.sem("yv"), r.val("yv"))
            r.dma("gpsimd", "dbg6", ds0_o[:], Ssb[:])
            g.wait_ge(r.sem("dbg6"), 16 * r.val("dbg6"))
            t.wait_ge(r.sem("dbg6"), 16 * r.val("dbg6"))
            v.wait_ge(r.sem("dbg6"), 16 * r.val("dbg6"))

        chunk_vmark = {}
        for c in range(NCH):
            b = c % 2
            for h in range(RH):
                oc = h // 2
                b0 = (h % 2) * 64
                regY = (h % 2) * 256
                regA = (h % 2) * 256 + 128
                hsl = slice(b * RH * 128 + h * 128, b * RH * 128 + (h + 1) * 128)
                nsl = slice(b * RH * 64 + h * 64, b * RH * 64 + (h + 1) * 64)
                v.wait_ge(r.sem("blL"), 16 * 4 * (c + 1))
                v.wait_ge(r.sem("yt"), r.val("yt"))
                v.tensor_copy(Sbf[:, h * 64:(h + 1) * 64],
                              Ssb[:, h * 64:(h + 1) * 64])
                ncp = r.mark("vector", "yv")
                t.wait_ge(r.sem("yv"), ncp)
                t.wait_ge(r.sem("blL"), 16 * 4 * (c + 1))
                t.matmul(psL[0:64, regY:regY + 128],
                         Sbf[:, h * 64:(h + 1) * 64], rh_s[:, hsl],
                         start=True, stop=True)
                t.matmul(psL[0:64, regA:regA + 64],
                         phi_s[:, nsl], Ssb[:, h * 64:(h + 1) * 64],
                         start=True, stop=True)
                nmm = r.mark("tensor", "yt")
                v.wait_ge(r.sem("yt"), nmm)
                v.tensor_tensor(ysb[b0:b0 + 64,
                                    oc * T_loc + c * 128:oc * T_loc + (c + 1) * 128],
                                psL[0:64, regY:regY + 128], yl_s[:, hsl], OP.add)
                v.tensor_tensor(tgn[0:64, 0:64], psL[0:64, regA:regA + 64],
                                zp_s[:, nsl], OP.add)
                r.fence("vector")
                hf = c // CH_H
                col = hf * (8 * CH_H) + oc * CH_H + (c % CH_H)
                v.tensor_scalar(Ssb[:, h * 64:(h + 1) * 64], tgn[0:64, 0:64],
                                wl_s[b0:b0 + 64, col:col + 1], None, OP.mult)
                r.mark("vector", "yv")
            chunk_vmark[c] = r.val("yv")
            if debug and c == 0:
                g.wait_ge(r.sem("yv"), r.val("yv"))
                r.dma("gpsimd", "dbg6", ds1_o[:], Ssb[:])
                g.wait_ge(r.sem("dbg6"), 16 * r.val("dbg6"))
                t.wait_ge(r.sem("dbg6"), 16 * r.val("dbg6"))
                v.wait_ge(r.sem("dbg6"), 16 * r.val("dbg6"))
            if c + 2 < NCH:
                g.wait_ge(r.sem("yv"), chunk_vmark[c])
                load_chunk(c + 2)

        def load_gtile(oc, i):
            bb = (oc * NT + i) % 2
            r.dma("gpsimd", "gld", gt[:, bb * TS:(bb + 1) * TS],
                  gfm_d[oc * 128:(oc + 1) * 128, i * TS:(i + 1) * TS])
            r.dma("gpsimd", "gld", rtt[:, bb * TS:(bb + 1) * TS],
                  rtfm_d[oc * 128:(oc + 1) * 128, i * TS:(i + 1) * TS])

        g.wait_ge(r.sem("yv"), chunk_vmark[NCH - 1])
        load_gtile(0, 0)
        if NT * 8 > 1:
            noc1, ni1 = divmod(1, NT)
            load_gtile(noc1, ni1)
        it = 0
        for oc in range(8):
            for i in range(NT):
                bb = it % 2
                ysl = ysb[:, oc * T_loc + i * TS:oc * T_loc + (i + 1) * TS]
                s.wait_ge(r.sem("yv"), chunk_vmark[NCH - 1])
                s.wait_ge(r.sem("yt"), r.val("yt"))
                s.activation(sqs[:], ysl, AF.Square)
                nsq = r.mark("scalar", "ys")
                t.wait_ge(r.sem("ys"), nsq)
                t.wait_ge(r.sem("yv"), r.val("yv"))
                t.matmul(psG[0:2, 0:TS], bones[:], ysl, start=True, stop=True)
                t.matmul(psG[0:2, 512:512 + TS], bones[:], sqs[:],
                         start=True, stop=True)
                nmm = r.mark("tensor", "yt")
                v.wait_ge(r.sem("yt"), nmm)
                v.tensor_scalar(gn1[:], psG[0:2, 0:TS], 1.0 / 64, None, OP.mult)
                v.tensor_scalar(gn2[:], psG[0:2, 512:512 + TS], 1.0 / 64,
                                None, OP.mult)
                r.fence("vector")
                v.tensor_tensor(gn3[:], gn1[:], gn1[:], OP.mult)
                r.fence("vector")
                v.tensor_tensor(gn2[:], gn2[:], gn3[:], OP.subtract)
                ngv = r.mark("vector", "yv")
                s.wait_ge(r.sem("yv"), ngv)
                s.activation(gn2[:], gn2[:], AF.Sqrt, bias=eps_s[0:2, :])
                nsr = r.mark("scalar", "ys")
                v.wait_ge(r.sem("ys"), nsr)
                v.reciprocal(gn3[:], gn2[:])
                r.fence("vector")
                v.tensor_copy(nrm[:, 0:TS], gn3[:])
                v.tensor_copy(nrm[:, TS:2 * TS], gn1[:])
                nbv = r.mark("vector", "yv")
                t.wait_ge(r.sem("yv"), nbv)
                t.matmul(psG[:, 1024:1024 + TS], sel2s[:], nrm[0:2, TS:2 * TS],
                         start=True, stop=True)
                t.matmul(psG[:, 1536:1536 + TS], sel2s[:], nrm[0:2, 0:TS],
                         start=True, stop=True)
                nbm = r.mark("tensor", "yt")
                v.wait_ge(r.sem("yt"), nbm)
                v.tensor_tensor(tgn[:], ysl, psG[:, 1024:1024 + TS], OP.subtract)
                r.fence("vector")
                v.tensor_tensor(tgn[:], tgn[:], psG[:, 1536:1536 + TS], OP.mult)
                naf = r.mark("vector", "yv")
                s.wait_ge(r.sem("yv"), naf)
                s.activation(tg2[:], tgn[:], AF.Identity,
                             bias=lnb_s[:, oc:oc + 1], scale=lnw_s[:, oc:oc + 1])
                nid = r.mark("scalar", "ys")
                v.wait_ge(r.sem("ys"), nid)
                v.wait_ge(r.sem("gld"), 16 * 2 * (it + 1))
                v.tensor_tensor(tg2[:], tg2[:], rtt[:, bb * TS:(bb + 1) * TS],
                                OP.add)
                r.fence("vector")
                v.wait_ge(r.sem("yst"), 16 * r.val("yst"))
                v.tensor_tensor(yo[:], tg2[:], gt[:, bb * TS:(bb + 1) * TS],
                                OP.mult)
                nyo = r.mark("vector", "yv")
                if debug and it == 0:
                    g.wait_ge(r.sem("yv"), nyo)
                    r.dma("gpsimd", "dbg6", dgn_o[:, 0:TS], gn1[:])
                    r.dma("gpsimd", "dbg6", dgn_o[:, 512:512 + TS], gn2[:])
                    r.dma("gpsimd", "dbg6", dgn_o[:, 1024:1024 + TS], gn3[:])
                    r.dma("gpsimd", "dbg6", dgn_o[:, 1536:1536 + TS],
                          nrm[:, 0:TS])
                    r.dma("gpsimd", "dbg6", dtg_o[:, 0:TS], tgn[:])
                    r.dma("gpsimd", "dbg6", dtg_o[:, 512:512 + TS], tg2[:])
                    g.wait_ge(r.sem("dbg6"), 16 * r.val("dbg6"))
                    v.wait_ge(r.sem("dbg6"), 16 * r.val("dbg6"))
                    s.wait_ge(r.sem("dbg6"), 16 * r.val("dbg6"))
                    t.wait_ge(r.sem("dbg6"), 16 * r.val("dbg6"))
                g.wait_ge(r.sem("yv"), nyo)
                r.dma("gpsimd", "yst", yout_d[oc * 128:(oc + 1) * 128,
                                              i * TS:(i + 1) * TS], yo[:])
                it += 1
                nxt = it + 1
                if nxt < 8 * NT:
                    noc, ni = divmod(nxt, NT)
                    g.wait_ge(r.sem("yv"), r.val("yv"))
                    load_gtile(noc, ni)
        if debug:
            g.wait_ge(r.sem("yv"), r.val("yv"))
            r.dma("gpsimd", "dbg6", dys_o[:], ysb[:])
        g.wait_ge(r.sem("yst"), 16 * r.val("yst"))
        g.wait_ge(r.sem("dbg6"), 16 * r.val("dbg6"))
        r.flush()
    return nc




def _fmvec(vec):
    return np.ascontiguousarray(np.asarray(vec, np.float32).reshape(8, 128).T)


def _moe_consts():
    tri1 = np.triu(np.ones((128, 128), np.float32), 1).astype(BF)
    trl1 = np.tril(np.ones((128, 128), np.float32), -1).astype(BF)
    tri0 = np.triu(np.ones((128, 128), np.float32), 0).astype(BF)
    t1_ = np.triu(np.ones((128, 128), np.float32), 1)
    t0_ = np.triu(np.ones((128, 128), np.float32), 0)
    l1_ = np.tril(np.ones((128, 128), np.float32), -1)
    bmask = np.concatenate([t1_, t0_, t0_, l1_, t1_], 1).astype(BF)
    bones = np.zeros((128, 2), np.float32)
    bones[0:64, 0] = 1.0
    bones[64:128, 1] = 1.0
    sel2 = np.zeros((2, 128), np.float32)
    sel2[0, 0:64] = 1.0
    sel2[1, 64:128] = 1.0
    return tri1, trl1, tri0, bones.astype(BF), sel2.astype(BF), bmask


def _run_moe(nc3, nc3b, nc4, ofm_per_core, wao, x_in, wg, w1e, w2e, consts,
             T, Tl, cores):
    ims = []
    for c in cores:
        im = {"ofm": ofm_per_core[c], "wao": wao,
              "x": np.ascontiguousarray(x_in[c * Tl:(c + 1) * Tl]), "wg": wg}
        for k in ("epsln", "triu_f32", "ones_row_f32", "sel127", "iota512",
                  "iota8", "triu16_f32", "ones16_f32"):
            im[k] = consts[k]
        ims.append(im)
    res3 = run_bass_kernel_spmd(nc3, ims, core_ids=cores)
    x1 = np.concatenate([np.asarray(res3.results[c]["x1"]) for c in cores], 0)
    meta = np.concatenate([np.asarray(res3.results[c]["meta"]) for c in cores], 0)
    cnts = np.concatenate([np.asarray(res3.results[c]["cnt"]) for c in cores], 0)

    ims_b = []
    for e in cores:
        xsel_e = np.concatenate(
            [np.asarray(res3.results[c]["xsel"])[e * M:(e + 1) * M]
             for c in cores], axis=1)
        ims_b.append({"xsel": np.ascontiguousarray(xsel_e),
                      "w1": np.ascontiguousarray(w1e[e]),
                      "w2": np.ascontiguousarray(w2e[e])})
    res3b = run_bass_kernel_spmd(nc3b, ims_b, core_ids=cores)

    ims_4 = []
    for c in cores:
        eo_c = np.concatenate(
            [np.asarray(res3b.results[e]["eo"])[c * P_LOC:(c + 1) * P_LOC]
             for e in cores], axis=0)
        pm = np.zeros((N_CORES, 1), np.float32)
        pm[:c] = 1.0
        im = {"x1": np.ascontiguousarray(x1[c * Tl:(c + 1) * Tl]),
              "eo": np.ascontiguousarray(eo_c),
              "meta": np.ascontiguousarray(meta[c * Tl:(c + 1) * Tl]),
              "cnts": cnts, "prevmask": pm}
        for k in ("ident_f32", "ones_row_f32", "iota8", "slotid"):
            im[k] = consts[k]
        ims_4.append(im)
    res4 = run_bass_kernel_spmd(nc4, ims_4, core_ids=cores)
    return np.concatenate([np.asarray(res4.results[c]["x2"]) for c in cores], 0)


def kernel(**inputs):
    x = np.asarray(inputs["L0_attn_a_x"], np.float32)
    _, T, _ = x.shape
    Tl = T // N_CORES
    cores = list(range(N_CORES))
    x0 = np.ascontiguousarray(x.reshape(T, M))
    consts = host_consts(T)
    tri1, trl1, tri0, bones_bf, sel2_bf, bmask_bf = _moe_consts()

    w_aq = np.asarray(inputs["L0_attn_a_w_aq"], np.float32)
    w_ak = np.asarray(inputs["L0_attn_a_w_ak"], np.float32)
    w_av = np.asarray(inputs["L0_attn_a_w_av"], np.float32)
    w_ao = np.asarray(inputs["L0_attn_a_w_ao"], np.float32)
    wg = np.asarray(inputs["L0_ffn_wg"], np.float32)
    w1e = np.asarray(inputs["L0_ffn_w1"], np.float32)
    w2e = np.asarray(inputs["L0_ffn_w2"], np.float32)

    nc1 = build_l1(Tl)
    ims = [{"x": np.ascontiguousarray(x0[c * Tl:(c + 1) * Tl]),
            "epsln": consts["epsln"]} for c in cores]
    res1 = run_bass_kernel_spmd(nc1, ims, core_ids=cores)
    ln1 = np.concatenate([np.asarray(res1.results[c]["ln1"]) for c in cores], 0)

    ln1fm = np.ascontiguousarray(ln1.T)
    nc2 = build_l2(T)
    ims = []
    for c in cores:
        im = {"ln1fm": ln1fm,
              "wq": np.ascontiguousarray(w_aq[:, c]),
              "wk": np.ascontiguousarray(w_ak[:, c]),
              "wv": np.ascontiguousarray(w_av[:, c])}
        for k in ("ropeC", "ropeS", "permT", "cmask0", "cmask1", "cmask2",
                  "cmask3", "ones_col", "ones_row_f32", "ident"):
            im[k] = consts[k]
        ims.append(im)
    res2 = run_bass_kernel_spmd(nc2, ims, core_ids=cores)
    ofm = np.concatenate([np.asarray(res2.results[c]["o"]) for c in cores], 0)

    nc3 = build_l3(Tl)
    nc3b = build_l3b()
    nc4 = build_l4(Tl, T)
    wao1 = np.ascontiguousarray(w_ao.reshape(NH * DH, M))
    ofm_pc = [np.ascontiguousarray(ofm[:, c * Tl:(c + 1) * Tl]) for c in cores]
    x2 = _run_moe(nc3, nc3b, nc4, ofm_pc, wao1, x0, wg, w1e, w2e, consts,
                  T, Tl, cores)

    Wr = np.asarray(inputs["L0_attn_b_Wr"], np.float32)
    Wk = np.asarray(inputs["L0_attn_b_Wk"], np.float32)
    Wv = np.asarray(inputs["L0_attn_b_Wv"], np.float32)
    Wo = np.asarray(inputs["L0_attn_b_Wo"], np.float32)
    rw1 = np.asarray(inputs["L0_attn_b_w1"], np.float32)
    rw2 = np.asarray(inputs["L0_attn_b_w2"], np.float32)
    ra1 = np.asarray(inputs["L0_attn_b_a1"], np.float32)
    ra2 = np.asarray(inputs["L0_attn_b_a2"], np.float32)
    rg1 = np.asarray(inputs["L0_attn_b_g1"], np.float32)
    rg2 = np.asarray(inputs["L0_attn_b_g2"], np.float32)
    cvs = {"cv_w0": _fmvec(inputs["L0_attn_b_w0"]),
           "cv_a0": _fmvec(inputs["L0_attn_b_a0"]),
           "cv_k_k": _fmvec(inputs["L0_attn_b_k_k"]),
           "cv_k_a": _fmvec(inputs["L0_attn_b_k_a"]),
           "cv_rk": _fmvec(inputs["L0_attn_b_r_k"]),
           "cv_mr": _fmvec(inputs["L0_attn_b_x_r"]),
           "cv_mw": _fmvec(inputs["L0_attn_b_x_w"]),
           "cv_mk": _fmvec(inputs["L0_attn_b_x_k"]),
           "cv_mv": _fmvec(inputs["L0_attn_b_x_v"]),
           "cv_ma": _fmvec(inputs["L0_attn_b_x_a"]),
           "cv_mg": _fmvec(inputs["L0_attn_b_x_g"])}
    nc5 = build_l5(Tl)
    ims = []
    for c in cores:
        prow = (np.zeros((1, M), np.float32) if c == 0
                else np.ascontiguousarray(x2[c * Tl - 1:c * Tl]))
        im = {"x2": np.ascontiguousarray(x2[c * Tl:(c + 1) * Tl]),
              "prevrow": prow, "Wr": Wr, "Wk": Wk, "Wv": Wv,
              "w1": rw1, "w2": rw2, "a1": ra1, "a2": ra2, "g1": rg1, "g2": rg2,
              "epsln": consts["epsln"], "triu1_bf": tri1, "tril1_bf": trl1,
              "triu0_bf": tri0, "ident": consts["ident"], "bmask_bf": bmask_bf,
              "blockones64": bones_bf, "sel2": sel2_bf,
              "eye64_f32": np.eye(64, dtype=np.float32)}
        im.update(cvs)
        ims.append(im)
    res5 = run_bass_kernel_spmd(nc5, ims, core_ids=cores)

    lnw_fm = _fmvec(inputs["L0_attn_b_ln_w"])
    lnb_fm = _fmvec(inputs["L0_attn_b_ln_b"])
    nc6 = build_l6(Tl)
    ims = []
    for c in cores:
        sppt = np.tile(np.eye(64, dtype=np.float32), (7 * RH, 1))
        spq = np.zeros((7 * RH * 64, 64), np.float32)
        for j in range(c):
            spP = np.asarray(res5.results[j]["span_P"], dtype=np.float32)
            spQ = np.asarray(res5.results[j]["span_Q"], dtype=np.float32)
            sppt[j * RH * 64:(j + 1) * RH * 64] = (
                spP.reshape(RH, 64, 64).transpose(0, 2, 1).reshape(RH * 64, 64))
            spq[j * RH * 64:(j + 1) * RH * 64] = spQ
        rc = res5.results[c]
        im = {"ylblob": np.asarray(rc["ylblob"]),
              "rhblob": np.asarray(rc["rhblob"]),
              "phiblob": np.asarray(rc["phiblob"]),
              "zpblob": np.asarray(rc["zpblob"]),
              "wlblob": np.asarray(rc["wlblob"]),
              "gfm": np.asarray(rc["gfm"]),
              "rtfm": np.asarray(rc["rtfm"]),
              "spPT": np.ascontiguousarray(sppt),
              "spQ": np.ascontiguousarray(spq),
              "lnw_fm": lnw_fm, "lnb_fm": lnb_fm, "epsgn": consts["epsgn"],
              "blockones64": bones_bf, "sel2": sel2_bf}
        ims.append(im)
    res6 = run_bass_kernel_spmd(nc6, ims, core_ids=cores)

    yout_pc = [np.ascontiguousarray(np.asarray(res6.results[c]["yout"]))
               for c in cores]
    x4 = _run_moe(nc3, nc3b, nc4, yout_pc, np.ascontiguousarray(Wo), x2,
                  wg, w1e, w2e, consts, T, Tl, cores)

    return x4.reshape(1, T, M).astype(np.float32)


# version 12
# speedup vs baseline: 1.0353x; 1.0353x over previous
import numpy as np
import ml_dtypes
from contextlib import ExitStack

import concourse.bass as bass
import concourse.mybir as mybir
from concourse.alu_op_type import AluOpType as OP
from concourse.bass_utils import run_bass_kernel_spmd

F32 = mybir.dt.float32
BF16 = mybir.dt.bfloat16
AF = mybir.ActivationFunctionType
AX = mybir.AxisListType
BF = ml_dtypes.bfloat16

N_CORES = 8
M = 1024
NH = 8
DH = 128
RH = 16
RHS = 64
E = 8
FFN = 4096
P_LOC = 512
LN_EPS = 1e-5
GN_EPS = 64e-5
RCHUNK = 128

ENGINES = ("sync", "tensor", "scalar", "vector", "gpsimd")


class _Call:
    __slots__ = ("meth", "args", "kwargs", "chain", "marks")

    def __init__(self, meth, args, kwargs):
        self.meth = meth
        self.args = args
        self.kwargs = kwargs
        self.chain = []
        self.marks = []

    def then_inc(self, sem, n):
        self.chain.append(("then_inc", (sem, n)))
        return self


class _EngRec:
    def __init__(self, rec, name):
        object.__setattr__(self, "_rec", rec)
        object.__setattr__(self, "_name", name)

    def __getattr__(self, meth):
        rec, name = self._rec, self._name

        def call(*a, **k):
            node = _Call(meth, a, k)
            rec.ops[name].append(node)
            return node
        return call


class Rec:

    def __init__(self, nc, es):
        self.nc = nc
        self.es = es
        self.ops = {e: [] for e in ENGINES}
        self.sems = {}
        self.sync = _EngRec(self, "sync")
        self.tensor = _EngRec(self, "tensor")
        self.scalar = _EngRec(self, "scalar")
        self.vector = _EngRec(self, "vector")
        self.gpsimd = _EngRec(self, "gpsimd")

    def sem(self, name):
        if name not in self.sems:
            self.sems[name] = [self.es.enter_context(self.nc.semaphore(name)), 0]
        return self.sems[name][0]

    def bump(self, name, k=1):
        self.sem(name)
        self.sems[name][1] += k
        return self.sems[name][1]

    def val(self, name):
        self.sem(name)
        return self.sems[name][1]

    def dma(self, eng, sem_name, out, in_):
        getattr(self, eng).dma_start(out=out, in_=in_).then_inc(self.sem(sem_name), 16)
        return 16 * self.bump(sem_name)

    def mark(self, eng, sem_name):
        ops = self.ops[eng]
        assert ops and ops[-1].meth not in ("wait_ge", "sem_inc"), \
            f"mark({eng}): no completion-capable instruction to attach to"
        node = ops[-1]
        for nm, cnt in node.marks:
            if nm == sem_name:
                return cnt
        node.then_inc(self.sem(sem_name), 1)
        cnt = self.bump(sem_name)
        node.marks.append((sem_name, cnt))
        return cnt

    def fence(self, eng, sem_name=None):
        node = self.ops[eng][-1]
        if node.marks:
            nm, cnt = node.marks[0]
        else:
            nm = sem_name or (eng + "_f")
            cnt = self.mark(eng, nm)
        getattr(self, eng).wait_ge(self.sem(nm), cnt)
        return cnt

    def sbuf(self, name, shape, dtype):
        return self.es.enter_context(self.nc.sbuf_tensor(name, shape, dtype))

    def psum(self, name, shape, dtype=F32):
        return self.es.enter_context(self.nc.psum_tensor(name, shape, dtype))

    def flush(self):
        with self.nc.allow_low_precision("bf16 psum transpose staging by design"), \
                self.nc.Block() as block:
            for e in ENGINES:
                ops = self.ops[e]
                if not ops:
                    continue

                def emit(eng, ops=ops):
                    for node in ops:
                        r = getattr(eng, node.meth)(*node.args, **node.kwargs)
                        for cname, cargs in node.chain:
                            r = getattr(r, cname)(*cargs)
                getattr(block, e)(emit)



def host_consts(T):
    c = {}
    pos = np.arange(T, dtype=np.float32)
    half = DH // 2
    freqs = np.power(10000.0, -np.arange(half, dtype=np.float32) / half)
    rad = pos[None, :] * freqs[:, None]
    cos = np.cos(rad).astype(np.float32)
    sin = np.sin(rad).astype(np.float32)
    c["ropeC"] = np.ascontiguousarray(np.concatenate([cos, cos], 0)).astype(BF)
    c["ropeS"] = np.ascontiguousarray(np.concatenate([-sin, sin], 0)).astype(BF)
    perm = np.zeros((DH, DH), np.float32)
    for j in range(DH):
        perm[(j + half) % DH, j] = 1.0
    c["permT"] = perm.astype(BF)
    for j in range(4):
        rr = np.arange(128)[:, None] + 128 * j
        cq = np.arange(512)[None, :]
        c[f"cmask{j}"] = np.ascontiguousarray((rr <= cq).astype(np.float32)).astype(BF)
    c["ones_col"] = np.ones((128, 1), np.float32).astype(BF)
    c["ones_row_f32"] = np.ones((1, 128), np.float32)
    c["ident"] = np.eye(128, dtype=np.float32).astype(BF)
    c["epsln"] = np.full((128, 1), LN_EPS, np.float32)
    c["epsgn"] = np.full((128, 1), GN_EPS, np.float32)
    c["triu_f32"] = np.ascontiguousarray(np.triu(np.ones((128, 128), np.float32)))
    s127 = np.zeros((128, 1), np.float32); s127[127, 0] = 1.0
    c["sel127"] = s127
    c["iota512"] = np.ascontiguousarray(
        np.broadcast_to(np.arange(512, dtype=np.float32), (128, 512)))
    c["iota8"] = np.ascontiguousarray(
        np.broadcast_to(np.arange(8, dtype=np.float32), (128, 8)))
    c["ident_f32"] = np.eye(128, dtype=np.float32)
    c["slotid"] = np.arange(128, dtype=np.float32).reshape(128, 1)
    c["triu16_f32"] = np.ascontiguousarray(
        np.triu(np.ones((16, 16), np.float32), 1))
    c["ones16_f32"] = np.ones((16, 1), np.float32)
    return c



def build_l1(T_loc):
    nc = bass.Bass("TRN2", target_bir_lowering=False)
    x_in = nc.dram_tensor("x", [T_loc, M], F32, kind="ExternalInput").ap()
    eps_d = nc.dram_tensor("epsln", [128, 1], F32, kind="ExternalInput").ap()
    ln_out = nc.dram_tensor("ln1", [T_loc, M], BF16, kind="ExternalOutput").ap()
    nt = T_loc // 128
    with ExitStack() as es:
        r = Rec(nc, es)
        xs = r.sbuf("xs", [128, nt * M], F32)
        lo = r.sbuf("lo", [128, nt * M], BF16)
        eps_s = r.sbuf("eps_s", [128, 1], F32)
        sc = ln_scratch(r, nt)
        r.dma("sync", "ld", eps_s[:], eps_d[:])
        for i in range(nt):
            tgt = r.dma("sync", "ld", xs[:, i * M:(i + 1) * M],
                        x_in[i * 128:(i + 1) * 128, :])
        r.scalar.wait_ge(r.sem("ld"), tgt)
        r.vector.wait_ge(r.sem("ld"), tgt)
        dsem, done = emit_ln(r, lambda i: xs[:, i * M:(i + 1) * M],
                             lambda i: lo[:, i * M:(i + 1) * M], nt, sc, eps_s[:])
        r.gpsimd.wait_ge(r.sem(dsem), done)
        for i in range(nt):
            tgt = r.dma("gpsimd", "st", ln_out[i * 128:(i + 1) * 128, :],
                        lo[:, i * M:(i + 1) * M])
        r.gpsimd.wait_ge(r.sem("st"), tgt)
        r.flush()
    return nc


def ln_scratch(r, nt, tag=""):
    sc = {k: r.sbuf(f"lnsc_{k}{tag}", [128, nt], F32)
          for k in ("sum", "sumsq", "mean", "rstd")}
    sc["junk"] = r.sbuf(f"lnsc_junk{tag}", [128, M], F32)
    sc["sq"] = r.sbuf(f"lnsc_sq{tag}", [128, 2 * M], F32)
    return sc


def emit_ln(r, xt, ot, nt, sc, eps_ap, tag="", out_also=None):
    s, v = r.scalar, r.vector
    sS, sV = "lnS" + tag, "lnV" + tag
    for i in range(nt):
        if i >= 2:
            s.wait_ge(r.sem(sV), i - 1)
        s.activation(sc["sq"][:, (i % 2) * M:(i % 2 + 1) * M], xt(i), AF.Square)
        ns = r.mark("scalar", sS)
        v.wait_ge(r.sem(sS), i + 1)
        v.reduce_sum(sc["sum"][:, i:i + 1], xt(i), axis=AX.X)
        v.reduce_sum(sc["sumsq"][:, i:i + 1], sc["sq"][:, (i % 2) * M:(i % 2 + 1) * M],
                     axis=AX.X)
        r.mark("vector", sV)
    r.fence("vector")
    v.tensor_scalar(sc["mean"][:, 0:nt], sc["sum"][:, 0:nt], 1.0 / M, None, OP.mult)
    v.tensor_scalar(sc["sumsq"][:, 0:nt], sc["sumsq"][:, 0:nt], 1.0 / M, None, OP.mult)
    r.fence("vector")
    v.tensor_tensor(sc["rstd"][:, 0:nt], sc["mean"][:, 0:nt], sc["mean"][:, 0:nt],
                    OP.mult)
    r.fence("vector")
    v.tensor_tensor(sc["sumsq"][:, 0:nt], sc["sumsq"][:, 0:nt], sc["rstd"][:, 0:nt],
                    OP.subtract)
    n = r.mark("vector", sV + "c")
    s.wait_ge(r.sem(sV + "c"), n)
    s.activation(sc["sumsq"][:, 0:nt], sc["sumsq"][:, 0:nt], AF.Sqrt, bias=eps_ap)
    n = r.mark("scalar", sS + "c")
    v.wait_ge(r.sem(sS + "c"), n)
    v.reciprocal(sc["rstd"][:, 0:nt], sc["sumsq"][:, 0:nt])
    r.fence("vector")
    v.tensor_tensor(sc["mean"][:, 0:nt], sc["mean"][:, 0:nt], sc["rstd"][:, 0:nt],
                    OP.mult)
    r.fence("vector")
    v.tensor_scalar(sc["mean"][:, 0:nt], sc["mean"][:, 0:nt], -1.0, None, OP.mult)
    n = r.mark("vector", sV + "c")
    s.wait_ge(r.sem(sV + "c"), n)
    for i in range(nt):
        s.activation(ot(i), xt(i), AF.Identity,
                     bias=sc["mean"][:, i:i + 1], scale=sc["rstd"][:, i:i + 1])
        if out_also is not None:
            s.activation(out_also(i), xt(i), AF.Identity,
                         bias=sc["mean"][:, i:i + 1], scale=sc["rstd"][:, i:i + 1])
    nf = r.mark("scalar", sS + "d")
    return sS + "d", nf



def build_l2(T, debug=False):
    assert T % 512 == 0
    NS = T // 512
    nc = bass.Bass("TRN2", target_bir_lowering=False)
    ln1 = nc.dram_tensor("ln1fm", [M, T], BF16, kind="ExternalInput").ap()
    w_d = {k: nc.dram_tensor(k, [M, DH], F32, kind="ExternalInput").ap()
           for k in ("wq", "wk", "wv")}
    ropeC_d = nc.dram_tensor("ropeC", [DH, T], BF16, kind="ExternalInput").ap()
    ropeS_d = nc.dram_tensor("ropeS", [DH, T], BF16, kind="ExternalInput").ap()
    perm_d = nc.dram_tensor("permT", [DH, DH], BF16, kind="ExternalInput").ap()
    cmask_d = [nc.dram_tensor(f"cmask{j}", [128, 512], BF16, kind="ExternalInput").ap()
               for j in range(4)]
    onescol_d = nc.dram_tensor("ones_col", [128, 1], BF16, kind="ExternalInput").ap()
    onesrow_d = nc.dram_tensor("ones_row_f32", [1, 128], F32, kind="ExternalInput").ap()
    ident_d = nc.dram_tensor("ident", [128, 128], BF16, kind="ExternalInput").ap()
    o_out = nc.dram_tensor("o", [DH, T], BF16, kind="ExternalOutput").ap()
    if debug:
        dbgq = nc.dram_tensor("dbgq", [DH, T], BF16, kind="ExternalOutput").ap()
        dbgk = nc.dram_tensor("dbgk", [DH, T], BF16, kind="ExternalOutput").ap()
        dbgv = nc.dram_tensor("dbgv", [128, T], BF16, kind="ExternalOutput").ap()
        dbgc = nc.dram_tensor("dbgc", [128, 4], F32, kind="ExternalOutput").ap()

    with ExitStack() as es:
        r = Rec(nc, es)
        Q = r.sbuf("Q", [128, T], BF16)
        K = r.sbuf("K", [128, T], BF16)
        Vt = r.sbuf("Vt", [128, T], BF16)
        w_s = r.sbuf("w_s", [128, 3 * M], F32)
        w_bf = r.sbuf("w_bf", [128, 3 * M], BF16)
        perm_s = r.sbuf("perm_s", [DH, DH], BF16)
        cmask_s = r.sbuf("cmask_s", [128, 4 * 512], BF16)
        onescol = r.sbuf("onescol", [128, 1], BF16)
        onesrow = r.sbuf("onesrow", [1, 128], F32)
        ident_s = r.sbuf("ident_s", [128, 128], BF16)
        ln_buf = r.sbuf("ln_buf", [128, 2 * 4096], BF16)
        ropeC_s = r.sbuf("ropeC_s", [128, 2 * 512], BF16)
        ropeS_s = r.sbuf("ropeS_s", [128, 2 * 512], BF16)
        qraw = r.sbuf("qraw", [128, 512], BF16)
        kraw = r.sbuf("kraw", [128, 512], BF16)
        vraw = r.sbuf("vraw", [128, 512], BF16)
        tmp1 = r.sbuf("tmp1", [128, 512], BF16)
        tmp2 = r.sbuf("tmp2", [128, 512], BF16)
        nsq = r.sbuf("nsq", [128, 512], BF16)
        nrow = r.sbuf("nrow", [1, 2 * NS], F32)
        sc1 = r.sbuf("sc1", [1, 4], F32)
        cb = r.sbuf("cb", [128, 1], F32)
        P_T = r.sbuf("P_T", [128, 2 * 1024], BF16)
        lrec = r.sbuf("lrec", [128, 512], F32)
        o_sb = r.sbuf("o_sb", [128, 2 * 512], BF16)

        ps_S = r.psum("ps_S", [128, 2 * 1024])
        ps_qkv = ps_S[:, 0:512]
        ps_sw = ps_S[:, 512:1024]
        ps_tr = r.psum("ps_tr", [128, 128], BF16)
        ps_O = r.psum("ps_O", [128, 512])
        ps_l = r.psum("ps_l", [1, 512])
        ps_n = r.psum("ps_n", [128, 512])

        t, s, v, g, sy = r.tensor, r.scalar, r.vector, r.gpsimd, r.sync

        for i, k in enumerate(("wq", "wk", "wv")):
            r.dma("sync", "ld",
                  w_s[:, i * M:(i + 1) * M].rearrange("p (c d) -> p c d", c=8),
                  w_d[k].rearrange("(c p) d -> p c d", p=128))
        r.dma("sync", "ld", perm_s[:], perm_d[:])
        for j in range(4):
            r.dma("sync", "ld", cmask_s[:, j * 512:(j + 1) * 512], cmask_d[j][:])
        r.dma("sync", "ld", onescol[:], onescol_d[:])
        r.dma("sync", "ld", onesrow[:], onesrow_d[:])
        tgt = r.dma("sync", "ld", ident_s[:], ident_d[:])
        r.vector.wait_ge(r.sem("ld"), tgt)
        v.tensor_copy(w_bf[:], w_s[:])
        n_wc = r.mark("vector", "wc")

        for i in range(NS):
            if i >= 2:
                r.sync.wait_ge(r.sem("qk"), 3 * (i - 1))
                r.sync.wait_ge(r.sem("qv"), 3 * (i - 1))
            r.dma("sync", "ld2",
                  ln_buf[:, (i % 2) * 4096:(i % 2 + 1) * 4096].rearrange(
                      "p (c t) -> p c t", c=8),
                  ln1[:, i * 512:(i + 1) * 512].rearrange("(c p) t -> p c t", p=128))
            r.dma("sync", "rld", ropeC_s[:, (i % 2) * 512:(i % 2 + 1) * 512],
                  ropeC_d[:, i * 512:(i + 1) * 512])
            r.dma("sync", "rld", ropeS_s[:, (i % 2) * 512:(i % 2 + 1) * 512],
                  ropeS_d[:, i * 512:(i + 1) * 512])

        t.wait_ge(r.sem("wc"), n_wc)
        nv = 0
        for i in range(NS):
            lnb = ln_buf[:, (i % 2) * 4096:(i % 2 + 1) * 4096]
            rC = ropeC_s[:, (i % 2) * 512:(i % 2 + 1) * 512]
            rS = ropeS_s[:, (i % 2) * 512:(i % 2 + 1) * 512]
            qsl = Q[:, i * 512:(i + 1) * 512]
            ksl = K[:, i * 512:(i + 1) * 512]
            t.wait_ge(r.sem("ld2"), 16 * (i + 1))
            if i > 0:
                t.wait_ge(r.sem("qv"), 3 * i)
            for c in range(8):
                t.matmul(ps_qkv, w_bf[:, 0 * M + c * DH:0 * M + (c + 1) * DH],
                         lnb[:, c * 512:(c + 1) * 512], start=(c == 0), stop=(c == 7))
            nq = r.mark("tensor", "qk")
            v.wait_ge(r.sem("qk"), nq)
            v.tensor_copy(qraw[:], ps_qkv)
            nv = r.mark("vector", "qv")
            t.wait_ge(r.sem("qv"), nv)
            for c in range(8):
                t.matmul(ps_qkv, w_bf[:, 1 * M + c * DH:1 * M + (c + 1) * DH],
                         lnb[:, c * 512:(c + 1) * 512], start=(c == 0), stop=(c == 7))
            t.matmul(ps_sw, perm_s[:], qraw[:], start=True, stop=True)
            nq = r.mark("tensor", "qk")
            v.wait_ge(r.sem("qk"), nq)
            v.tensor_copy(kraw[:], ps_qkv)
            v.tensor_tensor(tmp1[:], qraw[:], rC, OP.mult)
            v.tensor_tensor(tmp2[:], ps_sw, rS, OP.mult)
            v.tensor_tensor(tmp1[:], tmp1[:], tmp2[:], OP.add)
            v.tensor_scalar(qsl, tmp1[:], 1.0 / DH, None, OP.mult)
            nv = r.mark("vector", "qv")
            t.wait_ge(r.sem("qv"), nv)
            for c in range(8):
                t.matmul(ps_qkv, w_bf[:, 2 * M + c * DH:2 * M + (c + 1) * DH],
                         lnb[:, c * 512:(c + 1) * 512], start=(c == 0), stop=(c == 7))
            t.matmul(ps_sw, perm_s[:], kraw[:], start=True, stop=True)
            nq = r.mark("tensor", "qk")
            v.wait_ge(r.sem("qk"), nq)
            v.tensor_copy(vraw[:], ps_qkv)
            v.tensor_tensor(tmp1[:], kraw[:], rC, OP.mult)
            v.tensor_tensor(tmp2[:], ps_sw, rS, OP.mult)
            v.tensor_tensor(ksl, tmp1[:], tmp2[:], OP.add)
            nv = r.mark("vector", "qv")
            t.wait_ge(r.sem("qv"), nv)
            for j in range(4):
                t.transpose(ps_tr[:], vraw[:, j * 128:(j + 1) * 128], ident_s[:])
                njt = r.mark("tensor", "vtr")
                v.wait_ge(r.sem("vtr"), njt)
                v.tensor_copy(Vt[:, (i * 4 + j) * 128:(i * 4 + j + 1) * 128], ps_tr[:])
                r.mark("vector", "vtrv")
                if j < 3:
                    t.wait_ge(r.sem("vtrv"), r.val("vtrv"))
            s.wait_ge(r.sem("qv"), nv)
            s.wait_ge(r.sem("nr"), 2 * i)
            s.activation(nsq[:], qsl, AF.Square)
            ns_ = r.mark("scalar", "nsq")
            t.wait_ge(r.sem("nsq"), ns_)
            t.wait_ge(r.sem("nrv"), r.val("nrv"))
            t.matmul(ps_l[:], onescol[:], nsq[:], start=True, stop=True)
            nt_ = r.mark("tensor", "nr")
            s.wait_ge(r.sem("nr"), nt_)
            s.activation(nsq[:], ksl, AF.Square)
            ns_ = r.mark("scalar", "nsq")
            v.wait_ge(r.sem("nr"), nt_)
            v.reduce_max(nrow[0:1, i:i + 1], ps_l[0:1, :], axis=AX.X)
            r.mark("vector", "nrv")
            t.wait_ge(r.sem("nsq"), ns_)
            t.wait_ge(r.sem("nrv"), r.val("nrv"))
            t.matmul(ps_l[:], onescol[:], nsq[:], start=True, stop=True)
            nt_ = r.mark("tensor", "nr")
            v.wait_ge(r.sem("nr"), nt_)
            v.reduce_max(nrow[0:1, NS + i:NS + i + 1], ps_l[0:1, :], axis=AX.X)
            r.mark("vector", "nrv")

        r.fence("vector")
        v.reduce_max(sc1[0:1, 0:1], nrow[0:1, 0:NS], axis=AX.X)
        v.reduce_max(sc1[0:1, 1:2], nrow[0:1, NS:2 * NS], axis=AX.X)
        r.fence("vector")
        v.tensor_tensor(sc1[0:1, 2:3], sc1[0:1, 0:1], sc1[0:1, 1:2], OP.mult)
        n_ = r.mark("vector", "cb1")
        s.wait_ge(r.sem("cb1"), n_)
        s.activation(sc1[0:1, 3:4], sc1[0:1, 2:3], AF.Sqrt)
        n_ = r.mark("scalar", "cb2")
        t.wait_ge(r.sem("cb2"), n_)
        t.matmul(ps_O[:, 0:1], onesrow[:], sc1[0:1, 3:4], start=True, stop=True)
        n_ = r.mark("tensor", "cb3")
        v.wait_ge(r.sem("cb3"), n_)
        v.tensor_scalar(cb[:], ps_O[:, 0:1], -1.0, None, OP.mult)
        n_cb = r.mark("vector", "cb4")

        if debug:
            g.wait_ge(r.sem("cb4"), n_cb)
            r.dma("gpsimd", "st", dbgq[:], Q[:])
            r.dma("gpsimd", "st", dbgk[:], K[:])
            r.dma("gpsimd", "st", dbgv[:], Vt[:])
            r.dma("gpsimd", "st", dbgc[0:1, 0:4], sc1[0:1, 0:4])
            r.dma("gpsimd", "st", dbgc[1:2, 0:1], cb[0:1, 0:1])

        s.wait_ge(r.sem("cb4"), n_cb)
        batches = []
        for qs in range(NS):
            nkt = 4 * (qs + 1)
            nb = (nkt + 1) // 2
            for b in range(nb):
                kts = [kt for kt in (2 * b, 2 * b + 1) if kt < nkt]
                batches.append((qs, b, nb, kts))
        NB = len(batches)
        fm_cnt = {}
        oev_cnt = {}

        def emit_mm1(bi):
            qs, b, nb, kts = batches[bi]
            half = bi % 2
            qsl_ = Q[:, qs * 512:(qs + 1) * 512]
            if bi >= 2:
                t.wait_ge(r.sem("fx"), bi - 1)
            for ji, kt in enumerate(kts):
                t.matmul(ps_S[:, half * 1024 + ji * 512:half * 1024 + (ji + 1) * 512],
                         K[:, kt * 128:(kt + 1) * 128], qsl_,
                         start=True, stop=True)
            n1 = r.mark("tensor", "f1")
            s.wait_ge(r.sem("f1"), n1)
            if bi >= 2:
                s.wait_ge(r.sem("f2"), bi - 1)
            s.activation(P_T[:, half * 1024:half * 1024 + len(kts) * 512],
                         ps_S[:, half * 1024:half * 1024 + len(kts) * 512],
                         AF.Exp, bias=cb[:])
            nfx = r.mark("scalar", "fx")
            need_mask = [ji for ji, kt in enumerate(kts) if kt >= 4 * qs]
            if need_mask:
                v.wait_ge(r.sem("fx"), nfx)
                for ji in need_mask:
                    j = kts[ji] - 4 * qs
                    v.tensor_tensor(
                        P_T[:, half * 1024 + ji * 512:half * 1024 + (ji + 1) * 512],
                        P_T[:, half * 1024 + ji * 512:half * 1024 + (ji + 1) * 512],
                        cmask_s[:, j * 512:(j + 1) * 512], OP.mult)
                fm_cnt[bi] = r.mark("vector", "fm")

        def emit_mm2(bi):
            qs, b, nb, kts = batches[bi]
            half = bi % 2
            if bi in fm_cnt:
                t.wait_ge(r.sem("fm"), fm_cnt[bi])
            else:
                t.wait_ge(r.sem("fx"), bi + 1)
            if b == 0 and qs >= 1:
                t.wait_ge(r.sem("oev"), oev_cnt[qs - 1])
            for ji, kt in enumerate(kts):
                first = (b == 0 and ji == 0)
                last = (b == nb - 1 and ji == len(kts) - 1)
                t.matmul(ps_O[:], Vt[:, kt * 128:(kt + 1) * 128],
                         P_T[:, half * 1024 + ji * 512:half * 1024 + (ji + 1) * 512],
                         start=first, stop=last)
                t.matmul(ps_l[:], onescol[:],
                         P_T[:, half * 1024 + ji * 512:half * 1024 + (ji + 1) * 512],
                         start=first, stop=last)
            r.mark("tensor", "f2")
            if b == nb - 1:
                emit_norm(qs)

        def emit_norm(qs):
            v.wait_ge(r.sem("f2"), r.val("f2"))
            if qs >= 2:
                v.wait_ge(r.sem("st"), 16 * (qs - 1))
            v.reciprocal(lrec[0:1, :], ps_l[0:1, :])
            nrc = r.mark("vector", "orc")
            t.wait_ge(r.sem("orc"), nrc)
            t.matmul(ps_n[:], onesrow[:], lrec[0:1, :], start=True, stop=True)
            nbc = r.mark("tensor", "obc")
            v.wait_ge(r.sem("obc"), nbc)
            v.tensor_copy(lrec[:], ps_n[:])
            v.tensor_tensor(o_sb[:, (qs % 2) * 512:(qs % 2 + 1) * 512], ps_O[:],
                            lrec[:], OP.mult)
            oev_cnt[qs] = r.mark("vector", "oev")
            g.wait_ge(r.sem("oev"), oev_cnt[qs])
            r.dma("gpsimd", "st", o_out[:, qs * 512:(qs + 1) * 512],
                  o_sb[:, (qs % 2) * 512:(qs % 2 + 1) * 512])

        for bi in range(NB):
            emit_mm1(bi)
            if bi >= 1:
                emit_mm2(bi - 1)
        emit_mm2(NB - 1)

        g.wait_ge(r.sem("st"), 16 * r.val("st"))
        r.flush()
    return nc



def _np_ln(x, eps=LN_EPS):
    m = x.mean(-1, keepdims=True)
    v = x.var(-1, keepdims=True)
    return (x - m) / np.sqrt(v + eps)



def build_l3(T_loc):
    nt = T_loc // 128
    nc = bass.Bass("TRN2", target_bir_lowering=False)
    o_d = nc.dram_tensor("ofm", [M, T_loc], BF16, kind="ExternalInput").ap()
    wao_d = nc.dram_tensor("wao", [M, M], F32, kind="ExternalInput").ap()
    x_d = nc.dram_tensor("x", [T_loc, M], F32, kind="ExternalInput").ap()
    wg_d = nc.dram_tensor("wg", [M, E], F32, kind="ExternalInput").ap()
    eps_d = nc.dram_tensor("epsln", [128, 1], F32, kind="ExternalInput").ap()
    lt_d = nc.dram_tensor("triu_f32", [128, 128], F32, kind="ExternalInput").ap()
    onesrow_d = nc.dram_tensor("ones_row_f32", [1, 128], F32, kind="ExternalInput").ap()
    sel127_d = nc.dram_tensor("sel127", [128, 1], F32, kind="ExternalInput").ap()
    iota512_d = nc.dram_tensor("iota512", [128, 512], F32, kind="ExternalInput").ap()
    iota8_d = nc.dram_tensor("iota8", [128, 8], F32, kind="ExternalInput").ap()
    triu16_d = nc.dram_tensor("triu16_f32", [16, 16], F32, kind="ExternalInput").ap()
    ones16_d = nc.dram_tensor("ones16_f32", [16, 1], F32, kind="ExternalInput").ap()

    x1_o = nc.dram_tensor("x1", [T_loc, M], F32, kind="ExternalOutput").ap()
    tot_scr = nc.dram_tensor("tot_scr", [16, 8], F32, kind="Internal").ap()
    pref_scr = nc.dram_tensor("pref_scr", [16, 8], F32, kind="Internal").ap()
    ln2_o = nc.dram_tensor("ln2", [T_loc, M], BF16, kind="ExternalOutput").ap()
    xsel_o = nc.dram_tensor("xsel", [E * M, P_LOC], BF16, kind="ExternalOutput").ap()
    meta_o = nc.dram_tensor("meta", [T_loc, 4], F32, kind="ExternalOutput").ap()
    cnt_o = nc.dram_tensor("cnt", [1, E], F32, kind="ExternalOutput").ap()

    with ExitStack() as es:
        r = Rec(nc, es)
        t, s, v, g = r.tensor, r.scalar, r.vector, r.gpsimd
        o_fm = r.sbuf("o_fm", [128, 8 * T_loc], BF16)
        wstg = r.sbuf("wstg", [128, M], F32)
        wao_bf = r.sbuf("wao_bf", [128, 8 * M], BF16)
        wg_s = r.sbuf("wg_s", [128, 8 * E], F32)
        wg_bf = r.sbuf("wg_bf", [128, 8 * E], BF16)
        x1 = r.sbuf("x1_s", [128, nt * M], F32)
        ln2 = r.sbuf("ln2_s", [128, nt * M], BF16)
        ln2f = r.sbuf("ln2f", [128, M], BF16)
        xt_s = r.sbuf("xt_s", [128, 2 * M], F32)
        eps_s = r.sbuf("eps_s", [128, 1], F32)
        lt_s = r.sbuf("lt_s", [128, 128], F32)
        onesrow = r.sbuf("onesrow", [1, 128], F32)
        sel127 = r.sbuf("sel127_s", [128, 1], F32)
        iota512 = r.sbuf("iota512_s", [128, 512], F32)
        iota8 = r.sbuf("iota8_s", [128, 8], F32)
        lg = r.sbuf("lg", [128, 8], F32)
        mx = r.sbuf("mx", [128, 4], F32)
        ex8 = r.sbuf("ex8", [128, 8], F32)
        oh = r.sbuf("oh", [128, nt * 8], F32)
        cum = r.sbuf("cum", [128, 8], F32)
        carry = r.sbuf("carry", [1, 8], F32)
        locx = r.sbuf("locx", [128, nt], F32)
        eid = r.sbuf("eid", [128, nt], F32)
        gate = r.sbuf("gate", [128, nt], F32)
        locsel = r.sbuf("locsel", [128, 1], F32)
        dtile = r.sbuf("dtile", [128, nt * 512], BF16)
        lgall = r.sbuf("lgall", [128, nt * 8], F32)
        exall = r.sbuf("exall", [128, nt * 8], F32)
        cumall = r.sbuf("cumall", [128, nt * 8], F32)
        mxa = r.sbuf("mxa", [128, nt], F32)
        nega = r.sbuf("nega", [128, nt], F32)
        suma = r.sbuf("suma", [128, nt], F32)
        ehall = r.sbuf("ehall", [128, nt], F32)
        totrow = r.sbuf("totrow", [1, 128], F32)
        tot16 = r.sbuf("tot16", [16, 8], F32)
        pref16 = r.sbuf("pref16", [16, 8], F32)
        prefrow = r.sbuf("prefrow", [1, 128], F32)
        triu16 = r.sbuf("triu16_s", [16, 16], F32)
        ones16 = r.sbuf("ones16_s", [16, 1], F32)
        ln2f2 = r.sbuf("ln2f2", [128, M], BF16)
        xsel_s = r.sbuf("xsel_sb", [128, 8 * 512], BF16)
        sc = ln_scratch(r, nt, tag="3")
        ps8 = r.psum("ps8", [128, 4096])

        r.dma("sync", "ld", eps_s[:], eps_d[:])
        r.dma("sync", "ld", lt_s[:], lt_d[:])
        r.dma("sync", "ld", onesrow[:], onesrow_d[:])
        r.dma("sync", "ld", sel127[:], sel127_d[:])
        r.dma("sync", "ld", iota512[:], iota512_d[:])
        r.dma("sync", "ld", iota8[:], iota8_d[:])
        r.dma("sync", "ld", triu16[:], triu16_d[:])
        r.dma("sync", "ld", ones16[:], ones16_d[:])
        r.dma("sync", "ld", wg_s[:].rearrange("p (c e) -> p c e", c=8),
              wg_d.rearrange("(c p) e -> p c e", p=128))
        for c in range(8):
            r.dma("sync", "ld", o_fm[:, c * T_loc:(c + 1) * T_loc],
                  o_d[c * 128:(c + 1) * 128, :])
        n_ld = 16 * r.val("ld")
        for c in range(8):
            tgt_w = r.dma("sync", "ldw", wstg[:], wao_d[c * 128:(c + 1) * 128, :])
            v.wait_ge(r.sem("ldw"), tgt_w)
            if c > 0:
                pass
            v.tensor_copy(wao_bf[:, c * M:(c + 1) * M], wstg[:])
            nw = r.mark("vector", "wc")
            if c < 7:
                r.sync.wait_ge(r.sem("wc"), nw)
        v.wait_ge(r.sem("ld"), n_ld)
        v.tensor_copy(wg_bf[:], wg_s[:])
        r.mark("vector", "wc")

        for i in range(nt):
            if i >= 2:
                g.wait_ge(r.sem("xev"), i - 1)
            r.dma("gpsimd", "ldx", xt_s[:, (i % 2) * M:(i % 2 + 1) * M],
                  x_d[i * 128:(i + 1) * 128, :])
        t.wait_ge(r.sem("wc"), 9)
        t.wait_ge(r.sem("ld"), n_ld)
        for i in range(nt):
            if i >= 2:
                t.wait_ge(r.sem("xev"), i - 1)
            half = (i % 2) * 1024
            for c in range(8):
                lhs = o_fm[:, c * T_loc + i * 128:c * T_loc + (i + 1) * 128]
                t.matmul(ps8[:, half:half + 512], lhs,
                         wao_bf[:, c * M:c * M + 512], start=(c == 0), stop=False)
                t.matmul(ps8[:, half + 512:half + 1024], lhs,
                         wao_bf[:, c * M + 512:(c + 1) * M],
                         start=(c == 0), stop=(c == 7))
            nmm = r.mark("tensor", "xmm")
            v.wait_ge(r.sem("xmm"), nmm)
            v.wait_ge(r.sem("ldx"), 16 * (i + 1))
            v.tensor_tensor(x1[:, i * M:(i + 1) * M], ps8[:, half:half + 1024],
                            xt_s[:, (i % 2) * M:(i % 2 + 1) * M], OP.add)
            nev = r.mark("vector", "xev")
            g.wait_ge(r.sem("xev"), nev)
            r.dma("gpsimd", "st", x1_o[i * 128:(i + 1) * 128, :],
                  x1[:, i * M:(i + 1) * M])

        s.wait_ge(r.sem("xev"), nt)
        v.wait_ge(r.sem("xev"), nt)
        dsem, done = emit_ln(r, lambda i: x1[:, i * M:(i + 1) * M],
                             lambda i: ln2[:, i * M:(i + 1) * M], nt, sc, eps_s[:],
                             tag="2")
        g.wait_ge(r.sem(dsem), done)
        for i in range(nt):
            r.dma("gpsimd", "st", ln2_o[i * 128:(i + 1) * 128, :],
                  ln2[:, i * M:(i + 1) * M])

        v.wait_ge(r.sem(dsem), done)
        t.wait_ge(r.sem(dsem), done)
        r.scalar.wait_ge(r.sem(dsem), done)
        for i in range(nt):
            lf = (ln2f if i % 2 == 0 else ln2f2)
            if i > 1:
                r.scalar.wait_ge(r.sem("lgt"), i - 1)
            for c in range(8):
                r.scalar.dma_start_transpose(
                    lf[:, c * 128:(c + 1) * 128],
                    ln2[:, i * M + c * 128:i * M + (c + 1) * 128]
                ).then_inc(r.sem("trp"), 16)
                r.bump("trp")
            t.wait_ge(r.sem("trp"), 16 * 8 * (i + 1))
            if i > 1:
                t.wait_ge(r.sem("lgv"), i - 1)
            for c in range(8):
                t.matmul(ps8[:, 2048 + (i % 2) * 8:2056 + (i % 2) * 8],
                         lf[:, c * 128:(c + 1) * 128],
                         wg_bf[:, c * E:(c + 1) * E], start=(c == 0), stop=(c == 7))
            nlg = r.mark("tensor", "lgt")
            v.wait_ge(r.sem("lgt"), nlg)
            v.tensor_copy(lgall[:, i * 8:(i + 1) * 8],
                          ps8[:, 2048 + (i % 2) * 8:2056 + (i % 2) * 8])
            r.mark("vector", "lgv")
        for i in range(nt):
            v.reduce_max(mxa[:, i:i + 1], lgall[:, i * 8:(i + 1) * 8], axis=AX.X)
        r.fence("vector")
        v.tensor_scalar(nega[:], mxa[:], -1.0, None, OP.mult)
        r.fence("vector")
        for i in range(nt):
            v.tensor_scalar(oh[:, i * 8:(i + 1) * 8], lgall[:, i * 8:(i + 1) * 8],
                            mxa[:, i:i + 1], None, OP.is_equal)
        nv1 = r.mark("vector", "lgv")
        s.wait_ge(r.sem("lgv"), nv1)
        for i in range(nt):
            s.activation(exall[:, i * 8:(i + 1) * 8], lgall[:, i * 8:(i + 1) * 8],
                         AF.Exp, bias=nega[:, i:i + 1])
        nse = r.mark("scalar", "lgs")
        v.wait_ge(r.sem("lgs"), nse)
        for i in range(nt):
            v.reduce_sum(suma[:, i:i + 1], exall[:, i * 8:(i + 1) * 8], axis=AX.X)
        r.fence("vector")
        v.reciprocal(gate[:, 0:nt], suma[:, 0:nt])
        for i in range(nt):
            v.tensor_tensor(exall[:, i * 8:(i + 1) * 8], oh[:, i * 8:(i + 1) * 8],
                            iota8[:], OP.mult)
        for i in range(nt):
            v.reduce_sum(eid[:, i:i + 1], exall[:, i * 8:(i + 1) * 8], axis=AX.X)
        nv2 = r.mark("vector", "lgv")
        t.wait_ge(r.sem("lgv"), nv2)
        for i in range(nt):
            t.matmul(ps8[:, 2176 + i * 8:2184 + i * 8], lt_s[:],
                     oh[:, i * 8:(i + 1) * 8], start=True, stop=True)
        ncm = r.mark("tensor", "cmt")
        v.wait_ge(r.sem("cmt"), ncm)
        v.tensor_copy(cumall[:], ps8[:, 2176:2176 + nt * 8])
        ncv = r.mark("vector", "cmv")
        t.wait_ge(r.sem("cmv"), ncv)
        for i in range(nt):
            t.matmul(ps8[0:1, 2304 + i * 8:2312 + i * 8], sel127[:],
                     cumall[:, i * 8:(i + 1) * 8], start=True, stop=True)
        ntt = r.mark("tensor", "cmt")
        v.wait_ge(r.sem("cmt"), ntt)
        v.tensor_copy(totrow[0:1, 0:nt * 8], ps8[0:1, 2304:2304 + nt * 8])
        ntv = r.mark("vector", "cmv")
        g.wait_ge(r.sem("cmv"), ntv)
        r.dma("gpsimd", "t16",
              tot_scr.rearrange("(o i) e -> o (i e)", o=1)[:, 0:nt * 8],
              totrow[0:1, 0:nt * 8])
        g.wait_ge(r.sem("t16"), 16)
        r.dma("gpsimd", "t16", tot16[0:nt, 0:8], tot_scr[0:nt, :])
        t.wait_ge(r.sem("t16"), 32)
        t.matmul(ps8[0:nt, 2432:2440], triu16[0:nt, 0:nt], tot16[0:nt, :],
                 start=True, stop=True)
        t.matmul(ps8[0:1, 2448:2456], ones16[0:nt, :], tot16[0:nt, :],
                 start=True, stop=True)
        npf = r.mark("tensor", "cmt")
        v.wait_ge(r.sem("cmt"), npf)
        v.tensor_copy(pref16[0:nt, :], ps8[0:nt, 2432:2440])
        v.tensor_copy(carry[:], ps8[0:1, 2448:2456])
        npv = r.mark("vector", "crv")
        g.wait_ge(r.sem("crv"), npv)
        r.dma("gpsimd", "t16", pref_scr[0:nt, :], pref16[0:nt, :])
        g.wait_ge(r.sem("t16"), 48)
        r.dma("gpsimd", "t16", prefrow[0:1, 0:nt * 8],
              pref_scr.rearrange("(o i) e -> o (i e)", o=1)[:, 0:nt * 8])
        t.wait_ge(r.sem("t16"), 64)
        for i in range(nt):
            t.matmul(ps8[:, 1536 + i * 8:1544 + i * 8], onesrow[:],
                     prefrow[0:1, i * 8:(i + 1) * 8], start=True, stop=True)
        nbc = r.mark("tensor", "cmt")
        v.wait_ge(r.sem("cmt"), nbc)
        for i in range(nt):
            v.tensor_tensor(cumall[:, i * 8:(i + 1) * 8],
                            cumall[:, i * 8:(i + 1) * 8],
                            ps8[:, 1536 + i * 8:1544 + i * 8], OP.add)
        for i in range(nt):
            v.tensor_tensor(cumall[:, i * 8:(i + 1) * 8],
                            cumall[:, i * 8:(i + 1) * 8],
                            oh[:, i * 8:(i + 1) * 8], OP.subtract)
        for i in range(nt):
            v.tensor_tensor(cumall[:, i * 8:(i + 1) * 8],
                            cumall[:, i * 8:(i + 1) * 8],
                            oh[:, i * 8:(i + 1) * 8], OP.mult)
        for i in range(nt):
            v.reduce_sum(locx[:, i:i + 1], cumall[:, i * 8:(i + 1) * 8],
                         axis=AX.X)
        r.mark("vector", "locv")
        g.wait_ge(r.sem("crv"), npv)
        r.dma("gpsimd", "st", cnt_o[:], carry[:])

        mz = r.sbuf("mz", [128, nt * 4], F32)
        v.memset(mz[:], 0.0)
        r.fence("vector")
        for i in range(nt):
            v.tensor_copy(mz[:, i * 4:i * 4 + 1], eid[:, i:i + 1])
            v.tensor_copy(mz[:, i * 4 + 1:i * 4 + 2], locx[:, i:i + 1])
            v.tensor_copy(mz[:, i * 4 + 2:i * 4 + 3], gate[:, i:i + 1])
        nmz = r.mark("vector", "mzv")
        g.wait_ge(r.sem("mzv"), nmz)
        for i in range(nt):
            r.dma("gpsimd", "st", meta_o[i * 128:(i + 1) * 128, :],
                  mz[:, i * 4:(i + 1) * 4])

        v.wait_ge(r.sem("locv"), r.val("locv"))
        for e in range(E):
            if e > 0:
                v.wait_ge(r.sem("dmm"), e)
            for i in range(nt):
                v.tensor_scalar(ehall[:, i:i + 1], eid[:, i:i + 1], float(e),
                                None, OP.is_equal)
            for i in range(nt):
                v.tensor_scalar(dtile[:, i * 512:(i + 1) * 512], iota512[:],
                                locx[:, i:i + 1], ehall[:, i:i + 1],
                                OP.is_equal, OP.mult)
            ndt = r.mark("vector", "dbt")
            t.wait_ge(r.sem("dbt"), ndt)
            if e > 0:
                t.wait_ge(r.sem("dse"), e)
            for i in range(nt):
                for mc in range(8):
                    t.matmul(ps8[:, mc * 512:(mc + 1) * 512],
                             ln2[:, i * M + mc * 128:i * M + (mc + 1) * 128],
                             dtile[:, i * 512:(i + 1) * 512],
                             start=(i == 0), stop=(i == nt - 1))
            ndm = r.mark("tensor", "dmm")
            v.wait_ge(r.sem("dmm"), ndm)
            for mc in range(8):
                v.tensor_copy(xsel_s[:, mc * 512:(mc + 1) * 512],
                              ps8[:, mc * 512:(mc + 1) * 512])
            nds = r.mark("vector", "dse")
            g.wait_ge(r.sem("dse"), nds)
            for mc in range(8):
                r.dma("gpsimd", "st", xsel_o[e * M + mc * 128:e * M + (mc + 1) * 128, :],
                      xsel_s[:, mc * 512:(mc + 1) * 512])

        g.wait_ge(r.sem("st"), 16 * r.val("st"))
        r.flush()
    return nc



def build_l3b():
    NSLOT = N_CORES * P_LOC
    nseg = NSLOT // 512
    nc = bass.Bass("TRN2", target_bir_lowering=False)
    xsel_d = nc.dram_tensor("xsel", [M, NSLOT], BF16, kind="ExternalInput").ap()
    w1_d = nc.dram_tensor("w1", [M, FFN], F32, kind="ExternalInput").ap()
    w2_d = nc.dram_tensor("w2", [FFN, M], F32, kind="ExternalInput").ap()
    eo_o = nc.dram_tensor("eo", [NSLOT, M], BF16, kind="ExternalOutput").ap()

    with ExitStack() as es:
        r = Rec(nc, es)
        t, s, v, g = r.tensor, r.scalar, r.vector, r.gpsimd
        w1_bf = r.sbuf("w1_bf", [128, 8 * FFN], BF16)
        w2_bf = r.sbuf("w2_bf", [128, 32 * M], BF16)
        xs_s = r.sbuf("xs_s", [128, 2 * 8 * 512], BF16)
        h_s = r.sbuf("h_s", [128, 32 * 512], BF16)
        eo_s = r.sbuf("eo_s", [128, 2 * 4 * M], BF16)
        ps8 = r.psum("ps8", [128, 4096])

        for j in range(8):
            r.dma("gpsimd", "ldw1",
                  w1_bf[:].rearrange("p (c n) -> p c n", c=8)
                  [:, :, j * 512:(j + 1) * 512],
                  w1_d[:, j * 512:(j + 1) * 512].rearrange(
                      "(c p) n -> p c n", p=128))
            r.dma("gpsimd", "ldw2",
                  w2_bf[:].rearrange("p (c n) -> p c n", c=32)
                  [:, j * 4:(j + 1) * 4, :],
                  w2_d[j * 4 * 128:(j + 1) * 4 * 128, :].rearrange(
                      "(c p) n -> p c n", p=128))
        for sg in range(nseg):
            if sg >= 2:
                r.sync.wait_ge(r.sem("h1t"), 32 * (sg - 1))
            r.dma("sync", "ldx",
                  xs_s[:, (sg % 2) * 4096:(sg % 2 + 1) * 4096].rearrange(
                      "p (c t) -> p c t", c=8),
                  xsel_d[:, sg * 512:(sg + 1) * 512].rearrange(
                      "(c p) t -> p c t", p=128))
        for sg in range(nseg):
            xb = xs_s[:, (sg % 2) * 4096:(sg % 2 + 1) * 4096]
            eb = eo_s[:, (sg % 2) * 4096:(sg % 2 + 1) * 4096]
            t.wait_ge(r.sem("ldx"), 16 * (sg + 1))
            for fg in range(4):
                if sg == 0:
                    t.wait_ge(r.sem("ldw1"), 16 * (2 * fg + 2))
                if sg > 0 or fg > 0:
                    t.wait_ge(r.sem("hev"), 8 * (4 * sg + fg))
                if fg == 0 and sg > 0:
                    t.wait_ge(r.sem("eev"), 4 * sg)
                for fc8 in range(8):
                    f = fg * 8 + fc8
                    for mc in range(8):
                        t.matmul(ps8[:, fc8 * 512:(fc8 + 1) * 512],
                                 w1_bf[:, mc * FFN + f * 128:mc * FFN + (f + 1) * 128],
                                 xb[:, mc * 512:(mc + 1) * 512],
                                 start=(mc == 0), stop=(mc == 7))
                    r.mark("tensor", "h1t")
                nh = r.val("h1t")
                s.wait_ge(r.sem("h1t"), nh)
                for fc8 in range(8):
                    f = fg * 8 + fc8
                    s.activation(h_s[:, f * 512:(f + 1) * 512],
                                 ps8[:, fc8 * 512:(fc8 + 1) * 512], AF.Relu)
                    r.mark("scalar", "hev")
            nhe = r.val("hev")
            t.wait_ge(r.sem("hev"), nhe)
            if sg >= 2:
                t.wait_ge(r.sem("eev"), 4 * (sg - 1))
            for st in range(4):
                for fc in range(32):
                    if sg == 0 and st == 0 and fc % 4 == 0:
                        t.wait_ge(r.sem("ldw2"), 16 * (fc // 4 + 1))
                    t.matmul(ps8[:, st * 1024:st * 1024 + 512],
                             h_s[:, fc * 512 + st * 128:fc * 512 + (st + 1) * 128],
                             w2_bf[:, fc * M:fc * M + 512],
                             start=(fc == 0), stop=False)
                    t.matmul(ps8[:, st * 1024 + 512:(st + 1) * 1024],
                             h_s[:, fc * 512 + st * 128:fc * 512 + (st + 1) * 128],
                             w2_bf[:, fc * M + 512:(fc + 1) * M],
                             start=(fc == 0), stop=(fc == 31))
            ne2 = r.mark("tensor", "e2t")
            v.wait_ge(r.sem("e2t"), ne2)
            for st in range(4):
                v.tensor_copy(eb[:, st * M:(st + 1) * M],
                              ps8[:, st * 1024:(st + 1) * 1024])
                r.mark("vector", "eev")
            nev = r.val("eev")
            g.wait_ge(r.sem("eev"), nev)
            for st in range(4):
                r.dma("gpsimd", "st",
                      eo_o[sg * 512 + st * 128:sg * 512 + (st + 1) * 128, :],
                      eb[:, st * M:(st + 1) * M])
        g.wait_ge(r.sem("st"), 16 * r.val("st"))
        r.flush()
    return nc



def build_l4(T_loc, T_total):
    nt = T_loc // 128
    CAP = T_total // E
    nc = bass.Bass("TRN2", target_bir_lowering=False)
    x1_d = nc.dram_tensor("x1", [T_loc, M], F32, kind="ExternalInput").ap()
    eo_d = nc.dram_tensor("eo", [E * P_LOC, M], BF16, kind="ExternalInput").ap()
    meta_d = nc.dram_tensor("meta", [T_loc, 4], F32, kind="ExternalInput").ap()
    cnts_d = nc.dram_tensor("cnts", [N_CORES, E], F32, kind="ExternalInput").ap()
    pmask_d = nc.dram_tensor("prevmask", [N_CORES, 1], F32, kind="ExternalInput").ap()
    ident_d = nc.dram_tensor("ident_f32", [128, 128], F32, kind="ExternalInput").ap()
    onesrow_d = nc.dram_tensor("ones_row_f32", [1, 128], F32, kind="ExternalInput").ap()
    iota8_d = nc.dram_tensor("iota8", [128, 8], F32, kind="ExternalInput").ap()
    slotid_d = nc.dram_tensor("slotid", [128, 1], F32, kind="ExternalInput").ap()
    x2_o = nc.dram_tensor("x2", [T_loc, M], F32, kind="ExternalOutput").ap()

    with ExitStack() as es:
        r = Rec(nc, es)
        t, s, v, g = r.tensor, r.scalar, r.vector, r.gpsimd
        x1 = r.sbuf("x1_s", [128, nt * M], F32)
        eo_s = r.sbuf("eo_sb", [128, 32 * M], BF16)
        meta_s = r.sbuf("meta_sb", [128, nt * 4], F32)
        cnts_s = r.sbuf("cnts_s", [N_CORES, E], F32)
        pmask_s = r.sbuf("pmask_s", [N_CORES, 1], F32)
        ident_s = r.sbuf("ident_fs", [128, 128], F32)
        onesrow = r.sbuf("onesrow_s", [1, 128], F32)
        iota8 = r.sbuf("iota8_sb", [128, 8], F32)
        slotid = r.sbuf("slotid_s", [128, 1], F32)
        off_bc = r.sbuf("off_bc", [128, 8], F32)
        ohx = r.sbuf("ohx", [128, 8], F32)
        colv = r.sbuf("colv", [128, 4], F32)
        rows = r.sbuf("rows", [1, 3 * T_loc], F32)
        bce = r.sbuf("bce", [128, T_loc], F32)
        bcl = r.sbuf("bcl", [128, T_loc], F32)
        bcc = r.sbuf("bcc", [128, T_loc], BF16)
        lsel = r.sbuf("lsel", [128, T_loc], F32)
        dte = r.sbuf("dte", [128, 4 * T_loc], BF16)
        ps8 = r.psum("ps8", [128, 4096])

        for dst, srcd in ((cnts_s, cnts_d), (pmask_s, pmask_d), (ident_s, ident_d),
                          (onesrow, onesrow_d), (iota8, iota8_d), (slotid, slotid_d)):
            r.dma("sync", "ld", dst[:], srcd[:])
        for i in range(nt):
            r.dma("sync", "ld", x1[:, i * M:(i + 1) * M], x1_d[i * 128:(i + 1) * 128, :])
            r.dma("sync", "ld", meta_s[:, i * 4:(i + 1) * 4],
                  meta_d[i * 128:(i + 1) * 128, :])
        n_ld = 16 * r.val("ld")
        r.dma("gpsimd", "ldeo", eo_s[:].rearrange("p (c n) -> p c n", c=32),
              eo_d.rearrange("(c p) n -> p c n", p=128))
        n_eo = 16 * r.val("ldeo")

        t.wait_ge(r.sem("ld"), n_ld)
        t.matmul(ps8[0:1, 0:8], pmask_s[:], cnts_s[:], start=True, stop=True)
        nof = r.mark("tensor", "oft")
        v.wait_ge(r.sem("oft"), nof)
        v.tensor_copy(rows[0:1, 0:8], ps8[0:1, 0:8])
        nov = r.mark("vector", "ofv")
        t.wait_ge(r.sem("ofv"), nov)
        t.matmul(ps8[:, 16:24], onesrow[:], rows[0:1, 0:8], start=True, stop=True)
        nof = r.mark("tensor", "oft")
        v.wait_ge(r.sem("oft"), nof)
        v.tensor_copy(off_bc[:], ps8[:, 16:24])
        r.fence("vector")

        v.wait_ge(r.sem("ld"), n_ld)
        for i in range(nt):
            eidc = meta_s[:, i * 4:i * 4 + 1]
            locc = meta_s[:, i * 4 + 1:i * 4 + 2]
            gatec = meta_s[:, i * 4 + 2:i * 4 + 3]
            v.tensor_scalar(ohx[:], iota8[:], eidc, None, OP.is_equal)
            r.fence("vector")
            v.tensor_tensor(ohx[:], ohx[:], off_bc[:], OP.mult)
            r.fence("vector")
            v.reduce_sum(colv[:, 0:1], ohx[:], axis=AX.X)
            r.fence("vector")
            v.tensor_tensor(colv[:, 1:2], colv[:, 0:1], locc, OP.add)
            r.fence("vector")
            v.tensor_scalar(colv[:, 2:3], colv[:, 1:2], float(CAP), None, OP.is_lt)
            v.tensor_scalar(colv[:, 3:4], locc, float(P_LOC), None, OP.is_lt)
            r.fence("vector")
            v.tensor_tensor(colv[:, 2:3], colv[:, 2:3], colv[:, 3:4], OP.mult)
            r.fence("vector")
            v.tensor_tensor(colv[:, 3:4], colv[:, 2:3], gatec, OP.mult)
            nvc = r.mark("vector", "cfv")
            t.wait_ge(r.sem("cfv"), nvc)
            t.matmul(ps8[0:1, 1024:1152], eidc, ident_s[:], start=True, stop=True)
            t.matmul(ps8[0:1, 1152:1280], locc, ident_s[:], start=True, stop=True)
            t.matmul(ps8[0:1, 1280:1408], colv[:, 3:4], ident_s[:], start=True,
                     stop=True)
            nrt = r.mark("tensor", "rwt")
            v.wait_ge(r.sem("rwt"), nrt)
            v.tensor_copy(rows[0:1, i * 128:(i + 1) * 128], ps8[0:1, 1024:1152])
            v.tensor_copy(rows[0:1, T_loc + i * 128:T_loc + (i + 1) * 128],
                          ps8[0:1, 1152:1280])
            v.tensor_copy(rows[0:1, 2 * T_loc + i * 128:2 * T_loc + (i + 1) * 128],
                          ps8[0:1, 1280:1408])
            r.mark("vector", "rwv")
        SEG = 512 if T_loc % 512 == 0 else T_loc
        nseg3 = T_loc // SEG
        for whichsb, ofs in ((bce, 0), (bcl, T_loc), (bcc, 2 * T_loc)):
            for sgi in range(nseg3):
                t.wait_ge(r.sem("rwv"), nt)
                t.matmul(ps8[:, 2048:2048 + SEG], onesrow[:],
                         rows[0:1, ofs + sgi * SEG:ofs + (sgi + 1) * SEG],
                         start=True, stop=True)
                nbc = r.mark("tensor", "bct")
                v.wait_ge(r.sem("bct"), nbc)
                v.tensor_copy(whichsb[:, sgi * SEG:(sgi + 1) * SEG],
                              ps8[:, 2048:2048 + SEG])
                nbv = r.mark("vector", "bcv")
                t.wait_ge(r.sem("bcv"), nbv)

        g.wait_ge(r.sem("ldeo"), n_eo)
        v.wait_ge(r.sem("ld"), n_ld)
        for e in range(E):
            v.wait_ge(r.sem("bcv"), 3 * nseg3)
            if e > 0:
                v.wait_ge(r.sem("cbt"), e * nt)
            v.tensor_scalar(lsel[:], bce[:], float(e), None, OP.is_equal)
            v.tensor_scalar(dte[:, 0:T_loc], bce[:], float(e), None, OP.is_equal)
            r.fence("vector")
            v.tensor_tensor(lsel[:], lsel[:], bcl[:], OP.mult)
            r.fence("vector")
            v.tensor_tensor(lsel[:], lsel[:], dte[:, 0:T_loc], OP.add)
            r.fence("vector")
            v.tensor_scalar(lsel[:], lsel[:], -1.0, None, OP.add)
            r.fence("vector")
            for st in range(4):
                v.tensor_scalar(dte[:, st * T_loc:(st + 1) * T_loc], lsel[:],
                                slotid[:], None, OP.is_equal)
                r.fence("vector")
                v.tensor_tensor(dte[:, st * T_loc:(st + 1) * T_loc],
                                dte[:, st * T_loc:(st + 1) * T_loc], bcc[:], OP.mult)
                if st < 3:
                    v.tensor_scalar(lsel[:], lsel[:], -128.0, None, OP.add)
                    r.fence("vector")
            ndb = r.mark("vector", "dbv")
            t.wait_ge(r.sem("dbv"), ndb)
            for i in range(nt):
                k = e * nt + i
                half = (i % 2) * 2048
                if k >= 2:
                    t.wait_ge(r.sem("cbv"), k - 1)
                for st in range(4):
                    lhs = dte[:, st * T_loc + i * 128:st * T_loc + (i + 1) * 128]
                    rhsbase = (e * 4 + st) * M
                    t.matmul(ps8[:, half:half + 512], lhs,
                             eo_s[:, rhsbase:rhsbase + 512],
                             start=(st == 0), stop=False)
                    t.matmul(ps8[:, half + 512:half + 1024], lhs,
                             eo_s[:, rhsbase + 512:rhsbase + M],
                             start=(st == 0), stop=(st == 3))
                ncb = r.mark("tensor", "cbt")
                v.wait_ge(r.sem("cbt"), ncb)
                v.tensor_tensor(x1[:, i * M:(i + 1) * M], ps8[:, half:half + 1024],
                                x1[:, i * M:(i + 1) * M], OP.add)
                r.mark("vector", "cbv")
        nxv = r.val("cbv")
        g.wait_ge(r.sem("cbv"), nxv)
        for i in range(nt):
            r.dma("gpsimd", "st", x2_o[i * 128:(i + 1) * 128, :],
                  x1[:, i * M:(i + 1) * M])
        g.wait_ge(r.sem("st"), 16 * r.val("st"))
        r.flush()
    return nc



def build_l5(T_loc, debug=False):
    NCH = T_loc // RCHUNK
    HALF = min(T_loc, 1024)
    NHALF = T_loc // HALF
    CH_H = HALF // RCHUNK
    SUP = max(1, HALF // 512)
    SW = HALF // SUP
    nc = bass.Bass("TRN2", target_bir_lowering=False)
    x2_d = nc.dram_tensor("x2", [T_loc, M], F32, kind="ExternalInput").ap()
    prev_d = nc.dram_tensor("prevrow", [1, M], F32, kind="ExternalInput").ap()
    wbig_d = {k: nc.dram_tensor(k, [M, M], F32, kind="ExternalInput").ap()
              for k in ("Wr", "Wk", "Wv")}
    w1_d = nc.dram_tensor("w1", [M, 64], F32, kind="ExternalInput").ap()
    w2_d = nc.dram_tensor("w2", [64, M], F32, kind="ExternalInput").ap()
    a1_d = nc.dram_tensor("a1", [M, 64], F32, kind="ExternalInput").ap()
    a2_d = nc.dram_tensor("a2", [64, M], F32, kind="ExternalInput").ap()
    g1_d = nc.dram_tensor("g1", [M, 128], F32, kind="ExternalInput").ap()
    g2_d = nc.dram_tensor("g2", [128, M], F32, kind="ExternalInput").ap()
    cv_d = {nm: nc.dram_tensor("cv_" + nm, [128, 8], F32, kind="ExternalInput").ap()
            for nm in ("w0", "a0", "k_k", "k_a", "rk",
                       "mr", "mw", "mk", "mv", "ma", "mg")}
    eps_d = nc.dram_tensor("epsln", [128, 1], F32, kind="ExternalInput").ap()
    triu1_d = nc.dram_tensor("triu1_bf", [128, 128], BF16, kind="ExternalInput").ap()
    tril1_d = nc.dram_tensor("tril1_bf", [128, 128], BF16, kind="ExternalInput").ap()
    triu0_d = nc.dram_tensor("triu0_bf", [128, 128], BF16, kind="ExternalInput").ap()
    ident_d = nc.dram_tensor("ident", [128, 128], BF16, kind="ExternalInput").ap()
    bones_d = nc.dram_tensor("blockones64", [128, 2], BF16, kind="ExternalInput").ap()
    sel2_d = nc.dram_tensor("sel2", [2, 128], BF16, kind="ExternalInput").ap()
    eye64_d = nc.dram_tensor("eye64_f32", [64, 64], F32, kind="ExternalInput").ap()
    bmask_d = nc.dram_tensor("bmask_bf", [128, 640], BF16, kind="ExternalInput").ap()

    gfm_o = nc.dram_tensor("gfm", [M, T_loc], BF16, kind="ExternalOutput").ap()
    rtfm_o = nc.dram_tensor("rtfm", [M, T_loc], BF16, kind="ExternalOutput").ap()
    rhat_o = nc.dram_tensor("rhblob", [NCH * RH * 64, 128], BF16,
                            kind="ExternalOutput").ap()
    yloc_o = nc.dram_tensor("ylblob", [NCH * RH * 64, 128], BF16,
                            kind="ExternalOutput").ap()
    phi_o = nc.dram_tensor("phiblob", [NCH * RH * 64, 64], F32,
                           kind="ExternalOutput").ap()
    zp_o = nc.dram_tensor("zpblob", [NCH * RH * 64, 64], F32,
                          kind="ExternalOutput").ap()
    wl_o = nc.dram_tensor("wlblob", [NHALF * 128, 8 * CH_H], F32,
                          kind="ExternalOutput").ap()
    spp_o = nc.dram_tensor("span_P", [RH * 64, 64], F32, kind="ExternalOutput").ap()
    spq_o = nc.dram_tensor("span_Q", [RH * 64, 64], F32, kind="ExternalOutput").ap()
    if debug:
        dbg_o = {nm: nc.dram_tensor("dbg_" + nm, [128, 8 * min(T_loc, 1024)], BF16,
                                    kind="ExternalOutput").ap()
                 for nm in ("lnf", "rf", "kf", "vf", "agf", "lwf", "kkn",
                            "dRf", "dKf", "dBf", "dAf")}
        dbg2_o = nc.dram_tensor("dbg_b1", [128, 1024], BF16,
                                kind="ExternalOutput").ap()
        dbg3_o = nc.dram_tensor("dbg_phi", [64, 128], F32,
                                kind="ExternalOutput").ap()

    with ExitStack() as es:
        r = Rec(nc, es)
        t, s, v, g = r.tensor, r.scalar, r.vector, r.gpsimd
        eps_s = r.sbuf("eps_sb", [128, 1], F32)
        triu1 = r.sbuf("triu1", [128, 128], BF16)
        tril1 = r.sbuf("tril1", [128, 128], BF16)
        triu0 = r.sbuf("triu0", [128, 128], BF16)
        ident_s = r.sbuf("ident_sb", [128, 128], BF16)
        bones = r.sbuf("bones", [128, 2], BF16)
        sel2 = r.sbuf("sel2_s", [2, 128], BF16)
        eye64 = r.sbuf("eye64_s", [64, 64], F32)
        prow = r.sbuf("prow", [1, M], F32)
        pstat = r.sbuf("pstat", [1, 8], F32)
        cvec = {nm: r.sbuf("cvs_" + nm, [128, 8], F32) for nm in cv_d}
        cneg = {nm: r.sbuf("cvn_" + nm, [128, 8], F32)
                for nm in ("mr", "mw", "mk", "mv", "ma", "mg")}
        w1s = r.sbuf("w1s", [128, 8 * 64], BF16)
        w2s = r.sbuf("w2s", [64, M], BF16)
        a1s = r.sbuf("a1s", [128, 8 * 64], BF16)
        a2s = r.sbuf("a2s", [64, M], BF16)
        g1s = r.sbuf("g1s", [128, 8 * 128], BF16)
        g2s = r.sbuf("g2s", [128, M], BF16)
        wbig = r.sbuf("wbig", [128, 8 * 512], BF16)
        x2t = r.sbuf("x2t", [128, 2 * M], F32)
        ln3 = r.sbuf("ln3_sb", [128, 2 * M], BF16)
        lnf = r.sbuf("lnf", [128, 8 * (HALF + 16)], BF16)
        stash8 = r.sbuf("stash8", [128, 8], BF16)
        mixb = r.sbuf("mixb", [128, 8 * HALF], BF16)
        rf = r.sbuf("rf", [128, 8 * HALF], BF16)
        kf = r.sbuf("kf", [128, 8 * HALF], BF16)
        vf = r.sbuf("vf", [128, 8 * HALF], BF16)
        kkn = r.sbuf("kkn", [128, 8 * HALF], BF16)
        agf = r.sbuf("agf", [128, 8 * HALF], BF16)
        lwf = r.sbuf("lwf", [128, 8 * HALF], BF16)
        Af, Bf, Kf, Rf = kkn, agf, kf, rf
        wlh = r.sbuf("wlh", [128, 8 * CH_H], F32)
        lnstat = r.sbuf("lnstat", [128, 32], F32)
        nrmsc = r.sbuf("nrmsc", [2, 512], BF16)
        SBb0 = r.sbuf("SBb0", [128, 768], BF16)
        SBb1 = r.sbuf("SBb1", [128, 768], BF16)
        Tpb0 = r.sbuf("Tpb0", [128, 256], BF16)
        Tpb1 = r.sbuf("Tpb1", [128, 256], BF16)
        bmask_s = r.sbuf("bmask_s", [128, 640], BF16)
        tscr = r.sbuf("tscr", [128, 512], BF16)
        zer128 = r.sbuf("zer128", [128, 128], BF16)
        tscr2 = r.sbuf("tscr2", [128, 512], BF16)
        tf32 = r.sbuf("tf32", [128, 128], F32)
        outY = r.sbuf("outY", [64, RH * 128], BF16)
        outR = r.sbuf("outR", [64, RH * 128], BF16)
        outP = r.sbuf("outP", [64, RH * 64], F32)
        outZ = r.sbuf("outZ", [64, RH * 64], F32)
        spanP = r.sbuf("spanP", [64, RH * 64], F32)
        spanQ = r.sbuf("spanQ", [64, RH * 64], F32)
        ps8 = r.psum("ps8", [128, 3584])
        psT = r.psum("psT", [128, 512], BF16)

        for dst, srcd in ((eps_s, eps_d), (triu1, triu1_d), (tril1, tril1_d),
                          (triu0, triu0_d), (ident_s, ident_d), (bones, bones_d),
                          (sel2, sel2_d), (eye64, eye64_d), (prow, prev_d),
                          (bmask_s, bmask_d)):
            r.dma("sync", "ld", dst[:], srcd[:])
        for nm in cv_d:
            r.dma("sync", "ld", cvec[nm][:], cv_d[nm][:])
        r.dma("gpsimd", "ldw", w1s[:].rearrange("p (c n) -> p c n", c=8),
              w1_d.rearrange("(c p) n -> p c n", p=128))
        r.dma("gpsimd", "ldw", w2s[:], w2_d[:])
        r.dma("gpsimd", "ldw", a1s[:].rearrange("p (c n) -> p c n", c=8),
              a1_d.rearrange("(c p) n -> p c n", p=128))
        r.dma("gpsimd", "ldw", a2s[:], a2_d[:])
        r.dma("gpsimd", "ldw", g1s[:].rearrange("p (c n) -> p c n", c=8),
              g1_d.rearrange("(c p) n -> p c n", p=128))
        n_ldw = r.dma("gpsimd", "ldw", g2s[:], g2_d[:])
        n_ld = 16 * r.val("ld")

        v.wait_ge(r.sem("ld"), n_ld)
        for nm in cneg:
            v.tensor_scalar(cneg[nm][:], cvec[nm][:], -1.0, None, OP.mult)
        r.fence("vector")
        for nm in cneg:
            v.tensor_scalar(cneg[nm][:], cneg[nm][:], 1.0, None, OP.add)
        r.fence("vector")
        v.memset(spanQ[:], 0.0)
        v.memset(spanP[:], 0.0)
        r.fence("vector")
        for h in range(RH):
            v.tensor_tensor(spanP[:, h * 64:(h + 1) * 64],
                            spanP[:, h * 64:(h + 1) * 64], eye64[:], OP.add)
        _spinit = r.mark("vector", "b1v")

        s.wait_ge(r.sem("ld"), n_ld)
        s.activation(x2t[0:1, 0:M], prow[:], AF.Square)
        nps = r.mark("scalar", "pls")
        v.wait_ge(r.sem("pls"), nps)
        v.reduce_sum(pstat[0:1, 0:1], prow[:], axis=AX.X)
        v.reduce_sum(pstat[0:1, 1:2], x2t[0:1, 0:M], axis=AX.X)
        r.fence("vector")
        v.tensor_scalar(pstat[0:1, 2:3], pstat[0:1, 0:1], 1.0 / M, None, OP.mult)
        v.tensor_scalar(pstat[0:1, 3:4], pstat[0:1, 1:2], 1.0 / M, None, OP.mult)
        r.fence("vector")
        v.tensor_tensor(pstat[0:1, 4:5], pstat[0:1, 2:3], pstat[0:1, 2:3], OP.mult)
        r.fence("vector")
        v.tensor_tensor(pstat[0:1, 3:4], pstat[0:1, 3:4], pstat[0:1, 4:5], OP.subtract)
        npv = r.mark("vector", "plv")
        s.wait_ge(r.sem("plv"), npv)
        s.activation(pstat[0:1, 3:4], pstat[0:1, 3:4], AF.Sqrt, bias=eps_s[0:1, :])
        nps = r.mark("scalar", "pls")
        v.wait_ge(r.sem("pls"), nps)
        v.reciprocal(pstat[0:1, 5:6], pstat[0:1, 3:4])
        r.fence("vector")
        v.tensor_tensor(pstat[0:1, 6:7], pstat[0:1, 2:3], pstat[0:1, 5:6], OP.mult)
        r.fence("vector")
        v.tensor_scalar(pstat[0:1, 6:7], pstat[0:1, 6:7], -1.0, None, OP.mult)
        npv = r.mark("vector", "plv")
        s.wait_ge(r.sem("plv"), npv)
        s.activation(prow[:], prow[:], AF.Identity, bias=pstat[0:1, 6:7],
                     scale=pstat[0:1, 5:6])
        n_prow = r.mark("scalar", "pls")
        prow_dram = nc.dram_tensor("prow_scr", [1, M], F32, kind="Internal").ap()
        g.wait_ge(r.sem("pls"), n_prow)
        r.dma("gpsimd", "prd", prow_dram[:], prow[:])

        gc = 0
        b1_pi = 0
        b1_state = {}
        span_cnt = {h: _spinit for h in range(RH)}
        last_vsp = [_spinit]
        blb_gate = 0
        for hf in range(NHALF):
            if hf > 0:
                t.wait_ge(r.sem("b1v"), r.val("b1v"))
                s.wait_ge(r.sem("b1v"), r.val("b1v"))
            tb = hf * HALF
            ntile = HALF // 128
            a0_marks = []
            if hf > 0:
                g.wait_ge(r.sem("lap"), r.val("lap"))
            for i in range(ntile):
                if i >= 2:
                    g.wait_ge(r.sem("lnv"), a0_marks[i - 2])
                r.dma("gpsimd", "lxd", x2t[:, (i % 2) * M:(i % 2 + 1) * M],
                      x2_d[tb + i * 128:tb + (i + 1) * 128, :])
                s.wait_ge(r.sem("lxd"), 16 * r.val("lxd"))
                s.activation(ps8[:, (i % 2) * 1024:(i % 2 + 1) * 1024],
                             x2t[:, (i % 2) * M:(i % 2 + 1) * M], AF.Square)
                nss = r.mark("scalar", "lns")
                v.wait_ge(r.sem("lns"), nss)
                v.reduce_sum(lnstat[:, i:i + 1], x2t[:, (i % 2) * M:(i % 2 + 1) * M],
                             axis=AX.X)
                v.reduce_sum(lnstat[:, 8 + i:9 + i],
                             ps8[:, (i % 2) * 1024:(i % 2 + 1) * 1024],
                             axis=AX.X)
                a0_marks.append(r.mark("vector", "lnv"))
            r.fence("vector")
            v.tensor_scalar(lnstat[:, 16:16 + ntile], lnstat[:, 0:ntile], 1.0 / M,
                            None, OP.mult)
            v.tensor_scalar(lnstat[:, 8:8 + ntile], lnstat[:, 8:8 + ntile], 1.0 / M,
                            None, OP.mult)
            r.fence("vector")
            v.tensor_tensor(lnstat[:, 24:24 + ntile], lnstat[:, 16:16 + ntile],
                            lnstat[:, 16:16 + ntile], OP.mult)
            r.fence("vector")
            v.tensor_tensor(lnstat[:, 8:8 + ntile], lnstat[:, 8:8 + ntile],
                            lnstat[:, 24:24 + ntile], OP.subtract)
            nlv = r.mark("vector", "lnv2")
            s.wait_ge(r.sem("lnv2"), nlv)
            s.activation(lnstat[:, 8:8 + ntile], lnstat[:, 8:8 + ntile], AF.Sqrt,
                         bias=eps_s[:])
            nls = r.mark("scalar", "lns2")
            v.wait_ge(r.sem("lns2"), nls)
            v.reciprocal(lnstat[:, 24:24 + ntile], lnstat[:, 8:8 + ntile])
            r.fence("vector")
            v.tensor_tensor(lnstat[:, 16:16 + ntile], lnstat[:, 16:16 + ntile],
                            lnstat[:, 24:24 + ntile], OP.mult)
            r.fence("vector")
            v.tensor_scalar(lnstat[:, 16:16 + ntile], lnstat[:, 16:16 + ntile],
                            -1.0, None, OP.mult)
            nlv = r.mark("vector", "lnv2")

            if hf == 0:
                g.wait_ge(r.sem("prd"), 16)
                for c in range(8):
                    r.dma("gpsimd", "pcl",
                          lnf[:, c * (HALF + 16) + 15:c * (HALF + 16) + 16],
                          prow_dram[0:1, c * 128:(c + 1) * 128].rearrange(
                              "x p -> p x"))
                n_pcl = 16 * r.val("pcl")
            else:
                v.tensor_copy(lnf[:, 0:8].rearrange("p (c x) -> p c x", c=8)
                              if False else lnf[:, 0:1], stash8[:, 0:1])
                for c in range(8):
                    v.tensor_copy(lnf[:, c * (HALF + 16) + 15:c * (HALF + 16) + 16],
                                  stash8[:, c:c + 1])
                r.mark("vector", "pcv")
            s.wait_ge(r.sem("lnv2"), nlv)
            g.wait_ge(r.sem("lnv"), a0_marks[-1])
            a1_marks = []
            for i in range(ntile):
                if i >= 2:
                    g.wait_ge(r.sem("lap"), a1_marks[i - 2])
                r.dma("gpsimd", "lxd", x2t[:, (i % 2) * M:(i % 2 + 1) * M],
                      x2_d[tb + i * 128:tb + (i + 1) * 128, :])
                s.wait_ge(r.sem("lxd"), 16 * r.val("lxd"))
                if i >= 2:
                    s.wait_ge(r.sem("trp"), 16 * (r.val("trp") - 8))
                s.activation(ln3[:, (i % 2) * M:(i % 2 + 1) * M],
                             x2t[:, (i % 2) * M:(i % 2 + 1) * M], AF.Identity,
                             bias=lnstat[:, 16 + i:17 + i],
                             scale=lnstat[:, 24 + i:25 + i])
                a1_marks.append(r.mark("scalar", "lap"))
                for c in range(8):
                    r.scalar.dma_start_transpose(
                        lnf[:, c * (HALF + 16) + 16 + i * 128:
                            c * (HALF + 16) + 16 + (i + 1) * 128],
                        ln3[:, (i % 2) * M + c * 128:(i % 2) * M + (c + 1) * 128]
                    ).then_inc(r.sem("trp"), 16)
                    r.bump("trp")
            n_trp = 16 * r.val("trp")
            if hf + 1 < NHALF:
                v.wait_ge(r.sem("trp"), n_trp)
                for c in range(8):
                    v.tensor_copy(stash8[:, c:c + 1],
                                  lnf[:, c * (HALF + 16) + 15 + HALF:
                                      c * (HALF + 16) + 16 + HALF])
                r.mark("vector", "stv")

            def emit_mix(mu):
                v.wait_ge(r.sem("pjt"), r.val("pjt"))
                v.wait_ge(r.sem("z1t"), r.val("z1t"))
                v.wait_ge(r.sem("g1t"), r.val("g1t"))
                v.wait_ge(r.sem("trp"), n_trp)
                if hf == 0:
                    v.wait_ge(r.sem("pcl"), n_pcl)
                for c in range(8):
                    cb = c * (HALF + 16) + 16
                    for sg_ in range(SUP):
                        dsl = mixb[:, c * HALF + sg_ * SW:c * HALF + (sg_ + 1) * SW]
                        v.tensor_scalar(tscr[:, 0:SW],
                                        lnf[:, cb - 1 + sg_ * SW:
                                            cb - 1 + (sg_ + 1) * SW],
                                        cvec[mu][:, c:c + 1], None, OP.mult)
                        v.tensor_scalar(dsl,
                                        lnf[:, cb + sg_ * SW:cb + (sg_ + 1) * SW],
                                        cneg[mu][:, c:c + 1], None, OP.mult)
                        r.fence("vector")
                        v.tensor_tensor(dsl, dsl, tscr[:, 0:SW], OP.add)
                return r.mark("vector", "mxv")

            def emit_bigproj(wd, dst):
                nmx = r.val("mxv")
                for ochalf in range(2):
                    g.wait_ge(r.sem("pjt"), r.val("pjt"))
                    tgw = r.dma("gpsimd", "ldw2",
                                wbig[:].rearrange("p (c n) -> p c n", c=8),
                                wd[:, ochalf * 512:(ochalf + 1) * 512].rearrange(
                                    "(c p) n -> p c n", p=128))
                    t.wait_ge(r.sem("ldw2"), tgw)
                    t.wait_ge(r.sem("mxv"), nmx)
                    for oc4 in range(4):
                        oc = ochalf * 4 + oc4
                        for sg_ in range(SUP):
                            k = r.val("pjt")
                            if k >= 2:
                                t.wait_ge(r.sem("pjv"), k - 1)
                            for mc in range(8):
                                t.matmul(ps8[:, (k % 2) * 512:(k % 2) * 512 + SW],
                                         wbig[:, mc * 512 + oc4 * 128:
                                              mc * 512 + (oc4 + 1) * 128],
                                         mixb[:, mc * HALF + sg_ * SW:
                                              mc * HALF + (sg_ + 1) * SW],
                                         start=(mc == 0), stop=(mc == 7))
                            npj = r.mark("tensor", "pjt")
                            v.wait_ge(r.sem("pjt"), npj)
                            v.tensor_copy(dst[:, oc * HALF + sg_ * SW:
                                              oc * HALF + (sg_ + 1) * SW],
                                          ps8[:, (k % 2) * 512:(k % 2) * 512 + SW])
                            r.mark("vector", "pjv")

            def emit_smallpath(m1s, m2s, biasname, dst, kdim):
                nmx = r.val("mxv")
                t.wait_ge(r.sem("mxv"), nmx)
                for sg_ in range(SUP):
                    t.wait_ge(r.sem("z2s"), r.val("z2s"))
                    for mc in range(8):
                        t.matmul(ps8[0:kdim, 3072:3072 + SW],
                                 m1s[:, mc * kdim:(mc + 1) * kdim],
                                 mixb[:, mc * HALF + sg_ * SW:
                                      mc * HALF + (sg_ + 1) * SW],
                                 start=(mc == 0), stop=(mc == 7))
                    nz1 = r.mark("tensor", "z1t")
                    s.wait_ge(r.sem("z1t"), nz1)
                    s.activation(tscr2[0:kdim, 0:SW], ps8[0:kdim, 3072:3072 + SW],
                                 AF.Tanh if biasname == "w0" else AF.Identity)
                    nz2 = r.mark("scalar", "z2s")
                    t.wait_ge(r.sem("z2s"), nz2)
                    for oc in range(8):
                        if oc > 0 or sg_ > 0:
                            t.wait_ge(r.sem("z4s"), r.val("z4s"))
                        t.matmul(ps8[:, 1024:1024 + SW],
                                 m2s[0:kdim, oc * 128:(oc + 1) * 128],
                                 tscr2[0:kdim, 0:SW], start=True, stop=True)
                        nz3 = r.mark("tensor", "z3t")
                        s.wait_ge(r.sem("z3t"), nz3)
                        s.activation(dst[:, oc * HALF + sg_ * SW:
                                         oc * HALF + (sg_ + 1) * SW],
                                     ps8[:, 1024:1024 + SW], AF.Sigmoid,
                                     bias=cvec[biasname][:, oc:oc + 1])
                        r.mark("scalar", "z4s")

            emit_mix("mv"); emit_bigproj(wbig_d["Wv"], vf)
            emit_mix("mk"); emit_bigproj(wbig_d["Wk"], kf)
            emit_mix("mr"); emit_bigproj(wbig_d["Wr"], rf)
            emit_mix("mw")
            emit_smallpath(w1s, w2s, "w0", lwf, 64)
            nz = r.val("z4s")
            v.wait_ge(r.sem("z4s"), nz)
            for c in range(8):
                v.tensor_scalar(lwf[:, c * HALF:(c + 1) * HALF],
                                lwf[:, c * HALF:(c + 1) * HALF],
                                -0.6065306597126334, None, OP.mult)
            r.mark("vector", "lwv")
            emit_mix("ma")
            emit_smallpath(a1s, a2s, "a0", agf, 64)
            emit_mix("mg")
            nmx = r.val("mxv")
            t.wait_ge(r.sem("mxv"), nmx)
            for sg_ in range(SUP):
                t.wait_ge(r.sem("g2s2"), r.val("g2s2"))
                for mc in range(8):
                    t.matmul(ps8[:, 3072:3072 + SW], g1s[:, mc * 128:(mc + 1) * 128],
                             mixb[:, mc * HALF + sg_ * SW:mc * HALF + (sg_ + 1) * SW],
                             start=(mc == 0), stop=(mc == 7))
                ng1 = r.mark("tensor", "g1t")
                s.wait_ge(r.sem("g1t"), ng1)
                s.activation(tscr2[:, 0:SW], ps8[:, 3072:3072 + SW], AF.Sigmoid)
                ng2 = r.mark("scalar", "g2s2")
                t.wait_ge(r.sem("g2s2"), ng2)
                for oc in range(8):
                    if oc > 0 or sg_ > 0:
                        t.wait_ge(r.sem("g4v"), r.val("g4v"))
                    t.matmul(ps8[:, 1024:1024 + SW], g2s[:, oc * 128:(oc + 1) * 128],
                             tscr2[:, 0:SW], start=True, stop=True)
                    ng3 = r.mark("tensor", "g3t")
                    v.wait_ge(r.sem("g3t"), ng3)
                    v.wait_ge(r.sem("gst"), 16 * r.val("gst"))
                    v.tensor_copy(tscr[:, 0:SW], ps8[:, 1024:1024 + SW])
                    ng4 = r.mark("vector", "g4v")
                    g.wait_ge(r.sem("g4v"), ng4)
                    r.dma("gpsimd", "gst",
                          gfm_o[oc * 128:(oc + 1) * 128,
                                tb + sg_ * SW:tb + (sg_ + 1) * SW], tscr[:, 0:SW])
            npj_all = r.val("pjv")
            v.wait_ge(r.sem("pjv"), npj_all)
            s.wait_ge(r.sem("gst"), 16 * r.val("gst"))
            for oc in range(8):
                for sg_ in range(SUP):
                    sl = slice(oc * HALF + sg_ * SW, oc * HALF + (sg_ + 1) * SW)
                    v.wait_ge(r.sem("kkt"), r.val("kkt"))
                    v.tensor_scalar(kkn[:, sl], kf[:, sl],
                                    cvec["k_k"][:, oc:oc + 1], None, OP.mult)
                    nkk = r.mark("vector", "kkv")
                    s.wait_ge(r.sem("kkv"), nkk)
                    s.activation(tscr[:, 0:SW], kkn[:, sl], AF.Square)
                    nsq = r.mark("scalar", "kks")
                    t.wait_ge(r.sem("kks"), nsq)
                    t.matmul(ps8[0:2, 1024:1024 + SW], bones[:], tscr[:, 0:SW],
                             start=True, stop=True)
                    nbm = r.mark("tensor", "kkt")
                    s.wait_ge(r.sem("kkt"), nbm)
                    s.activation(ps8[0:2, 2048:2048 + SW], ps8[0:2, 1024:1024 + SW],
                                 AF.Sqrt)
                    nsr = r.mark("scalar", "kks")
                    v.wait_ge(r.sem("kks"), nsr)
                    v.tensor_scalar(ps8[0:2, 2048:2048 + SW],
                                    ps8[0:2, 2048:2048 + SW], 1e-12, None, OP.max)
                    r.fence("vector")
                    v.reciprocal(nrmsc[0:2, 0:SW], ps8[0:2, 2048:2048 + SW])
                    nrc = r.mark("vector", "kkv")
                    t.wait_ge(r.sem("kkv"), nrc)
                    t.matmul(ps8[:, 1024:1024 + SW], sel2[:], nrmsc[0:2, 0:SW],
                             start=True, stop=True)
                    nbc = r.mark("tensor", "kkt")
                    v.wait_ge(r.sem("kkt"), nbc)
                    v.tensor_tensor(kkn[:, sl], kkn[:, sl], ps8[:, 1024:1024 + SW],
                                    OP.mult)
                    r.mark("vector", "kkv")
            r.fence("vector")
            for oc in range(8):
                for sg_ in range(SUP):
                    sl = slice(oc * HALF + sg_ * SW, oc * HALF + (sg_ + 1) * SW)
                    v.tensor_scalar(tscr[:, 0:SW], agf[:, sl], -1.0,
                                    cvec["k_a"][:, oc:oc + 1], OP.add, OP.mult)
                    r.fence("vector")
                    v.tensor_scalar(tscr[:, 0:SW], tscr[:, 0:SW], 1.0, None, OP.add)
                    r.fence("vector")
                    v.tensor_tensor(kf[:, sl], kf[:, sl], tscr[:, 0:SW], OP.mult)
            r.mark("vector", "kev")
            r.fence("vector")
            v.wait_ge(r.sem("kkt"), r.val("kkt"))
            for oc in range(8):
                for sg_ in range(SUP):
                    sl = slice(oc * HALF + sg_ * SW, oc * HALF + (sg_ + 1) * SW)
                    v.wait_ge(r.sem("rtt"), r.val("rtt"))
                    v.wait_ge(r.sem("rst"), 16 * r.val("rst"))
                    v.tensor_tensor(tscr[:, 0:SW], rf[:, sl], kf[:, sl], OP.mult)
                    r.fence("vector")
                    v.tensor_scalar(tscr[:, 0:SW], tscr[:, 0:SW],
                                    cvec["rk"][:, oc:oc + 1], None, OP.mult)
                    nrt = r.mark("vector", "rtv")
                    t.wait_ge(r.sem("rtv"), nrt)
                    t.matmul(ps8[0:2, 1024:1024 + SW], bones[:], tscr[:, 0:SW],
                             start=True, stop=True)
                    nrm_ = r.mark("tensor", "rtt")
                    v.wait_ge(r.sem("rtt"), nrm_)
                    v.tensor_copy(nrmsc[0:2, 0:SW], ps8[0:2, 1024:1024 + SW])
                    nrv = r.mark("vector", "rtv")
                    t.wait_ge(r.sem("rtv"), nrv)
                    t.matmul(ps8[:, 1024:1024 + SW], sel2[:], nrmsc[0:2, 0:SW],
                             start=True, stop=True)
                    nbc = r.mark("tensor", "rtt")
                    v.wait_ge(r.sem("rtt"), nbc)
                    v.wait_ge(r.sem("rst"), 16 * r.val("rst"))
                    v.tensor_tensor(tscr[:, 0:SW], vf[:, sl], ps8[:, 1024:1024 + SW],
                                    OP.mult)
                    nrv = r.mark("vector", "rtv")
                    g.wait_ge(r.sem("rtv"), nrv)
                    r.dma("gpsimd", "rst",
                          rtfm_o[oc * 128:(oc + 1) * 128,
                                 tb + sg_ * SW:tb + (sg_ + 1) * SW], tscr[:, 0:SW])

            if debug and hf == 0:
                v.wait_ge(r.sem("rtv"), r.val("rtv"))
                g.wait_ge(r.sem("rtv"), r.val("rtv"))
                g.wait_ge(r.sem("lwv"), r.val("lwv"))
                for nm, tens in (("rf", rf), ("kf", kf), ("vf", vf), ("agf", agf),
                                 ("lwf", lwf), ("kkn", kkn)):
                    r.dma("gpsimd", "dbg", dbg_o[nm][:], tens[:])
                r.dma("gpsimd", "dbg", dbg_o["lnf"][:, 0:8 * HALF],
                      lnf[:].rearrange("p (c t) -> p c t", c=8)[:, :, 16:HALF + 16])
                g.wait_ge(r.sem("dbg"), 16 * r.val("dbg"))
                v.wait_ge(r.sem("dbg"), 16 * r.val("dbg"))
                s.wait_ge(r.sem("dbg"), 16 * r.val("dbg"))

            v.wait_ge(r.sem("rtv"), r.val("rtv"))
            v.wait_ge(r.sem("lwv"), r.val("lwv"))
            v.wait_ge(r.sem("rst"), 16 * r.val("rst"))
            v.memset(zer128[:], 0.0)
            r.fence("vector")
            for oc in range(8):
                for c in range(CH_H):
                    v.tensor_tensor_scan(
                        ps8[:, c * 128:(c + 1) * 128],
                        lwf[:, oc * HALF + c * 128:oc * HALF + (c + 1) * 128],
                        zer128[:], 0.0, OP.add, OP.add)
                nsc = r.mark("vector", "dsv")
                for c in range(CH_H):
                    v.tensor_copy(wlh[:, oc * CH_H + c:oc * CH_H + c + 1],
                                  ps8[:, c * 128 + 127:c * 128 + 128])
                nwl = r.mark("vector", "dsv")
                s.wait_ge(r.sem("dsv"), nwl)
                s.activation(wlh[:, oc * CH_H:(oc + 1) * CH_H],
                             wlh[:, oc * CH_H:(oc + 1) * CH_H], AF.Exp)
                s.activation(ps8[:, 1024:1024 + HALF], ps8[:, 0:HALF], AF.Exp)
                s.activation(ps8[:, 2048:2048 + HALF], ps8[:, 0:HALF], AF.Exp,
                             scale=-1.0)
                nex = r.mark("scalar", "dss")
                for sg_ in range(SUP):
                    s.activation((tscr2 if sg_ == 0 else tscr)[:, 0:SW],
                                 lwf[:, oc * HALF + sg_ * SW:
                                     oc * HALF + (sg_ + 1) * SW],
                                 AF.Exp, scale=-1.0)
                nem = r.mark("scalar", "dss")
                v.wait_ge(r.sem("dss"), nem)
                sl = slice(oc * HALF, (oc + 1) * HALF)
                v.tensor_tensor(tf32[:, 0:128] if False else mixb[:, 0:HALF],
                                kkn[:, sl], agf[:, sl], OP.mult)
                r.fence("vector")
                v.tensor_tensor(kkn[:, sl], kkn[:, sl], ps8[:, 1024:1024 + HALF],
                                OP.mult)
                r.fence("vector")
                for sg_ in range(SUP):
                    v.tensor_tensor(kkn[:, oc * HALF + sg_ * SW:
                                        oc * HALF + (sg_ + 1) * SW],
                                    kkn[:, oc * HALF + sg_ * SW:
                                        oc * HALF + (sg_ + 1) * SW],
                                    (tscr2 if sg_ == 0 else tscr)[:, 0:SW], OP.mult)
                r.fence("vector")
                v.tensor_scalar(kkn[:, sl], kkn[:, sl], -1.0, None, OP.mult)
                v.tensor_tensor(agf[:, sl], mixb[:, 0:HALF],
                                ps8[:, 2048:2048 + HALF], OP.mult)
                v.tensor_tensor(rf[:, sl], rf[:, sl], ps8[:, 1024:1024 + HALF],
                                OP.mult)
                v.tensor_tensor(kf[:, sl], kf[:, sl], ps8[:, 2048:2048 + HALF],
                                OP.mult)
                r.mark("vector", "dsv")
            ndv = r.val("dsv")

            if debug and hf == 0:
                v.wait_ge(r.sem("dsv"), ndv)
                g.wait_ge(r.sem("dsv"), ndv)
                for nm, tens in (("dRf", Rf), ("dKf", Kf), ("dBf", Bf), ("dAf", Af)):
                    r.dma("gpsimd", "dbg", dbg_o[nm][:], tens[:])
                g.wait_ge(r.sem("dbg"), 16 * r.val("dbg"))
                t.wait_ge(r.sem("dbg"), 16 * r.val("dbg"))

            for c in range(CH_H):
                for gi in range(RH // 2):
                    infos = []
                    for h in (2 * gi, 2 * gi + 1):
                        pi = b1_pi
                        b1_pi += 1
                        p = pi % 2
                        oc = h // 2
                        b0 = (h % 2) * 64
                        csl = slice(oc * HALF + c * 128, oc * HALF + (c + 1) * 128)
                        infos.append({
                            "pi": pi, "p": p, "h": h, "oc": oc, "b0": b0,
                            "SB": (SBb0 if p == 0 else SBb1),
                            "Tp": (Tpb0 if p == 0 else Tpb1),
                            "b": p * 1536,
                            "Asl": Af[b0:b0 + 64, csl], "Bsl": Bf[b0:b0 + 64, csl],
                            "Ksl": Kf[b0:b0 + 64, csl], "Rsl": Rf[b0:b0 + 64, csl],
                            "Vsl": vf[b0:b0 + 64, csl], "tp": (b0, 0),
                            "t_s": {}, "v_sx": {}, "v_mc": {}})
                    for nf in infos:
                        b = nf["b"]
                        t.wait_ge(r.sem("dsv"), ndv)
                        if nf["pi"] >= 2:
                            t.wait_ge(r.sem("b1v"), b1_state[nf["pi"] - 2])
                        tp = nf["tp"]
                        t.matmul(ps8[:, b:b + 128], nf["Asl"], nf["Bsl"],
                                 start=True, stop=True, tile_position=tp)
                        t.matmul(ps8[:, b + 128:b + 256], nf["Bsl"], nf["Asl"],
                                 start=True, stop=True, tile_position=tp)
                        t.matmul(ps8[:, b + 256:b + 384], nf["Ksl"], nf["Asl"],
                                 start=True, stop=True, tile_position=tp)
                        t.matmul(ps8[:, b + 384:b + 512], nf["Bsl"], nf["Rsl"],
                                 start=True, stop=True, tile_position=tp)
                        t.matmul(ps8[:, b + 512:b + 640], nf["Ksl"], nf["Rsl"],
                                 start=True, stop=True, tile_position=tp)
                        nf["t_build"] = r.mark("tensor", "b1t")
                    for nf in infos:
                        b, SB = nf["b"], nf["SB"]
                        v.wait_ge(r.sem("b1t"), nf["t_build"])
                        v.tensor_tensor(SB[:, 0:384],
                                        ps8[:, b + 256:b + 640],
                                        bmask_s[:, 0:384], OP.mult)
                        v.tensor_tensor(SB[:, 512:768],
                                        ps8[:, b + 0:b + 256],
                                        bmask_s[:, 384:640], OP.mult)
                        nf["v_evac"] = r.mark("vector", "b1v")
                    for nf in infos:
                        idb = ident_s[nf["b0"]:nf["b0"] + 64, nf["b0"]:nf["b0"] + 64]
                        t.wait_ge(r.sem("b1v"), nf["v_evac"])
                        pT = psT[:, nf["p"] * 256:(nf["p"] + 1) * 256]
                        t.transpose(pT[:, 0:64], nf["Asl"], idb)
                        t.transpose(pT[:, 64:128], nf["Vsl"], idb)
                        t.transpose(pT[:, 128:192], nf["Bsl"], idb)
                        t.transpose(pT[:, 192:256], nf["Ksl"], idb)
                        nf["t_tr"] = r.mark("tensor", "b1t")
                    for nf in infos:
                        Tp = nf["Tp"]
                        v.wait_ge(r.sem("b1t"), nf["t_tr"])
                        v.tensor_copy(Tp[:], psT[:, nf["p"] * 256:(nf["p"] + 1) * 256])
                        nf["v_tp"] = r.mark("vector", "b1v")
                    for nf in infos:
                        b, SB, Tp = nf["b"], nf["SB"], nf["Tp"]
                        t.wait_ge(r.sem("b1v"), nf["v_tp"])
                        t.matmul(ps8[:, b + 640:b + 704], SB[:, 0:128],
                                 Tp[:, 64:128], start=True, stop=True)
                        nf["t_nv"] = r.mark("tensor", "b1t")
                    for nf in infos:
                        b, SB, Tp = nf["b"], nf["SB"], nf["Tp"]
                        v.wait_ge(r.sem("b1t"), nf["t_nv"])
                        v.tensor_copy(SB[:, 384:448], Tp[:, 0:64])
                        v.tensor_copy(SB[:, 448:512], ps8[:, b + 640:b + 704])
                        nf["v_x"] = r.mark("vector", "b1v")
                    for k in range(7):
                        for nf in infos:
                            b, SB = nf["b"], nf["SB"]
                            if k == 0:
                                t.wait_ge(r.sem("b1v"), nf["v_x"])
                            else:
                                t.wait_ge(r.sem("b1v"), nf["v_mc"][k - 1])
                            w = 256 if k < 6 else 128
                            t.matmul(ps8[:, b + 512:b + 512 + w],
                                     SB[:, 640:768], SB[:, 384:384 + w],
                                     start=True, stop=True)
                            if k < 6:
                                t.matmul(ps8[:, b + 768:b + 896],
                                         SB[:, 512:640], SB[:, 640:768],
                                         start=True, stop=True)
                            nf["t_s"][k] = r.mark("tensor", "b1t")
                        for nf in infos:
                            b, SB = nf["b"], nf["SB"]
                            v.wait_ge(r.sem("b1t"), nf["t_s"][k])
                            v.tensor_tensor(SB[:, 384:512], SB[:, 384:512],
                                            ps8[:, b + 512:b + 640], OP.add)
                            if k < 6:
                                v.tensor_copy(SB[:, 512:768],
                                              ps8[:, b + 640:b + 896])
                            nf["v_mc"][k] = r.mark("vector", "b1v")
                    for nf in infos:
                        b, SB, Tp, tp = nf["b"], nf["SB"], nf["Tp"], nf["tp"]
                        t.wait_ge(r.sem("b1v"), nf["v_mc"][6])
                        t.matmul(ps8[0:64, b + 896:b + 1024], SB[:, 448:512],
                                 SB[:, 128:256], start=True, stop=False)
                        t.matmul(ps8[0:64, b + 896:b + 1024], Tp[:, 64:128],
                                 SB[:, 256:384], start=False, stop=True)
                        t.matmul(ps8[0:64, b + 1024:b + 1152], SB[:, 384:448],
                                 SB[:, 128:256], start=True, stop=False)
                        t.matmul(ps8[0:64, b + 1024:b + 1152],
                                 ident_s[nf["b0"]:nf["b0"] + 64,
                                         nf["b0"]:nf["b0"] + 64], nf["Rsl"],
                                 start=False, stop=True, tile_position=tp)
                        t.matmul(ps8[0:64, b + 1152:b + 1216], SB[:, 384:448],
                                 Tp[:, 128:192], start=True, stop=True)
                        t.matmul(ps8[0:64, b + 1216:b + 1280], Tp[:, 128:192],
                                 SB[:, 448:512], start=True, stop=False)
                        t.matmul(ps8[0:64, b + 1216:b + 1280], Tp[:, 192:256],
                                 Tp[:, 64:128], start=False, stop=True)
                        nf["t_out"] = r.mark("tensor", "b1t")
                    for nf in infos:
                        b, h = nf["b"], nf["h"]
                        v.wait_ge(r.sem("b1t"), nf["t_out"])
                        v.wait_ge(r.sem("blb"), blb_gate)
                        v.tensor_tensor(outP[:, h * 64:(h + 1) * 64],
                                        ps8[0:64, b + 1152:b + 1216], eye64[:],
                                        OP.add)
                        v.tensor_copy(outY[:, h * 128:(h + 1) * 128],
                                      ps8[0:64, b + 896:b + 1024])
                        v.tensor_copy(outR[:, h * 128:(h + 1) * 128],
                                      ps8[0:64, b + 1024:b + 1152])
                        v.tensor_copy(outZ[:, h * 64:(h + 1) * 64],
                                      ps8[0:64, b + 1216:b + 1280])
                        nf["v_op"] = r.mark("vector", "b1v")
                    for nf in infos:
                        b, h, p = nf["b"], nf["h"], nf["p"]
                        oc, b0 = nf["oc"], nf["b0"]
                        t.wait_ge(r.sem("b1v"), nf["v_op"])
                        t.wait_ge(r.sem("b1v"), last_vsp[0])
                        if h in span_cnt:
                            t.wait_ge(r.sem("b1v"), span_cnt[h])
                        t.matmul(ps8[0:64, 1280:1344],
                                 outP[:, h * 64:(h + 1) * 64],
                                 spanP[:, h * 64:(h + 1) * 64],
                                 start=True, stop=True)
                        t.matmul(ps8[0:64, 1344:1408],
                                 outP[:, h * 64:(h + 1) * 64],
                                 spanQ[:, h * 64:(h + 1) * 64],
                                 start=True, stop=False)
                        t.matmul(ps8[0:64, 1344:1408], eye64[:],
                                 outZ[:, h * 64:(h + 1) * 64],
                                 start=False, stop=True)
                        nf["t_sp"] = r.mark("tensor", "b1t")
                        wlcol = wlh[b0:b0 + 64, oc * CH_H + c:oc * CH_H + c + 1]
                        v.wait_ge(r.sem("b1t"), nf["t_sp"])
                        v.tensor_scalar(spanP[:, h * 64:(h + 1) * 64],
                                        ps8[0:64, 1280:1344], wlcol,
                                        None, OP.mult)
                        v.tensor_scalar(spanQ[:, h * 64:(h + 1) * 64],
                                        ps8[0:64, 1344:1408], wlcol,
                                        None, OP.mult)
                        nf["v_sp"] = r.mark("vector", "b1v")
                        span_cnt[h] = nf["v_sp"]
                        last_vsp[0] = nf["v_sp"]
                        b1_state[nf["pi"]] = nf["v_sp"]
                    if debug and hf == 0 and c == 0 and gi == 0:
                        g.wait_ge(r.sem("b1v"), r.val("b1v"))
                        r.dma("gpsimd", "dbg", dbg2_o[:, 0:256], SBb0[:, 512:768])
                        r.dma("gpsimd", "dbg", dbg2_o[:, 256:384], SBb0[:, 0:128])
                        r.dma("gpsimd", "dbg", dbg2_o[:, 384:640], SBb0[:, 128:384])
                        r.dma("gpsimd", "dbg", dbg2_o[:, 640:896], Tpb0[:])
                        r.dma("gpsimd", "dbg", dbg2_o[:, 896:1024], SBb0[:, 384:512])
                        r.dma("gpsimd", "dbg", dbg3_o[:, 0:64], outP[:, 0:64])
                        g.wait_ge(r.sem("dbg"), 16 * r.val("dbg"))
                        t.wait_ge(r.sem("dbg"), 16 * r.val("dbg"))
                        v.wait_ge(r.sem("dbg"), 16 * r.val("dbg"))
                        s.wait_ge(r.sem("dbg"), 16 * r.val("dbg"))
                nbv = r.val("b1v")
                g.wait_ge(r.sem("b1v"), nbv)
                r.dma("gpsimd", "blb",
                      rhat_o[gc * RH * 64:(gc + 1) * RH * 64, :].rearrange(
                          "(h p) t -> p h t", p=64),
                      outR[:].rearrange("p (h t) -> p h t", h=RH))
                r.dma("gpsimd", "blb",
                      yloc_o[gc * RH * 64:(gc + 1) * RH * 64, :].rearrange(
                          "(h p) t -> p h t", p=64),
                      outY[:].rearrange("p (h t) -> p h t", h=RH))
                r.dma("gpsimd", "blb",
                      phi_o[gc * RH * 64:(gc + 1) * RH * 64, :].rearrange(
                          "(h p) t -> p h t", p=64),
                      outP[:].rearrange("p (h t) -> p h t", h=RH))
                r.dma("gpsimd", "blb",
                      zp_o[gc * RH * 64:(gc + 1) * RH * 64, :].rearrange(
                          "(h p) t -> p h t", p=64),
                      outZ[:].rearrange("p (h t) -> p h t", h=RH))
                blb_gate = 16 * r.val("blb")
                gc += 1
            g.wait_ge(r.sem("dsv"), ndv)
            r.dma("gpsimd", "blb", wl_o[hf * 128:(hf + 1) * 128, :], wlh[:])

        g.wait_ge(r.sem("b1v"), r.val("b1v"))
        r.dma("gpsimd", "blb", spp_o[:].rearrange("(h p) n -> p h n", p=64),
              spanP[:].rearrange("p (h n) -> p h n", h=RH))
        r.dma("gpsimd", "blb", spq_o[:].rearrange("(h p) n -> p h n", p=64),
              spanQ[:].rearrange("p (h n) -> p h n", h=RH))
        g.wait_ge(r.sem("blb"), 16 * r.val("blb"))
        g.wait_ge(r.sem("gst"), 16 * r.val("gst"))
        g.wait_ge(r.sem("rst"), 16 * r.val("rst"))
        r.flush()
    return nc




def build_l6(T_loc, debug=False):
    NCH = T_loc // RCHUNK
    HALF = min(T_loc, 1024)
    NHALF = T_loc // HALF
    CH_H = HALF // RCHUNK
    TS = min(512, T_loc)
    NT = T_loc // TS
    nc = bass.Bass("TRN2", target_bir_lowering=False)
    yl_d = nc.dram_tensor("ylblob", [NCH * RH * 64, 128], BF16,
                          kind="ExternalInput").ap()
    rh_d = nc.dram_tensor("rhblob", [NCH * RH * 64, 128], BF16,
                          kind="ExternalInput").ap()
    phi_d = nc.dram_tensor("phiblob", [NCH * RH * 64, 64], F32,
                           kind="ExternalInput").ap()
    zp_d = nc.dram_tensor("zpblob", [NCH * RH * 64, 64], F32,
                          kind="ExternalInput").ap()
    wl_d = nc.dram_tensor("wlblob", [NHALF * 128, 8 * CH_H], F32,
                          kind="ExternalInput").ap()
    gfm_d = nc.dram_tensor("gfm", [M, T_loc], BF16, kind="ExternalInput").ap()
    rtfm_d = nc.dram_tensor("rtfm", [M, T_loc], BF16, kind="ExternalInput").ap()
    sppt_d = nc.dram_tensor("spPT", [7 * RH * 64, 64], F32,
                            kind="ExternalInput").ap()
    spq_d = nc.dram_tensor("spQ", [7 * RH * 64, 64], F32,
                           kind="ExternalInput").ap()
    lnw_d = nc.dram_tensor("lnw_fm", [128, 8], F32, kind="ExternalInput").ap()
    lnb_d = nc.dram_tensor("lnb_fm", [128, 8], F32, kind="ExternalInput").ap()
    eps_d = nc.dram_tensor("epsgn", [128, 1], F32, kind="ExternalInput").ap()
    bones_d = nc.dram_tensor("blockones64", [128, 2], BF16,
                             kind="ExternalInput").ap()
    sel2_d = nc.dram_tensor("sel2", [2, 128], BF16, kind="ExternalInput").ap()
    yout_d = nc.dram_tensor("yout", [M, T_loc], BF16, kind="ExternalOutput").ap()
    if debug:
        dys_o = nc.dram_tensor("dbg_ysb", [128, 8 * T_loc], BF16,
                               kind="ExternalOutput").ap()
        ds0_o = nc.dram_tensor("dbg_s0", [64, RH * 64], F32,
                               kind="ExternalOutput").ap()
        ds1_o = nc.dram_tensor("dbg_s1", [64, RH * 64], F32,
                               kind="ExternalOutput").ap()
        dgn_o = nc.dram_tensor("dbg_gn", [2, 4 * 512], F32,
                               kind="ExternalOutput").ap()
        dtg_o = nc.dram_tensor("dbg_tg", [128, 2 * 512], F32,
                               kind="ExternalOutput").ap()

    with ExitStack() as es:
        r = Rec(nc, es)
        t, s, v, g = r.tensor, r.scalar, r.vector, r.gpsimd
        yl_s = r.sbuf("yl_s", [64, 2 * RH * 128], BF16)
        rh_s = r.sbuf("rh_s", [64, 2 * RH * 128], BF16)
        phi_s = r.sbuf("phi_s", [64, 2 * RH * 64], F32)
        zp_s = r.sbuf("zp_s", [64, 2 * RH * 64], F32)
        wl_s = r.sbuf("wl_s", [128, NHALF * 8 * CH_H], F32)
        sppt_s = r.sbuf("sppt_s", [64, 7 * RH * 64], F32)
        spq_s = r.sbuf("spq_s", [64, 7 * RH * 64], F32)
        Ssb = r.sbuf("Ssb", [64, RH * 64], F32)
        Sbf = r.sbuf("Sbf", [64, RH * 64], BF16)
        ysb = r.sbuf("ysb", [128, 8 * T_loc], BF16)
        lnw_s = r.sbuf("lnw_s", [128, 8], F32)
        lnb_s = r.sbuf("lnb_s", [128, 8], F32)
        eps_s = r.sbuf("eps_s6", [128, 1], F32)
        bones = r.sbuf("bones6", [128, 2], BF16)
        sel2s = r.sbuf("sel2_6", [2, 128], BF16)
        sqs = r.sbuf("sqs", [128, TS], BF16)
        gn1 = r.sbuf("gn1", [2, TS], F32)
        gn2 = r.sbuf("gn2", [2, TS], F32)
        gn3 = r.sbuf("gn3", [2, TS], F32)
        nrm = r.sbuf("nrm6", [2, 2 * TS], BF16)
        tgn = r.sbuf("tgn", [128, TS], F32)
        tg2 = r.sbuf("tg2", [128, TS], F32)
        yo = r.sbuf("yo", [128, TS], BF16)
        gt = r.sbuf("gt", [128, 2 * TS], BF16)
        rtt = r.sbuf("rtt", [128, 2 * TS], BF16)
        psL = r.psum("psL", [128, 512])
        psG = r.psum("psG", [128, 2048])

        for dst, srcd in ((wl_s, None), ):
            pass
        r.dma("sync", "ld6", wl_s[:].rearrange("p (f c) -> p f c", f=NHALF),
              wl_d.rearrange("(f p) c -> p f c", p=128))
        r.dma("sync", "ld6", sppt_s[:].rearrange("p (q n) -> p q n", q=7 * RH),
              sppt_d.rearrange("(q p) n -> p q n", p=64))
        r.dma("sync", "ld6", spq_s[:].rearrange("p (q n) -> p q n", q=7 * RH),
              spq_d.rearrange("(q p) n -> p q n", p=64))
        for dst, srcd in ((lnw_s, lnw_d), (lnb_s, lnb_d), (eps_s, eps_d),
                          (bones, bones_d), (sel2s, sel2_d)):
            r.dma("sync", "ld6", dst[:], srcd[:])
        nld = 16 * r.val("ld6")

        def load_chunk(c):
            b = c % 2
            r.dma("gpsimd", "blL",
                  yl_s[:, b * RH * 128:(b + 1) * RH * 128].rearrange(
                      "p (h t) -> p h t", h=RH),
                  yl_d[c * RH * 64:(c + 1) * RH * 64, :].rearrange(
                      "(h p) t -> p h t", p=64))
            r.dma("gpsimd", "blL",
                  rh_s[:, b * RH * 128:(b + 1) * RH * 128].rearrange(
                      "p (h t) -> p h t", h=RH),
                  rh_d[c * RH * 64:(c + 1) * RH * 64, :].rearrange(
                      "(h p) t -> p h t", p=64))
            r.dma("gpsimd", "blL",
                  phi_s[:, b * RH * 64:(b + 1) * RH * 64].rearrange(
                      "p (h n) -> p h n", h=RH),
                  phi_d[c * RH * 64:(c + 1) * RH * 64, :].rearrange(
                      "(h p) n -> p h n", p=64))
            r.dma("gpsimd", "blL",
                  zp_s[:, b * RH * 64:(b + 1) * RH * 64].rearrange(
                      "p (h n) -> p h n", h=RH),
                  zp_d[c * RH * 64:(c + 1) * RH * 64, :].rearrange(
                      "(h p) n -> p h n", p=64))

        load_chunk(0)
        if NCH > 1:
            load_chunk(1)

        v.wait_ge(r.sem("ld6"), nld)
        v.memset(Ssb[:], 0.0)
        r.mark("vector", "yv")
        for j in range(7):
            for h in range(RH):
                q = j * RH + h
                reg = (q % 2) * 256 + 128
                t.wait_ge(r.sem("ld6"), nld)
                t.wait_ge(r.sem("yv"), r.val("yv"))
                t.matmul(psL[0:64, reg:reg + 64], sppt_s[:, q * 64:(q + 1) * 64],
                         Ssb[:, h * 64:(h + 1) * 64], start=True, stop=True)
                nmm = r.mark("tensor", "yt")
                v.wait_ge(r.sem("yt"), nmm)
                v.tensor_tensor(Ssb[:, h * 64:(h + 1) * 64],
                                psL[0:64, reg:reg + 64],
                                spq_s[:, q * 64:(q + 1) * 64], OP.add)
                r.mark("vector", "yv")

        if debug:
            g.wait_ge(r.sem("yv"), r.val("yv"))
            r.dma("gpsimd", "dbg6", ds0_o[:], Ssb[:])
            g.wait_ge(r.sem("dbg6"), 16 * r.val("dbg6"))
            t.wait_ge(r.sem("dbg6"), 16 * r.val("dbg6"))
            v.wait_ge(r.sem("dbg6"), 16 * r.val("dbg6"))

        chunk_vmark = {}
        for c in range(NCH):
            b = c % 2
            for h in range(RH):
                oc = h // 2
                b0 = (h % 2) * 64
                regY = (h % 2) * 256
                regA = (h % 2) * 256 + 128
                hsl = slice(b * RH * 128 + h * 128, b * RH * 128 + (h + 1) * 128)
                nsl = slice(b * RH * 64 + h * 64, b * RH * 64 + (h + 1) * 64)
                v.wait_ge(r.sem("blL"), 16 * 4 * (c + 1))
                v.wait_ge(r.sem("yt"), r.val("yt"))
                v.tensor_copy(Sbf[:, h * 64:(h + 1) * 64],
                              Ssb[:, h * 64:(h + 1) * 64])
                ncp = r.mark("vector", "yv")
                t.wait_ge(r.sem("yv"), ncp)
                t.wait_ge(r.sem("blL"), 16 * 4 * (c + 1))
                t.matmul(psL[0:64, regY:regY + 128],
                         Sbf[:, h * 64:(h + 1) * 64], rh_s[:, hsl],
                         start=True, stop=True)
                t.matmul(psL[0:64, regA:regA + 64],
                         phi_s[:, nsl], Ssb[:, h * 64:(h + 1) * 64],
                         start=True, stop=True)
                nmm = r.mark("tensor", "yt")
                v.wait_ge(r.sem("yt"), nmm)
                v.tensor_tensor(ysb[b0:b0 + 64,
                                    oc * T_loc + c * 128:oc * T_loc + (c + 1) * 128],
                                psL[0:64, regY:regY + 128], yl_s[:, hsl], OP.add)
                v.tensor_tensor(tgn[0:64, 0:64], psL[0:64, regA:regA + 64],
                                zp_s[:, nsl], OP.add)
                r.fence("vector")
                hf = c // CH_H
                col = hf * (8 * CH_H) + oc * CH_H + (c % CH_H)
                v.tensor_scalar(Ssb[:, h * 64:(h + 1) * 64], tgn[0:64, 0:64],
                                wl_s[b0:b0 + 64, col:col + 1], None, OP.mult)
                r.mark("vector", "yv")
            chunk_vmark[c] = r.val("yv")
            if debug and c == 0:
                g.wait_ge(r.sem("yv"), r.val("yv"))
                r.dma("gpsimd", "dbg6", ds1_o[:], Ssb[:])
                g.wait_ge(r.sem("dbg6"), 16 * r.val("dbg6"))
                t.wait_ge(r.sem("dbg6"), 16 * r.val("dbg6"))
                v.wait_ge(r.sem("dbg6"), 16 * r.val("dbg6"))
            if c + 2 < NCH:
                g.wait_ge(r.sem("yv"), chunk_vmark[c])
                load_chunk(c + 2)

        def load_gtile(oc, i):
            bb = (oc * NT + i) % 2
            r.dma("gpsimd", "gld", gt[:, bb * TS:(bb + 1) * TS],
                  gfm_d[oc * 128:(oc + 1) * 128, i * TS:(i + 1) * TS])
            r.dma("gpsimd", "gld", rtt[:, bb * TS:(bb + 1) * TS],
                  rtfm_d[oc * 128:(oc + 1) * 128, i * TS:(i + 1) * TS])

        g.wait_ge(r.sem("yv"), chunk_vmark[NCH - 1])
        load_gtile(0, 0)
        if NT * 8 > 1:
            noc1, ni1 = divmod(1, NT)
            load_gtile(noc1, ni1)
        it = 0
        for oc in range(8):
            for i in range(NT):
                bb = it % 2
                ysl = ysb[:, oc * T_loc + i * TS:oc * T_loc + (i + 1) * TS]
                s.wait_ge(r.sem("yv"), chunk_vmark[NCH - 1])
                s.wait_ge(r.sem("yt"), r.val("yt"))
                s.activation(sqs[:], ysl, AF.Square)
                nsq = r.mark("scalar", "ys")
                t.wait_ge(r.sem("ys"), nsq)
                t.wait_ge(r.sem("yv"), r.val("yv"))
                t.matmul(psG[0:2, 0:TS], bones[:], ysl, start=True, stop=True)
                t.matmul(psG[0:2, 512:512 + TS], bones[:], sqs[:],
                         start=True, stop=True)
                nmm = r.mark("tensor", "yt")
                v.wait_ge(r.sem("yt"), nmm)
                v.tensor_scalar(gn1[:], psG[0:2, 0:TS], 1.0 / 64, None, OP.mult)
                v.tensor_scalar(gn2[:], psG[0:2, 512:512 + TS], 1.0 / 64,
                                None, OP.mult)
                r.fence("vector")
                v.tensor_tensor(gn3[:], gn1[:], gn1[:], OP.mult)
                r.fence("vector")
                v.tensor_tensor(gn2[:], gn2[:], gn3[:], OP.subtract)
                ngv = r.mark("vector", "yv")
                s.wait_ge(r.sem("yv"), ngv)
                s.activation(gn2[:], gn2[:], AF.Sqrt, bias=eps_s[0:2, :])
                nsr = r.mark("scalar", "ys")
                v.wait_ge(r.sem("ys"), nsr)
                v.reciprocal(gn3[:], gn2[:])
                r.fence("vector")
                v.tensor_copy(nrm[:, 0:TS], gn3[:])
                v.tensor_copy(nrm[:, TS:2 * TS], gn1[:])
                nbv = r.mark("vector", "yv")
                t.wait_ge(r.sem("yv"), nbv)
                t.matmul(psG[:, 1024:1024 + TS], sel2s[:], nrm[0:2, TS:2 * TS],
                         start=True, stop=True)
                t.matmul(psG[:, 1536:1536 + TS], sel2s[:], nrm[0:2, 0:TS],
                         start=True, stop=True)
                nbm = r.mark("tensor", "yt")
                v.wait_ge(r.sem("yt"), nbm)
                v.tensor_tensor(tgn[:], ysl, psG[:, 1024:1024 + TS], OP.subtract)
                r.fence("vector")
                v.tensor_tensor(tgn[:], tgn[:], psG[:, 1536:1536 + TS], OP.mult)
                naf = r.mark("vector", "yv")
                s.wait_ge(r.sem("yv"), naf)
                s.activation(tg2[:], tgn[:], AF.Identity,
                             bias=lnb_s[:, oc:oc + 1], scale=lnw_s[:, oc:oc + 1])
                nid = r.mark("scalar", "ys")
                v.wait_ge(r.sem("ys"), nid)
                v.wait_ge(r.sem("gld"), 16 * 2 * (it + 1))
                v.tensor_tensor(tg2[:], tg2[:], rtt[:, bb * TS:(bb + 1) * TS],
                                OP.add)
                r.fence("vector")
                v.wait_ge(r.sem("yst"), 16 * r.val("yst"))
                v.tensor_tensor(yo[:], tg2[:], gt[:, bb * TS:(bb + 1) * TS],
                                OP.mult)
                nyo = r.mark("vector", "yv")
                if debug and it == 0:
                    g.wait_ge(r.sem("yv"), nyo)
                    r.dma("gpsimd", "dbg6", dgn_o[:, 0:TS], gn1[:])
                    r.dma("gpsimd", "dbg6", dgn_o[:, 512:512 + TS], gn2[:])
                    r.dma("gpsimd", "dbg6", dgn_o[:, 1024:1024 + TS], gn3[:])
                    r.dma("gpsimd", "dbg6", dgn_o[:, 1536:1536 + TS],
                          nrm[:, 0:TS])
                    r.dma("gpsimd", "dbg6", dtg_o[:, 0:TS], tgn[:])
                    r.dma("gpsimd", "dbg6", dtg_o[:, 512:512 + TS], tg2[:])
                    g.wait_ge(r.sem("dbg6"), 16 * r.val("dbg6"))
                    v.wait_ge(r.sem("dbg6"), 16 * r.val("dbg6"))
                    s.wait_ge(r.sem("dbg6"), 16 * r.val("dbg6"))
                    t.wait_ge(r.sem("dbg6"), 16 * r.val("dbg6"))
                g.wait_ge(r.sem("yv"), nyo)
                r.dma("gpsimd", "yst", yout_d[oc * 128:(oc + 1) * 128,
                                              i * TS:(i + 1) * TS], yo[:])
                it += 1
                nxt = it + 1
                if nxt < 8 * NT:
                    noc, ni = divmod(nxt, NT)
                    g.wait_ge(r.sem("yv"), r.val("yv"))
                    load_gtile(noc, ni)
        if debug:
            g.wait_ge(r.sem("yv"), r.val("yv"))
            r.dma("gpsimd", "dbg6", dys_o[:], ysb[:])
        g.wait_ge(r.sem("yst"), 16 * r.val("yst"))
        g.wait_ge(r.sem("dbg6"), 16 * r.val("dbg6"))
        r.flush()
    return nc




def _fmvec(vec):
    return np.ascontiguousarray(np.asarray(vec, np.float32).reshape(8, 128).T)


def _moe_consts():
    tri1 = np.triu(np.ones((128, 128), np.float32), 1).astype(BF)
    trl1 = np.tril(np.ones((128, 128), np.float32), -1).astype(BF)
    tri0 = np.triu(np.ones((128, 128), np.float32), 0).astype(BF)
    t1_ = np.triu(np.ones((128, 128), np.float32), 1)
    t0_ = np.triu(np.ones((128, 128), np.float32), 0)
    l1_ = np.tril(np.ones((128, 128), np.float32), -1)
    bmask = np.concatenate([t1_, t0_, t0_, l1_, t1_], 1).astype(BF)
    bones = np.zeros((128, 2), np.float32)
    bones[0:64, 0] = 1.0
    bones[64:128, 1] = 1.0
    sel2 = np.zeros((2, 128), np.float32)
    sel2[0, 0:64] = 1.0
    sel2[1, 64:128] = 1.0
    return tri1, trl1, tri0, bones.astype(BF), sel2.astype(BF), bmask


def _run_moe(nc3, nc3b, nc4, ofm_per_core, wao, x_in, wg, w1e, w2e, consts,
             T, Tl, cores):
    ims = []
    for c in cores:
        im = {"ofm": ofm_per_core[c], "wao": wao,
              "x": np.ascontiguousarray(x_in[c * Tl:(c + 1) * Tl]), "wg": wg}
        for k in ("epsln", "triu_f32", "ones_row_f32", "sel127", "iota512",
                  "iota8", "triu16_f32", "ones16_f32"):
            im[k] = consts[k]
        ims.append(im)
    res3 = run_bass_kernel_spmd(nc3, ims, core_ids=cores)
    x1 = np.concatenate([np.asarray(res3.results[c]["x1"]) for c in cores], 0)
    meta = np.concatenate([np.asarray(res3.results[c]["meta"]) for c in cores], 0)
    cnts = np.concatenate([np.asarray(res3.results[c]["cnt"]) for c in cores], 0)

    ims_b = []
    for e in cores:
        xsel_e = np.concatenate(
            [np.asarray(res3.results[c]["xsel"])[e * M:(e + 1) * M]
             for c in cores], axis=1)
        ims_b.append({"xsel": np.ascontiguousarray(xsel_e),
                      "w1": np.ascontiguousarray(w1e[e]),
                      "w2": np.ascontiguousarray(w2e[e])})
    res3b = run_bass_kernel_spmd(nc3b, ims_b, core_ids=cores)

    ims_4 = []
    for c in cores:
        eo_c = np.concatenate(
            [np.asarray(res3b.results[e]["eo"])[c * P_LOC:(c + 1) * P_LOC]
             for e in cores], axis=0)
        pm = np.zeros((N_CORES, 1), np.float32)
        pm[:c] = 1.0
        im = {"x1": np.ascontiguousarray(x1[c * Tl:(c + 1) * Tl]),
              "eo": np.ascontiguousarray(eo_c),
              "meta": np.ascontiguousarray(meta[c * Tl:(c + 1) * Tl]),
              "cnts": cnts, "prevmask": pm}
        for k in ("ident_f32", "ones_row_f32", "iota8", "slotid"):
            im[k] = consts[k]
        ims_4.append(im)
    res4 = run_bass_kernel_spmd(nc4, ims_4, core_ids=cores)
    return np.concatenate([np.asarray(res4.results[c]["x2"]) for c in cores], 0)


def kernel(**inputs):
    x = np.asarray(inputs["L0_attn_a_x"], np.float32)
    _, T, _ = x.shape
    Tl = T // N_CORES
    cores = list(range(N_CORES))
    x0 = np.ascontiguousarray(x.reshape(T, M))
    consts = host_consts(T)
    tri1, trl1, tri0, bones_bf, sel2_bf, bmask_bf = _moe_consts()

    w_aq = np.asarray(inputs["L0_attn_a_w_aq"], np.float32)
    w_ak = np.asarray(inputs["L0_attn_a_w_ak"], np.float32)
    w_av = np.asarray(inputs["L0_attn_a_w_av"], np.float32)
    w_ao = np.asarray(inputs["L0_attn_a_w_ao"], np.float32)
    wg = np.asarray(inputs["L0_ffn_wg"], np.float32)
    w1e = np.asarray(inputs["L0_ffn_w1"], np.float32)
    w2e = np.asarray(inputs["L0_ffn_w2"], np.float32)

    nc1 = build_l1(Tl)
    ims = [{"x": np.ascontiguousarray(x0[c * Tl:(c + 1) * Tl]),
            "epsln": consts["epsln"]} for c in cores]
    res1 = run_bass_kernel_spmd(nc1, ims, core_ids=cores)
    ln1 = np.concatenate([np.asarray(res1.results[c]["ln1"]) for c in cores], 0)

    ln1fm = np.ascontiguousarray(ln1.T)
    nc2 = build_l2(T)
    ims = []
    for c in cores:
        im = {"ln1fm": ln1fm,
              "wq": np.ascontiguousarray(w_aq[:, c]),
              "wk": np.ascontiguousarray(w_ak[:, c]),
              "wv": np.ascontiguousarray(w_av[:, c])}
        for k in ("ropeC", "ropeS", "permT", "cmask0", "cmask1", "cmask2",
                  "cmask3", "ones_col", "ones_row_f32", "ident"):
            im[k] = consts[k]
        ims.append(im)
    res2 = run_bass_kernel_spmd(nc2, ims, core_ids=cores)
    ofm = np.concatenate([np.asarray(res2.results[c]["o"]) for c in cores], 0)

    nc3 = build_l3(Tl)
    nc3b = build_l3b()
    nc4 = build_l4(Tl, T)
    wao1 = np.ascontiguousarray(w_ao.reshape(NH * DH, M))
    ofm_pc = [np.ascontiguousarray(ofm[:, c * Tl:(c + 1) * Tl]) for c in cores]
    x2 = _run_moe(nc3, nc3b, nc4, ofm_pc, wao1, x0, wg, w1e, w2e, consts,
                  T, Tl, cores)

    Wr = np.asarray(inputs["L0_attn_b_Wr"], np.float32)
    Wk = np.asarray(inputs["L0_attn_b_Wk"], np.float32)
    Wv = np.asarray(inputs["L0_attn_b_Wv"], np.float32)
    Wo = np.asarray(inputs["L0_attn_b_Wo"], np.float32)
    rw1 = np.asarray(inputs["L0_attn_b_w1"], np.float32)
    rw2 = np.asarray(inputs["L0_attn_b_w2"], np.float32)
    ra1 = np.asarray(inputs["L0_attn_b_a1"], np.float32)
    ra2 = np.asarray(inputs["L0_attn_b_a2"], np.float32)
    rg1 = np.asarray(inputs["L0_attn_b_g1"], np.float32)
    rg2 = np.asarray(inputs["L0_attn_b_g2"], np.float32)
    cvs = {"cv_w0": _fmvec(inputs["L0_attn_b_w0"]),
           "cv_a0": _fmvec(inputs["L0_attn_b_a0"]),
           "cv_k_k": _fmvec(inputs["L0_attn_b_k_k"]),
           "cv_k_a": _fmvec(inputs["L0_attn_b_k_a"]),
           "cv_rk": _fmvec(inputs["L0_attn_b_r_k"]),
           "cv_mr": _fmvec(inputs["L0_attn_b_x_r"]),
           "cv_mw": _fmvec(inputs["L0_attn_b_x_w"]),
           "cv_mk": _fmvec(inputs["L0_attn_b_x_k"]),
           "cv_mv": _fmvec(inputs["L0_attn_b_x_v"]),
           "cv_ma": _fmvec(inputs["L0_attn_b_x_a"]),
           "cv_mg": _fmvec(inputs["L0_attn_b_x_g"])}
    nc5 = build_l5(Tl)
    ims = []
    for c in cores:
        prow = (np.zeros((1, M), np.float32) if c == 0
                else np.ascontiguousarray(x2[c * Tl - 1:c * Tl]))
        im = {"x2": np.ascontiguousarray(x2[c * Tl:(c + 1) * Tl]),
              "prevrow": prow, "Wr": Wr, "Wk": Wk, "Wv": Wv,
              "w1": rw1, "w2": rw2, "a1": ra1, "a2": ra2, "g1": rg1, "g2": rg2,
              "epsln": consts["epsln"], "triu1_bf": tri1, "tril1_bf": trl1,
              "triu0_bf": tri0, "ident": consts["ident"], "bmask_bf": bmask_bf,
              "blockones64": bones_bf, "sel2": sel2_bf,
              "eye64_f32": np.eye(64, dtype=np.float32)}
        im.update(cvs)
        ims.append(im)
    res5 = run_bass_kernel_spmd(nc5, ims, core_ids=cores)

    lnw_fm = _fmvec(inputs["L0_attn_b_ln_w"])
    lnb_fm = _fmvec(inputs["L0_attn_b_ln_b"])
    nc6 = build_l6(Tl)
    ims = []
    for c in cores:
        sppt = np.tile(np.eye(64, dtype=np.float32), (7 * RH, 1))
        spq = np.zeros((7 * RH * 64, 64), np.float32)
        for j in range(c):
            spP = np.asarray(res5.results[j]["span_P"], dtype=np.float32)
            spQ = np.asarray(res5.results[j]["span_Q"], dtype=np.float32)
            sppt[j * RH * 64:(j + 1) * RH * 64] = (
                spP.reshape(RH, 64, 64).transpose(0, 2, 1).reshape(RH * 64, 64))
            spq[j * RH * 64:(j + 1) * RH * 64] = spQ
        rc = res5.results[c]
        im = {"ylblob": np.asarray(rc["ylblob"]),
              "rhblob": np.asarray(rc["rhblob"]),
              "phiblob": np.asarray(rc["phiblob"]),
              "zpblob": np.asarray(rc["zpblob"]),
              "wlblob": np.asarray(rc["wlblob"]),
              "gfm": np.asarray(rc["gfm"]),
              "rtfm": np.asarray(rc["rtfm"]),
              "spPT": np.ascontiguousarray(sppt),
              "spQ": np.ascontiguousarray(spq),
              "lnw_fm": lnw_fm, "lnb_fm": lnb_fm, "epsgn": consts["epsgn"],
              "blockones64": bones_bf, "sel2": sel2_bf}
        ims.append(im)
    res6 = run_bass_kernel_spmd(nc6, ims, core_ids=cores)

    yout_pc = [np.ascontiguousarray(np.asarray(res6.results[c]["yout"]))
               for c in cores]
    x4 = _run_moe(nc3, nc3b, nc4, yout_pc, np.ascontiguousarray(Wo), x2,
                  wg, w1e, w2e, consts, T, Tl, cores)

    return x4.reshape(1, T, M).astype(np.float32)
